# version 19
# baseline (speedup 1.0000x reference)
import numpy as np
import ml_dtypes
from contextlib import ExitStack

import concourse.bass as bass
import concourse.mybir as mybir
from concourse.bass_utils import run_bass_kernel_spmd

F32 = mybir.dt.float32
BF16 = mybir.dt.bfloat16
I32 = mybir.dt.int32
AF = mybir.ActivationFunctionType
ALU = mybir.AluOpType
AX = mybir.AxisListType

BATCH, SEQ, CDIM = 2, 8192, 1024
NHEAD, KH = 16, 64
FDIM = 4096
TOK = 2048
TILE = 512
NT_A = SEQ // TILE
NT = TOK // TILE
EPS = 1e-5

ENGS = ("sync", "scalar", "vector", "tensor", "gpsimd")
DEBUG_PHASES = 4
DEBUG_A_TILES = NT_A
DEBUG_WPREP = True
DEBUG_A_STEPS = 9
DEBUG_TAPS = False


class Prog:
    def __init__(self):
        self.items = {e: [] for e in ENGS}
        self.cnt = {}

    def add_sem(self, name):
        self.cnt[name] = 0

    def val(self, name):
        return self.cnt[name]

    def emit(self, eng, fn=None, waits=(), inc=None, inc_by=1):
        ws = tuple((s, v) for s, v in waits if v > 0)
        for s, v in ws:
            assert v <= self.cnt[s], f"wait on future {s}:{v}>{self.cnt[s]}"
        if inc is not None:
            assert fn is not None
            self.cnt[inc] += inc_by
        self.items[eng].append((fn, ws, inc, inc_by))
        return self.cnt[inc] if inc is not None else None

    def fence(self):
        snap = [(s, v) for s, v in self.cnt.items() if v > 0]
        for e in ENGS:
            self.emit(e, None, waits=snap)

    def flush(self, block, sems):
        def runner(items):
            def go(eng):
                for fn, waits, inc, inc_by in items:
                    for snm, v in waits:
                        eng.wait_ge(sems[snm], v)
                    if fn is not None:
                        inst = fn(eng)
                        if inc is not None:
                            inst.then_inc(sems[inc], inc_by)
            return go
        for e in ENGS:
            if self.items[e]:
                getattr(block, e)(runner(self.items[e]))
            self.items[e] = []


def build_program():
    nc = bass.Bass("TRN2", target_bir_lowering=False, detect_race_conditions=False)

    xfull = nc.dram_tensor("xfull", [SEQ, CDIM], F32, kind="ExternalInput").ap()
    xhalo = nc.dram_tensor("xhalo", [2, CDIM], F32, kind="ExternalInput").ap()
    hmask = nc.dram_tensor("hmask", [2, 1], F32, kind="ExternalInput").ap()
    p2d_d = nc.dram_tensor("p2d", [CDIM, CDIM], F32, kind="ExternalInput").ap()
    mblk_d = nc.dram_tensor("mblk", [128, NHEAD * KH], F32, kind="ExternalInput").ap()
    mwt_d = nc.dram_tensor("mwt", [CDIM, CDIM], F32, kind="ExternalInput").ap()
    mbias_d = nc.dram_tensor("mbias", [128, CDIM], F32, kind="ExternalInput").ap()
    w1t_d = nc.dram_tensor("w1t", [CDIM, FDIM], F32, kind="ExternalInput").ap()
    w2t_d = nc.dram_tensor("w2t", [FDIM, CDIM], F32, kind="ExternalInput").ap()
    convp_d = nc.dram_tensor("convp", [128, 128], F32, kind="ExternalInput").ap()
    idbf_d = nc.dram_tensor("idbf", [128, 128], BF16, kind="ExternalInput").ap()
    idf32_d = nc.dram_tensor("idf32", [128, 128], F32, kind="ExternalInput").ap()
    ones1_d = nc.dram_tensor("ones1", [1, 128], F32, kind="ExternalInput").ap()
    out_d = nc.dram_tensor("out", [TOK, CDIM], F32, kind="ExternalOutput").ap()

    KIND = "ExternalOutput" if DEBUG_TAPS else "Internal"
    tern1_d = nc.dram_tensor("tern1_s", [CDIM, FDIM], BF16, kind=KIND).ap()
    tern2_d = nc.dram_tensor("tern2_s", [FDIM, CDIM], BF16, kind=KIND).ap()
    x2_d = nc.dram_tensor("x2_s", [TOK, CDIM], F32, kind=KIND).ap()
    xqs_d = nc.dram_tensor("xqs_s", [8, 128, TOK], BF16, kind=KIND).ap()
    hq_d = nc.dram_tensor("hq_s", [32, 128, TOK], BF16, kind=KIND).ap()
    if DEBUG_TAPS:
        dbg_g = nc.dram_tensor("dbg_g", [KH, 1024], BF16, kind="ExternalOutput").ap()
        dbg_a = nc.dram_tensor("dbg_a", [128, 1024], BF16, kind="ExternalOutput").ap()
        dbg_q = nc.dram_tensor("dbg_q", [128, 8, 512], BF16, kind="ExternalOutput").ap()
        dbg_y = nc.dram_tensor("dbg_y", [128, 1024], F32, kind="ExternalOutput").ap()
        dbg_st = nc.dram_tensor("dbg_st", [128, 32], F32, kind="ExternalOutput").ap()
        dbg_u = nc.dram_tensor("dbg_u", [128, 1024], BF16, kind="ExternalOutput").ap()
        dbg_uT = nc.dram_tensor("dbg_uT", [128, 1024], BF16, kind="ExternalOutput").ap()
        dbg_qt = nc.dram_tensor("dbg_qt", [128, 1024], BF16, kind="ExternalOutput").ap()
        dbg_xt = nc.dram_tensor("dbg_xt", [128, 1024], F32, kind="ExternalOutput").ap()
        dbg_sq = nc.dram_tensor("dbg_sq", [128, 1024], F32, kind="ExternalOutput").ap()
        dbg_qall = nc.dram_tensor("dbg_qall", [64, 128, 1024], BF16, kind="ExternalOutput").ap()

    P = Prog()
    DMA_SEMS = {q: [f"{q}{i}" for i in range(4)] for q in ("dx", "dw", "dg")}
    for q in DMA_SEMS:
        for snm in DMA_SEMS[q]:
            P.add_sem(snm)
    for snm in ("v", "s", "t"):
        P.add_sem(snm)
    _rr = {q: 0 for q in DMA_SEMS}

    MM = dict(skip_group_check=True)

    with ExitStack() as es:
        def sb(stack, name, shape, dt):
            return stack.enter_context(nc.sbuf_tensor(name, shape, dt))

        sems = {}
        for q in DMA_SEMS:
            for snm in DMA_SEMS[q]:
                sems[snm] = es.enter_context(nc.semaphore(snm))
        for snm in ("v", "s", "t"):
            sems[snm] = es.enter_context(nc.semaphore(snm))

        ps0 = es.enter_context(nc.psum_tensor("ps0", [128, 1024], F32))
        ps1 = es.enter_context(nc.psum_tensor("ps1", [128, 512], F32))
        ps2 = es.enter_context(nc.psum_tensor("ps2", [128, 512], F32))
        ps3 = es.enter_context(nc.psum_tensor("ps3", [128, 512], F32))
        ps4 = es.enter_context(nc.psum_tensor("ps4", [128, 512], F32))
        ps5 = es.enter_context(nc.psum_tensor("ps5", [128, 512], F32))
        ps6 = es.enter_context(nc.psum_tensor("ps6", [128, 512], F32))
        ps3b = ps3[:].bitcast(BF16)
        ps6b = ps6[:].bitcast(BF16)

        pb = sb(es, "pb", [128, 8, 1024], BF16)
        mwtb = sb(es, "mwtb", [128, 8, 1024], BF16)
        mblkb = sb(es, "mblkb", [128, NHEAD * KH], BF16)
        ablk = sb(es, "ablk", [128, 8 * 128], BF16)
        mbias = sb(es, "mbias_t", [128, 1024], F32)
        convp = sb(es, "convp_t", [128, 128], F32)
        idbf = sb(es, "idbf_t", [128, 128], BF16)
        idf32 = sb(es, "idf32_t", [128, 128], F32)
        ones1 = sb(es, "ones1_t", [1, 128], F32)
        gsb = sb(es, "gsb", [128, 8 * 128], BF16)
        cpool = sb(es, "cpool", [128, 4], F32)
        wsc = sb(es, "wsc", [128, 4], F32)
        invb2res = sb(es, "invb2res", [128, 16], F32)
        xqsTh = sb(es, "xqsTh", [128, 8, 2], BF16)
        hedge = sb(es, "hedge", [128, 64], BF16)
        stat = sb(es, "stat", [128, 32], F32)
        tiny = sb(es, "tiny", [128, 16], F32)
        sqb = sb(es, "sqb", [128, 1024], F32)
        xh = sb(es, "xh_t", [2, CDIM], F32)
        zcol = sb(es, "zcol", [128, KH], BF16)
        hmk = sb(es, "hmk_t", [2, 1], F32)

        def dma(qeng, queue, out, in_, waits=()):
            qsem = DMA_SEMS[queue][_rr[queue] % len(DMA_SEMS[queue])]
            _rr[queue] += 1
            def f(e, out=out, in_=in_):
                return e.dma_start(out=out, in_=in_)
            waits = tuple(waits) + ((qsem, P.val(qsem)),)
            v = P.emit(qeng, f, waits=waits, inc=qsem, inc_by=16)
            return (qsem, v)

        def qdone(queue):
            return tuple((snm, P.val(snm)) for snm in DMA_SEMS[queue])

        def ew(eng, fn, waits=(), inc=None):
            return P.emit(eng, fn, waits=waits, inc=inc, inc_by=1)

        def vsettle():
            P.emit("vector", None, waits=(("v", P.val("v")),))

        ms_v = {0: 0, 1: 0}
        sqb_free = [0]
        sqrt_s = {0: 0, 1: 0}

        def ln_stats(xap, pslice, par, wx):
            o = 16 * par
            ew("vector", lambda e: e.tensor_reduce(
                out=stat[pslice, o:o + 1], in_=xap, axis=AX.X, op=ALU.add),
                waits=wx, inc="v")
            vsettle()
            ew("scalar", lambda e: e.activation(
                out=sqb[pslice, :], in_=xap, func=AF.Square),
                waits=wx + (("v", sqb_free[0]),), inc="s")
            sq_s = P.val("s")
            ew("vector", lambda e: e.tensor_reduce(
                out=stat[pslice, o + 1:o + 2], in_=sqb[pslice, :], axis=AX.X,
                op=ALU.add), waits=(("v", ms_v[par]), ("s", sq_s)), inc="v")
            vsettle()
            sqb_free[0] = P.val("v")
            ew("vector", lambda e: e.tensor_scalar(
                out=stat[pslice, o + 4:o + 5], in0=stat[pslice, o:o + 1],
                scalar1=1.0 / 1024, scalar2=None, op0=ALU.mult), inc="v")
            vsettle()
            ew("vector", lambda e: e.tensor_scalar(
                out=stat[pslice, o + 5:o + 6], in0=stat[pslice, o + 1:o + 2],
                scalar1=1.0 / 1024, scalar2=None, op0=ALU.mult), inc="v")
            vsettle()
            ms_v[par] = P.val("v")
            ew("vector", lambda e: e.scalar_tensor_tensor(
                out=stat[pslice, o + 6:o + 7], in0=stat[pslice, o + 4:o + 5],
                scalar=stat[pslice, o + 4:o + 5], in1=stat[pslice, o + 5:o + 6],
                op0=ALU.mult, op1=ALU.subtract),
                waits=(("s", sqrt_s[par]),), inc="v")
            vsettle()
            vV = P.val("v")
            ew("scalar", lambda e: e.activation(
                out=stat[pslice, o + 7:o + 8], in_=stat[pslice, o + 6:o + 7],
                func=AF.Sqrt, bias=cpool[pslice, 0:1], scale=-1.0),
                waits=(("v", vV),), inc="s")
            sqrt_s[par] = P.val("s")
            vS2 = P.val("s")
            ew("vector", lambda e: e.reciprocal(
                out=stat[pslice, o + 2:o + 3], in_=stat[pslice, o + 7:o + 8]),
                waits=(("s", vS2),), inc="v")
            vsettle()
            ew("vector", lambda e: e.scalar_tensor_tensor(
                out=stat[pslice, o + 3:o + 4], in0=stat[pslice, o + 4:o + 5],
                scalar=-1.0, in1=stat[pslice, o + 2:o + 3],
                op0=ALU.mult, op1=ALU.mult), inc="v")
            vsettle()
            return P.val("v")

        def quant1(x2ap, pslice, par, wready, xqi, xqsb, xqs_gate):
            o = 16 * par
            ln_stats(x2ap, pslice, par, wready)
            ew("vector", lambda e: e.tensor_reduce(
                out=stat[pslice, o + 8:o + 9], in_=x2ap, axis=AX.X, op=ALU.max), inc="v")
            vsettle()
            ew("vector", lambda e: e.tensor_reduce(
                out=stat[pslice, o + 9:o + 10], in_=x2ap, axis=AX.X, op=ALU.min), inc="v")
            vsettle()
            ew("vector", lambda e: e.tensor_sub(
                out=stat[pslice, o + 10:o + 11], in0=stat[pslice, o + 8:o + 9],
                in1=stat[pslice, o + 4:o + 5]), inc="v")
            vsettle()
            ew("vector", lambda e: e.tensor_sub(
                out=stat[pslice, o + 11:o + 12], in0=stat[pslice, o + 4:o + 5],
                in1=stat[pslice, o + 9:o + 10]), inc="v")
            vsettle()
            ew("vector", lambda e: e.tensor_max(
                out=stat[pslice, o + 12:o + 13], in0=stat[pslice, o + 10:o + 11],
                in1=stat[pslice, o + 11:o + 12]), inc="v")
            vsettle()
            ew("vector", lambda e: e.tensor_tensor(
                out=stat[pslice, o + 13:o + 14], in0=stat[pslice, o + 12:o + 13],
                in1=stat[pslice, o + 2:o + 3], op=ALU.mult), inc="v")
            vsettle()
            ew("vector", lambda e: e.tensor_scalar(
                out=stat[pslice, o + 14:o + 15], in0=stat[pslice, o + 13:o + 14],
                scalar1=EPS, scalar2=None, op0=ALU.max), inc="v")
            vsettle()
            ew("vector", lambda e: e.reciprocal(
                out=stat[pslice, o + 15:o + 16], in_=stat[pslice, o + 14:o + 15]), inc="v")
            vsettle()
            ew("vector", lambda e: e.tensor_scalar(
                out=stat[pslice, o + 8:o + 9], in0=stat[pslice, o + 15:o + 16],
                scalar1=127.0, scalar2=None, op0=ALU.mult), inc="v")
            vsettle()
            ew("vector", lambda e: e.tensor_tensor(
                out=stat[pslice, o + 9:o + 10], in0=stat[pslice, o + 2:o + 3],
                in1=stat[pslice, o + 8:o + 9], op=ALU.mult), inc="v")
            vsettle()
            ew("vector", lambda e: e.scalar_tensor_tensor(
                out=stat[pslice, o + 10:o + 11], in0=stat[pslice, o + 4:o + 5],
                scalar=-1.0, in1=stat[pslice, o + 9:o + 10],
                op0=ALU.mult, op1=ALU.mult), inc="v")
            vsettle()
            ew("vector", lambda e: e.scalar_tensor_tensor(
                out=stat[pslice, o + 11:o + 12], in0=stat[pslice, o + 14:o + 15],
                scalar=1.0 / 127, in1=wsc[pslice, 1:2],
                op0=ALU.mult, op1=ALU.mult), inc="v")
            vsettle()
            ew("vector", lambda e: e.tensor_scalar(
                out=xqi[pslice, :], in0=x2ap, scalar1=stat[pslice, o + 9:o + 10],
                scalar2=stat[pslice, o + 10:o + 11], op0=ALU.mult, op1=ALU.add), inc="v")
            vsettle()
            ew("vector", lambda e: e.tensor_scalar(
                out=xqsb[pslice, :], in0=xqi[pslice, :],
                scalar1=stat[pslice, o + 11:o + 12], scalar2=None, op0=ALU.mult),
                waits=xqs_gate, inc="v")
            vsettle()
            return P.val("v")

        with nc.Block() as block:
            with ExitStack() as esQ:
                qTres = sb(esQ, "qTres", [128, 8, TOK], BF16)
                esA = esQ.enter_context(ExitStack())
                pstage = sb(esA, "pstage", [128, 4 * 1024], F32)
                xt = [sb(esA, f"xt{i}", [128, 1024], F32) for i in range(4)]
                sqb2 = [sb(esA, f"sqb2_{i}", [128, 1024], F32) for i in range(2)]
                statA = sb(esA, "statA", [128, 96], F32)
                ub = [sb(esA, f"ub{i}", [128, 1024], BF16) for i in range(2)]
                uT = [sb(esA, f"uT{i}", [128, 1024], BF16) for i in range(2)]
                qtok = [sb(esA, f"qtok{i}", [128, 1024], BF16) for i in range(2)]
                wstage = [sb(esA, f"wstage{i}", [128, 4096], F32) for i in range(2)]
                wi32 = sb(esA, "wi32", [128, 4096], I32)
                wtb = [sb(esA, f"wtb{i}", [128, 4096], BF16) for i in range(2)]
                wabs = sb(esA, "wabs", [128, 48], F32)

                ew("vector", lambda e: e.memset(cpool[:, 0:1], EPS), inc="v")
                ew("vector", lambda e: e.memset(cpool[:, 1:2], 0.0), inc="v")
                ew("vector", lambda e: e.memset(cpool[:, 2:3], 1.0), inc="v")
                ew("vector", lambda e: e.memset(zcol[:], 0.0), inc="v")
                ew("vector", lambda e: e.memset(ablk[:], 0.0), inc="v")
                ew("vector", lambda e: e.memset(hedge[:], 0.0), inc="v")
                v_consts = P.val("v")

                dma("sync", "dx", mbias[:], mbias_d)
                dma("sync", "dx", convp[:], convp_d)
                dma("sync", "dx", idbf[:], idbf_d)
                dma("sync", "dx", idf32[:], idf32_d)
                dma("sync", "dx", ones1[:], ones1_d)
                dma("sync", "dx", xh[:], xhalo)
                dma("sync", "dx", hmk[:], hmask)
                dma("sync", "dx", pstage[:, 0:NHEAD * KH], mblk_d)
                dx_small = qdone("dx")
                ew("vector", lambda e: e.tensor_copy(
                    out=mblkb[:], in_=pstage[:, 0:NHEAD * KH]),
                    waits=dx_small, inc="v")
                v_gate = P.val("v")
                v_mblk = v_gate
                for dst_b, src_d in ((pb, p2d_d), (mwtb, mwt_d)):
                    for h in range(2):
                        for cc in range(4):
                            dma("sync", "dx",
                                pstage[:, cc * 1024:(cc + 1) * 1024],
                                src_d[(h * 4 + cc) * 128:(h * 4 + cc + 1) * 128, :],
                                waits=(("v", v_gate),) if cc == 0 else ())
                        dxp = qdone("dx")
                        for cc in range(4):
                            ew("vector", lambda e, dst_b=dst_b, cc=cc, h=h: e.tensor_copy(
                                out=dst_b[:, h * 4 + cc, :],
                                in_=pstage[:, cc * 1024:(cc + 1) * 1024]),
                                waits=dxp if cc == 0 else (), inc="v")
                        v_gate = P.val("v")
                v_pb = v_gate
                v_mwtb = v_gate

                wblocks = [("w1", bi) for bi in range(8)] + [("w2", bi) for bi in range(32)]
                wstate = {}

                def w_src(kind, bi):
                    if kind == "w1":
                        return w1t_d[bi * 128:(bi + 1) * 128, :], 4096
                    return w2t_d[bi * 128:(bi + 1) * 128, :], 1024

                def emit_wpass1(idx):
                    kind, bi = wblocks[idx]
                    src, ncols = w_src(kind, bi)
                    dst = wstage[idx % 2][:, 0:ncols]
                    wfree = wstate.get(("p1v", idx - 2), 0)
                    d = dma("scalar", "dw", dst, src, waits=(("v", wfree),))
                    col = bi if kind == "w1" else 8 + bi
                    ew("vector", lambda e, dst=dst, col=col: e.tensor_reduce(
                        out=wabs[:, col:col + 1], in_=dst, axis=AX.X, op=ALU.add,
                        apply_absolute_value=True), waits=(d,), inc="v")
                    wstate[("p1v", idx)] = P.val("v")

                def emit_wscales():
                    vA = wstate[("p1v", len(wblocks) - 1)]
                    vsettle()
                    ew("vector", lambda e: e.tensor_reduce(
                        out=tiny[:, 0:1], in_=wabs[:, 0:8], axis=AX.X, op=ALU.add),
                        waits=(("v", vA),), inc="v")
                    ew("vector", lambda e: e.tensor_reduce(
                        out=tiny[:, 1:2], in_=wabs[:, 8:40], axis=AX.X, op=ALU.add),
                        inc="v")
                    vsettle()
                    vv = P.val("v")
                    ew("tensor", lambda e: e.matmul(
                        out=ps6[0:1, 0:2], lhsT=cpool[:, 2:3], rhs=tiny[:, 0:2],
                        start=True, stop=True, **MM), waits=(("v", vv),), inc="t")
                    tt = P.val("t")
                    ew("vector", lambda e: e.tensor_scalar(
                        out=tiny[0:1, 2:3], in0=ps6[0:1, 0:1],
                        scalar1=1.0 / (CDIM * FDIM), scalar2=1e-5,
                        op0=ALU.mult, op1=ALU.max), waits=(("t", tt),), inc="v")
                    ew("vector", lambda e: e.tensor_scalar(
                        out=tiny[0:1, 3:4], in0=ps6[0:1, 1:2],
                        scalar1=1.0 / (CDIM * FDIM), scalar2=1e-5,
                        op0=ALU.mult, op1=ALU.max), inc="v")
                    ew("vector", lambda e: e.reciprocal(
                        out=tiny[0:1, 4:5], in_=tiny[0:1, 2:3]), inc="v")
                    ew("vector", lambda e: e.reciprocal(
                        out=tiny[0:1, 5:6], in_=tiny[0:1, 3:4]), inc="v")
                    vsettle()
                    vv2 = P.val("v")
                    ew("tensor", lambda e: e.matmul(
                        out=ps6[:, 4:8], lhsT=ones1[:], rhs=tiny[0:1, 2:6],
                        start=True, stop=True, **MM), waits=(("v", vv2),), inc="t")
                    tt2 = P.val("t")
                    ew("vector", lambda e: e.tensor_copy(out=wsc[:, 1:2], in_=ps6[:, 4:5]),
                       waits=(("t", tt2),), inc="v")
                    ew("vector", lambda e: e.tensor_copy(out=wsc[:, 3:4], in_=ps6[:, 5:6]),
                       inc="v")
                    ew("vector", lambda e: e.tensor_copy(out=wsc[:, 0:1], in_=ps6[:, 6:7]),
                       inc="v")
                    ew("vector", lambda e: e.tensor_copy(out=wsc[:, 2:3], in_=ps6[:, 7:8]),
                       inc="v")
                    wstate["scales_v"] = P.val("v")

                def emit_wpass2(idx):
                    kind, bi = wblocks[idx]
                    src, ncols = w_src(kind, bi)
                    dst = wstage[idx % 2][:, 0:ncols]
                    wfree = wstate.get(("p2v", idx - 2), 0)
                    d = dma("scalar", "dw", dst, src,
                            waits=(("v", wstate["scales_v"]), ("v", wfree)))
                    scol = wsc[:, 0:1] if kind == "w1" else wsc[:, 2:3]
                    ew("vector", lambda e, dst=dst, scol=scol, ncols=ncols: e.tensor_scalar(
                        out=wi32[:, 0:ncols], in0=dst, scalar1=scol, scalar2=None,
                        op0=ALU.mult), waits=(d,), inc="v")
                    ew("vector", lambda e, ncols=ncols, idx=idx: e.tensor_scalar(
                        out=wtb[idx % 2][:, 0:ncols], in0=wi32[:, 0:ncols], scalar1=1.0,
                        scalar2=-1.0, op0=ALU.min, op1=ALU.max),
                        waits=(wstate.get(("p2g", idx - 2), ("v", 0)),), inc="v")
                    wstate[("p2v", idx)] = P.val("v")
                    tdst = (tern1_d if kind == "w1" else tern2_d)[bi * 128:(bi + 1) * 128, :]
                    wstate[("p2g", idx)] = dma("gpsimd", "dg", tdst, wtb[idx % 2][:, 0:ncols],
                                               waits=(("v", P.val("v")),))

                wq1 = iter(range(len(wblocks)))
                wq2 = iter(range(len(wblocks)))
                scales_done = [False]

                psA_free = {0: ("s", 0), 1: ("s", 0)}
                psT_free = ("v", 0)
                xt4_free = {k: (0, 0) for k in range(4)}
                sqb2_free = {0: 0, 1: 0}
                statA_sfree = {0: 0, 1: 0}
                t_G_last = 0
                ub_free = {0: 0, 1: 0}
                uT_free = {0: 0, 1: 0}
                qtok_free = {0: 0, 1: 0}
                qtok_dump = {0: ("v", 0), 1: ("v", 0)}

                pendG = [None]
                ps6T_free = [0]

                def flush_G(gt, par, s_qk, owned):
                    nonlocal t_G_last
                    if gt == 0:
                        for pg in (ps1, ps2):
                            ew("tensor", lambda e, pg=pg, par=par: e.matmul(
                                out=pg[0:KH, :], lhsT=zcol[:, :],
                                rhs=qtok[par][:, 0:512],
                                start=True, stop=False, **MM),
                                waits=(("v", s_qk),) if pg is ps1 else (),
                                inc="t")
                    for n in range(16):
                        pg = ps1 if n < 8 else ps2
                        off = (n % 8) * 64
                        ew("tensor", lambda e, par=par, n=n, pg=pg, off=off, gt=gt:
                            e.matmul(
                                out=pg[0:KH, off:off + KH],
                                lhsT=qtok[par][:, n * KH:(n + 1) * KH],
                                rhs=qtok[par][:, n * KH:(n + 1) * KH],
                                start=False, stop=(gt == DEBUG_A_TILES * 4 - 1), **MM),
                            waits=(("v", s_qk),) if n == 0 else (), inc="t")
                    t_G_last = P.val("t")
                    if owned:
                        for j in range(8):
                            ew("tensor", lambda e, par=par, j=j: e.transpose(
                                out=ps6b[:, j * 128:(j + 1) * 128],
                                in_=qtok[par][:, j * 128:(j + 1) * 128],
                                identity=idbf[:]),
                                waits=(("v", ps6T_free[0]),) if j == 0 else (),
                                inc="t")
                        t_tp2 = P.val("t")
                        ew("vector", lambda e, gt=gt: e.tensor_copy(
                            out=qTres[:, :, gt * 128:(gt + 1) * 128],
                            in_=ps6b[:].rearrange("p (c t) -> p c t", c=8)),
                            waits=(("t", t_tp2),), inc="v")
                        ps6T_free[0] = P.val("v")
                    qtok_free[par] = P.val("t")

                def emit_pass1_unit(t, k):
                    so2 = 48 * (t % 2)
                    gt2 = t * 4 + k
                    d = dma("sync", "dx", xt[k][:],
                            xfull[gt2 * 128:(gt2 + 1) * 128, :],
                            waits=(("v", xt4_free[k][0]), ("s", xt4_free[k][1])))
                    ew("vector", lambda e, k=k, so2=so2: e.tensor_reduce(
                        out=statA[:, so2 + k:so2 + k + 1], in_=xt[k][:], axis=AX.X,
                        op=ALU.add), waits=(d, ("v", v_consts),
                                            ("s", statA_sfree[t % 2])), inc="v")
                    ew("scalar", lambda e, k=k: e.activation(
                        out=sqb2[k % 2][:], in_=xt[k][:], func=AF.Square),
                        waits=(d, ("v", sqb2_free[k % 2])), inc="s")
                    sq_s = P.val("s")
                    ew("vector", lambda e, k=k, so2=so2: e.tensor_reduce(
                        out=statA[:, so2 + 4 + k:so2 + 5 + k], in_=sqb2[k % 2][:],
                        axis=AX.X, op=ALU.add), waits=(("s", sq_s),), inc="v")
                    sqb2_free[k % 2] = P.val("v")

                def emit_chain(t):
                    so = 48 * (t % 2)
                    vsettle()
                    ew("vector", lambda e, so=so: e.tensor_scalar(
                        out=statA[:, so + 8:so + 12], in0=statA[:, so + 0:so + 4],
                        scalar1=1.0 / 1024, scalar2=None, op0=ALU.mult), inc="v")
                    vsettle()
                    ew("vector", lambda e, so=so: e.tensor_scalar(
                        out=statA[:, so + 12:so + 16], in0=statA[:, so + 4:so + 8],
                        scalar1=1.0 / 1024, scalar2=None, op0=ALU.mult), inc="v")
                    vsettle()
                    ew("vector", lambda e, so=so: e.tensor_tensor(
                        out=statA[:, so + 16:so + 20], in0=statA[:, so + 8:so + 12],
                        in1=statA[:, so + 8:so + 12], op=ALU.mult), inc="v")
                    vsettle()
                    ew("vector", lambda e, so=so: e.tensor_sub(
                        out=statA[:, so + 20:so + 24], in0=statA[:, so + 16:so + 20],
                        in1=statA[:, so + 12:so + 16]), inc="v")
                    vsettle()
                    vV = P.val("v")
                    ew("scalar", lambda e, so=so: e.activation(
                        out=statA[:, so + 24:so + 28], in_=statA[:, so + 20:so + 24],
                        func=AF.Sqrt, bias=cpool[:, 0:1], scale=-1.0),
                        waits=(("v", vV),), inc="s")
                    sA_sqrt = P.val("s")
                    ew("vector", lambda e, so=so: e.reciprocal(
                        out=statA[:, so + 28:so + 32], in_=statA[:, so + 24:so + 28]),
                        waits=(("s", sA_sqrt),), inc="v")
                    vsettle()
                    ew("vector", lambda e, so=so: e.scalar_tensor_tensor(
                        out=statA[:, so + 32:so + 36], in0=statA[:, so + 8:so + 12],
                        scalar=-1.0, in1=statA[:, so + 28:so + 32],
                        op0=ALU.mult, op1=ALU.mult), inc="v")
                    vsettle()
                    return P.val("v")

                for k in range(4):
                    emit_pass1_unit(0, k)
                v_stats = emit_chain(0)
                for t_i in range(DEBUG_A_TILES):
                    owned = t_i < NT
                    so = 48 * (t_i % 2)
                    for k in range(4):
                        gt = t_i * 4 + k
                        par = gt % 2
                        ew("scalar", lambda e, k=k, par=par, so=so: e.activation(
                            out=ub[par][:], in_=xt[k][:], func=AF.Identity,
                            bias=statA[:, so + 32 + k:so + 33 + k],
                            scale=statA[:, so + 28 + k:so + 29 + k]),
                            waits=(("v", v_stats), ("t", ub_free[par])), inc="s")
                        s_u = P.val("s")
                        xt4_free[k] = (v_stats, s_u)
                        if k == 3:
                            statA_sfree[t_i % 2] = s_u
                        for j in range(8):
                            ew("tensor", lambda e, par=par, j=j: e.transpose(
                                out=ps3b[:, j * 128:(j + 1) * 128],
                                in_=ub[par][:, j * 128:(j + 1) * 128],
                                identity=idbf[:]),
                                waits=(("s", s_u), psT_free) if j == 0 else (),
                                inc="t")
                        ub_free[par] = P.val("t")
                        t_tp = P.val("t")
                        ew("vector", lambda e, par=par: e.tensor_copy(
                            out=uT[par][:], in_=ps3b[:]),
                            waits=(("t", t_tp), ("t", uT_free[par])), inc="v")
                        psT_free = ("v", P.val("v"))
                        s_uT = ("v", P.val("v"))
                        if k >= 1 and t_i + 1 < DEBUG_A_TILES:
                            emit_pass1_unit(t_i + 1, k - 1)
                        if pendG[0] is not None:
                            flush_G(*pendG[0])
                            pendG[0] = None
                        qa0, qa1 = (ps0[:, 0:512], ps0[:, 512:1024]) if par == 0 \
                            else (ps4[:], ps5[:])
                        for cc in range(8):
                            w0 = (s_uT, psA_free[par], ("v", v_pb)) \
                                if cc == 0 else ()
                            ew("tensor", lambda e, par=par, cc=cc, qa0=qa0: e.matmul(
                                out=qa0,
                                lhsT=uT[par][:, cc * 128:(cc + 1) * 128],
                                rhs=pb[:, cc, 0:512],
                                start=(cc == 0), stop=(cc == 7), **MM),
                                waits=w0, inc="t")
                            ew("tensor", lambda e, par=par, cc=cc, qa1=qa1: e.matmul(
                                out=qa1,
                                lhsT=uT[par][:, cc * 128:(cc + 1) * 128],
                                rhs=pb[:, cc, 512:1024],
                                start=(cc == 0), stop=(cc == 7), **MM), inc="t")
                        uT_free[par] = P.val("t")
                        t_q = P.val("t")
                        ew("vector", lambda e, par=par, qa0=qa0, qa1=qa1:
                            (e.tensor_copy(out=qtok[par][:, 0:512], in_=qa0),
                             e.tensor_copy(out=qtok[par][:, 512:1024], in_=qa1))[-1],
                            waits=(("t", t_q), ("t", qtok_free[par]),
                                   qtok_dump[par]), inc="v")
                        psA_free[par] = ("v", P.val("v"))
                        s_qk = P.val("v")
                        if DEBUG_TAPS:
                            qdump = dma("gpsimd", "dg", dbg_qall[gt, :, :],
                                        qtok[par][:], waits=(("v", s_qk),))
                            qtok_dump[par] = qdump
                        if DEBUG_A_STEPS >= 5:
                            pendG[0] = (gt, par, s_qk, owned)
                    if t_i + 1 < DEBUG_A_TILES:
                        emit_pass1_unit(t_i + 1, 3)
                        v_stats = emit_chain(t_i + 1)
                    if not DEBUG_WPREP:
                        continue
                    if t_i < 7 or DEBUG_A_TILES < NT_A:
                        for _ in range(6):
                            idx = next(wq1, None)
                            if idx is not None:
                                emit_wpass1(idx)
                        if t_i == 6 and not scales_done[0]:
                            for idx in wq1:
                                emit_wpass1(idx)
                            emit_wscales()
                            scales_done[0] = True
                    else:
                        for _ in range(5):
                            idx = next(wq2, None)
                            if idx is not None:
                                emit_wpass2(idx)
                if pendG[0] is not None:
                    flush_G(*pendG[0])
                    pendG[0] = None
                if DEBUG_WPREP:
                    if not scales_done[0]:
                        for idx in wq1:
                            emit_wpass1(idx)
                        emit_wscales()
                        scales_done[0] = True
                    for idx in wq2:
                        emit_wpass2(idx)

                if DEBUG_A_STEPS < 6:
                    P.fence()
                    P.flush(block, sems)
                    return nc
                ew("vector", lambda e: e.tensor_copy(out=gsb[0:KH, 0:512], in_=ps1[0:KH, :]),
                   waits=(("t", t_G_last),), inc="v")
                ew("vector", lambda e: e.tensor_copy(out=gsb[0:KH, 512:1024], in_=ps2[0:KH, :]),
                   inc="v")
                v_gsb = P.val("v")
                for n in range(16):
                    pr, ei = n // 2, n % 2
                    ew("tensor", lambda e, n=n, pr=pr, ei=ei: e.matmul(
                        out=ps4[ei * KH:(ei + 1) * KH, pr * KH:(pr + 1) * KH],
                        lhsT=mblkb[0:KH, n * KH:(n + 1) * KH],
                        rhs=gsb[0:KH, n * KH:(n + 1) * KH],
                        start=True, stop=True,
                        tile_position=(0, ei * KH), **MM),
                        waits=(("v", v_gsb), ("v", v_mblk)) if n == 0 else (),
                        inc="t")
                t_a = P.val("t")
                for n in range(16):
                    pr, ei = n // 2, n % 2
                    ew("vector", lambda e, pr=pr, ei=ei: e.tensor_copy(
                        out=ablk[ei * KH:(ei + 1) * KH,
                                 pr * 128 + ei * KH:pr * 128 + (ei + 1) * KH],
                        in_=ps4[ei * KH:(ei + 1) * KH, pr * KH:(pr + 1) * KH]),
                        waits=(("t", t_a),) if n == 0 else (), inc="v")
                v_ablk = P.val("v")
                if DEBUG_TAPS:
                    dma("gpsimd", "dg", dbg_g, gsb[0:KH, :], waits=(("v", v_gsb),))
                    dma("gpsimd", "dg", dbg_a, ablk[:], waits=(("v", v_ablk),))
                    for cc in range(8):
                        dma("gpsimd", "dg", dbg_q[:, cc, :], qTres[:, cc, 0:512],
                            waits=(("v", P.val("v")),))

                P.fence()
                P.flush(block, sems)
                esA.close()
                if DEBUG_PHASES < 2:
                    return nc

                with ExitStack() as esB:
                    nudT = sb(esB, "nudT", [128, 8, 512], BF16)
                    yb = [sb(esB, f"yb{i}", [128, 1024], F32) for i in range(2)]
                    vb = sb(esB, "vb", [128, 1024], F32)
                    xob = [sb(esB, f"xob{i}", [128, 1024], F32) for i in range(2)]
                    x2b = [sb(esB, f"x2b{i}", [128, 1024], F32) for i in range(2)]
                    xqi = sb(esB, "xqi", [128, 1024], I32)
                    xqsb = sb(esB, "xqsb", [128, 1024], BF16)
                    xqsT = sb(esB, "xqsT", [128, 8, 512], BF16)
                    uTh = sb(esB, "uTh", [128, 8, 2], BF16)
                    nudTh = sb(esB, "nudTh", [128, 8, 2], BF16)
                    qhb = sb(esB, "qhb", [2, 1024], BF16)

                    xob_free = {0: 0, 1: 0}
                    x2b_free = {0: (0, ("v", 0)), 1: (0, ("v", 0))}
                    psT_free2 = ("v", 0)
                    ps4_free, ps5_free = 0, 0
                    ps0_free = 0
                    yb_free_s = {0: 0, 1: 0}
                    nudT_free_t = 0
                    xqsT_free = 0
                    xqsT_store_dg = ("v", 0)

                    for i in range(NT):
                        for p in range(8):
                            pg = ps4 if p % 2 == 0 else ps5
                            pfree = ps4_free if p % 2 == 0 else ps5_free
                            ew("tensor", lambda e, p=p, pg=pg, i=i: e.matmul(
                                out=pg[:], lhsT=ablk[:, p * 128:(p + 1) * 128],
                                rhs=qTres[:, p, i * 512:(i + 1) * 512],
                                start=True, stop=True, **MM),
                                waits=(("v", v_ablk), ("v", pfree)), inc="t")
                            tn = P.val("t")
                            ew("vector", lambda e, p=p, pg=pg: e.tensor_copy(
                                out=nudT[:, p, :], in_=pg[:]),
                                waits=(("t", tn), ("t", nudT_free_t)), inc="v")
                            if p % 2 == 0:
                                ps4_free = P.val("v")
                            else:
                                ps5_free = P.val("v")
                        v_nud = P.val("v")
                        for k in range(4):
                            oc = i * 4 + k
                            par = oc % 2
                            for jc in range(8):
                                w0 = (("v", v_nud), ("v", ps0_free), ("v", v_mwtb)) \
                                    if jc == 0 else ()
                                ew("tensor", lambda e, jc=jc, k=k: e.matmul(
                                    out=ps0[:, 0:512],
                                    lhsT=nudT[:, jc, k * 128:(k + 1) * 128],
                                    rhs=mwtb[:, jc, 0:512],
                                    start=(jc == 0), stop=(jc == 7), **MM),
                                    waits=w0, inc="t")
                                ew("tensor", lambda e, jc=jc, k=k: e.matmul(
                                    out=ps0[:, 512:1024],
                                    lhsT=nudT[:, jc, k * 128:(k + 1) * 128],
                                    rhs=mwtb[:, jc, 512:1024],
                                    start=(jc == 0), stop=(jc == 7), **MM), inc="t")
                            if k == 3:
                                nudT_free_t = P.val("t")
                            t_mix = P.val("t")
                            ew("vector", lambda e, par=par: e.tensor_add(
                                out=yb[par][:], in0=ps0[:], in1=mbias[:]),
                                waits=(("t", t_mix), ("s", yb_free_s[par])), inc="v")
                            ps0_free = P.val("v")
                            v_y = P.val("v")
                            if DEBUG_TAPS and i == 0 and k == 0:
                                dma("gpsimd", "dg", dbg_y, yb[par][:], waits=(("v", v_y),))
                            dxo = dma("sync", "dx", xob[par][:],
                                      xfull[oc * 128:(oc + 1) * 128, :],
                                      waits=(("v", xob_free[par]),))
                            v_st = ln_stats(yb[par][:], slice(0, 128), par, (("v", v_y),))
                            ew("scalar", lambda e, par=par: e.activation(
                                out=vb[:], in_=yb[par][:], func=AF.Identity,
                                bias=stat[:, 16 * par + 3:16 * par + 4],
                                scale=stat[:, 16 * par + 2:16 * par + 3]),
                                waits=(("v", v_st),), inc="s")
                            s_v = P.val("s")
                            yb_free_s[par] = s_v
                            ew("vector", lambda e, par=par: e.tensor_add(
                                out=x2b[par][:], in0=vb[:], in1=xob[par][:]),
                                waits=(("s", s_v), dxo,
                                       ("v", x2b_free[par][0]),
                                       x2b_free[par][1]), inc="v")
                            xob_free[par] = P.val("v")
                            v_x2 = P.val("v")
                            dgx2 = dma("gpsimd", "dg",
                                       x2_d[oc * 128:(oc + 1) * 128, :],
                                       x2b[par][:], waits=(("v", v_x2),))
                            v_q = quant1(x2b[par][:], slice(0, 128), par,
                                         (("v", v_x2),), xqi, xqsb,
                                         (("t", xqsT_free),))
                            if DEBUG_TAPS and i == 0 and k == 0:
                                dma("gpsimd", "dg", dbg_st, stat[:], waits=(("v", v_q),))
                                dma("gpsimd", "dg", dbg_xt, x2b[par][:], waits=(("v", v_q),))
                                dma("gpsimd", "dg", dbg_u, xqsb[:], waits=(("v", v_q),))
                                P.fence()
                            x2b_free[par] = (v_q, dgx2)
                            for j in range(8):
                                ew("tensor", lambda e, j=j: e.transpose(
                                    out=ps3b[:, j * 128:(j + 1) * 128],
                                    in_=xqsb[:, j * 128:(j + 1) * 128],
                                    identity=idbf[:]),
                                    waits=(("v", v_q), psT_free2)
                                    if j == 0 else (), inc="t")
                            xqsT_free = P.val("t")
                            t_tp = P.val("t")
                            ew("vector", lambda e, k=k: e.tensor_copy(
                                out=xqsT[:, :, k * 128:(k + 1) * 128],
                                in_=ps3b[:].rearrange("p (c t) -> p c t", c=8)),
                                waits=(("t", t_tp), xqsT_store_dg), inc="v")
                            psT_free2 = ("v", P.val("v"))
                        vw = P.val("v")
                        for cc in range(8):
                            xqsT_store_dg = dma(
                                "gpsimd", "dg",
                                xqs_d[cc, :, i * 512:(i + 1) * 512],
                                xqsT[:, cc, :],
                                waits=(("v", vw),) if cc == 0 else ())
                    dg_xqs = qdone("dg")

                    id2 = idbf[0:2, 0:2]
                    v_st = ln_stats(xh[:], slice(0, 2), 0, dx_small)
                    ew("scalar", lambda e: e.activation(
                        out=xqsb[0:2, :], in_=xh[:], func=AF.Identity,
                        bias=stat[0:2, 3:4], scale=stat[0:2, 2:3]),
                        waits=(("v", v_st),), inc="s")
                    s_uh = P.val("s")
                    for j in range(8):
                        ew("tensor", lambda e, j=j: e.matmul(
                            out=ps4[:, j * 2:(j + 1) * 2],
                            lhsT=xqsb[0:2, j * 128:(j + 1) * 128], rhs=id2,
                            start=True, stop=True, **MM),
                            waits=(("s", s_uh), ("v", ps4_free)) if j == 0 else (),
                            inc="t")
                    t_tp = P.val("t")
                    ew("vector", lambda e: e.tensor_copy(
                        out=uTh[:], in_=ps4[:, 0:16].rearrange("p (c t) -> p c t", c=8)),
                        waits=(("t", t_tp),), inc="v")
                    ps4_free = P.val("v")
                    v_uTh = P.val("v")
                    for cc in range(8):
                        ew("tensor", lambda e, cc=cc: e.matmul(
                            out=ps0[0:2, 0:512], lhsT=uTh[:, cc, :],
                            rhs=pb[:, cc, 0:512], start=(cc == 0), stop=(cc == 7), **MM),
                            waits=(("v", v_uTh), ("v", ps0_free)) if cc == 0 else (),
                            inc="t")
                        ew("tensor", lambda e, cc=cc: e.matmul(
                            out=ps0[0:2, 512:1024], lhsT=uTh[:, cc, :],
                            rhs=pb[:, cc, 512:1024], start=(cc == 0), stop=(cc == 7),
                            **MM), inc="t")
                    t_qh = P.val("t")
                    ew("vector", lambda e: e.tensor_copy(out=qhb[:], in_=ps0[0:2, :]),
                       waits=(("t", t_qh),), inc="v")
                    ps0_free = P.val("v")
                    v_qh = P.val("v")
                    for j in range(8):
                        ew("tensor", lambda e, j=j: e.matmul(
                            out=ps4[:, j * 2:(j + 1) * 2],
                            lhsT=qhb[0:2, j * 128:(j + 1) * 128], rhs=id2,
                            start=True, stop=True, **MM),
                            waits=(("v", v_qh), ("v", ps4_free)) if j == 0 else (),
                            inc="t")
                    t_tp = P.val("t")
                    ew("vector", lambda e: e.tensor_copy(
                        out=uTh[:], in_=ps4[:, 0:16].rearrange("p (c t) -> p c t", c=8)),
                        waits=(("t", t_tp),), inc="v")
                    ps4_free = P.val("v")
                    v_qTh = P.val("v")
                    for p in range(8):
                        ew("tensor", lambda e, p=p: e.matmul(
                            out=ps4[:, 0:2], lhsT=ablk[:, p * 128:(p + 1) * 128],
                            rhs=uTh[:, p, :], start=True, stop=True, **MM),
                            waits=(("v", v_qTh), ("v", ps4_free)) if p == 0 else
                            (("v", ps4_free),), inc="t")
                        tn = P.val("t")
                        ew("vector", lambda e, p=p: e.tensor_copy(
                            out=nudTh[:, p, :], in_=ps4[:, 0:2]),
                            waits=(("t", tn),), inc="v")
                        ps4_free = P.val("v")
                    v_nudh = P.val("v")
                    for jc in range(8):
                        ew("tensor", lambda e, jc=jc: e.matmul(
                            out=ps0[0:2, 0:512], lhsT=nudTh[:, jc, :],
                            rhs=mwtb[:, jc, 0:512], start=(jc == 0), stop=(jc == 7),
                            **MM),
                            waits=(("v", v_nudh), ("v", ps0_free)) if jc == 0 else (),
                            inc="t")
                        ew("tensor", lambda e, jc=jc: e.matmul(
                            out=ps0[0:2, 512:1024], lhsT=nudTh[:, jc, :],
                            rhs=mwtb[:, jc, 512:1024], start=(jc == 0), stop=(jc == 7),
                            **MM), inc="t")
                    t_mixh = P.val("t")
                    ew("vector", lambda e: e.tensor_add(
                        out=yb[0][0:2, :], in0=ps0[0:2, :], in1=mbias[0:2, :]),
                        waits=(("t", t_mixh), ("s", yb_free_s[0])), inc="v")
                    ps0_free = P.val("v")
                    v_yh = P.val("v")
                    v_st = ln_stats(yb[0][0:2, :], slice(0, 2), 0, (("v", v_yh),))
                    ew("scalar", lambda e: e.activation(
                        out=vb[0:2, :], in_=yb[0][0:2, :], func=AF.Identity,
                        bias=stat[0:2, 3:4], scale=stat[0:2, 2:3]),
                        waits=(("v", v_st),), inc="s")
                    s_vh = P.val("s")
                    ew("vector", lambda e: e.tensor_add(
                        out=x2b[0][0:2, :], in0=vb[0:2, :], in1=xh[:]),
                        waits=(("s", s_vh), ("v", x2b_free[0][0]),
                               x2b_free[0][1]), inc="v")
                    v_x2h = P.val("v")
                    v_qq = quant1(x2b[0][0:2, :], slice(0, 2), 0, (("v", v_x2h),),
                                  xqi, xqsb, (("t", xqsT_free),))
                    ew("vector", lambda e: e.tensor_scalar(
                        out=qhb[0:2, :], in0=xqsb[0:2, :], scalar1=hmk[:, 0:1],
                        scalar2=None, op0=ALU.mult),
                        waits=(("v", v_qq),) + dx_small, inc="v")
                    v_mq = P.val("v")
                    for j in range(8):
                        ew("tensor", lambda e, j=j: e.matmul(
                            out=ps4[:, j * 2:(j + 1) * 2],
                            lhsT=qhb[0:2, j * 128:(j + 1) * 128], rhs=id2,
                            start=True, stop=True, **MM),
                            waits=(("v", v_mq), ("v", ps4_free)) if j == 0 else (),
                            inc="t")
                    t_tp = P.val("t")
                    ew("vector", lambda e: e.tensor_copy(
                        out=xqsTh[:], in_=ps4[:, 0:16].rearrange("p (c t) -> p c t", c=8)),
                        waits=(("t", t_tp),), inc="v")
                    v_xqsTh = P.val("v")

                    P.fence()
                    P.flush(block, sems)
                    if DEBUG_PHASES < 3:
                        return nc

            with ExitStack() as esC:
                xqsTb = [sb(esC, f"xqsTb{i}", [128, 8, 512], BF16) for i in range(2)]
                t1s = [sb(esC, f"t1s{i}", [128, 8, 1024], BF16) for i in range(2)]
                hbuf = [sb(esC, f"hbuf{i}", [128, 32, 514], BF16) for i in range(2)]
                tconv = [sb(esC, f"tconv{i}", [128, 512], F32) for i in range(2)]
                mxb = sb(esC, "mxb", [128, 512], BF16)
                mnb = sb(esC, "mnb", [128, 512], BF16)
                s2row = sb(esC, "s2row", [1, 512], F32)
                s2b = sb(esC, "s2b", [128, 512], F32)
                qi32 = [sb(esC, f"qi32_{i}", [128, 512], I32) for i in range(2)]

                xqsTb_free = {0: 0, 1: 0}
                t1s_free = {0: 0, 1: 0}
                hbuf_free = {0: (), 1: ()}
                hbuf_ready = {}
                evac_s = {i: [0] * 32 for i in range(NT)}
                psH_free = {0: 0, 1: 0}
                copyleft_done = {}
                ps4c_free = 0
                mx_free_t = 0
                ps3b_free_c = 0
                ps5c_free = 0
                ps6c_free = 0
                stat_c_free = 0
                v_hedge = [0]

                def gen_mm1(i):
                    par = i % 2
                    for cc in range(8):
                        dma("scalar", "dw", xqsTb[par][:, cc, :],
                            xqs_d[cc, :, i * 512:(i + 1) * 512],
                            waits=dg_xqs + (("t", xqsTb_free[par]),)
                            if cc == 0 else ())
                    dxq = qdone("dw")
                    for g in range(4):
                        gpar = g % 2
                        for dc in range(8):
                            dma("scalar", "dw", t1s[gpar][:, dc, :],
                                tern1_d[dc * 128:(dc + 1) * 128,
                                        g * 1024:(g + 1) * 1024],
                                waits=(("t", t1s_free[gpar]),
                                       wstate[("p2g", 7)])
                                if dc == 0 else ())
                        dt1 = qdone("dw")
                        for fj in range(8):
                            fc = g * 8 + fj
                            hpar = fc % 2
                            ph = ps1 if hpar == 0 else ps2
                            for dc in range(8):
                                w0 = []
                                if fj == 0 and dc == 0:
                                    w0 += list(dt1) + list(dxq)
                                    if i == 0 and g == 0:
                                        w0.append(("v", v_xqsTh))
                                if dc == 0:
                                    w0.append(("s", psH_free[hpar]))
                                if dc == 0 and fc == 0 and i >= 1:
                                    w0 += list(hbuf_free[par]) + \
                                        [("v", copyleft_done.get(i - 1, 0))]
                                ew("tensor", lambda e, gpar=gpar, dc=dc, fj=fj,
                                    ph=ph, par=par: e.matmul(
                                        out=ph[:],
                                        lhsT=t1s[gpar][:, dc, fj * 128:(fj + 1) * 128],
                                        rhs=xqsTb[par][:, dc, :],
                                        start=(dc == 0), stop=(dc == 7), **MM),
                                    waits=tuple(w0), inc="t")
                                if i == 0:
                                    ew("tensor", lambda e, gpar=gpar, dc=dc, fj=fj,
                                        fc=fc: e.matmul(
                                            out=ps4[:, fc * 2:(fc + 1) * 2],
                                            lhsT=t1s[gpar][:, dc,
                                                           fj * 128:(fj + 1) * 128],
                                            rhs=xqsTh[:, dc, :],
                                            start=(dc == 0), stop=(dc == 7), **MM),
                                        inc="t")
                            t_f = P.val("t")
                            ew("scalar", lambda e, ph=ph, par=par, fc=fc: e.copy(
                                out=hbuf[par][:, fc, 1:513], in_=ph[:]),
                                waits=(("t", t_f),), inc="s")
                            psH_free[hpar] = P.val("s")
                            evac_s[i][fc] = P.val("s")
                            yield True
                        t1s_free[gpar] = P.val("t")
                    xqsTb_free[par] = P.val("t")
                    hbuf_ready[i] = P.val("s")
                    if i == 0:
                        t_pig = P.val("t")
                        ew("vector", lambda e: e.tensor_copy(
                            out=hedge[:], in_=ps4[:, 0:64]),
                            waits=(("t", t_pig),), inc="v")
                        v_hedge[0] = P.val("v")

                def emit_copyleft(i):
                    par = i % 2
                    if i > 0:
                        src = hbuf[(i - 1) % 2][:, :, 512:513]
                        wts = (("s", hbuf_ready[i]),)
                    else:
                        src = hedge[:].rearrange("p (f c) -> p f c", c=2)[:, :, 0:1]
                        wts = (("s", hbuf_ready[i]), ("v", v_hedge[0]))
                    ew("vector", lambda e, par=par, src=src: e.tensor_copy(
                        out=hbuf[par][:, :, 0:1], in_=src), waits=wts, inc="v")
                    copyleft_done[i] = P.val("v")

                def gen_convquant(i):
                    nonlocal mx_free_t, ps3b_free_c, ps5c_free, ps6c_free, stat_c_free
                    par = i % 2
                    silu_s = {}
                    conv_v = {}

                    def emit_folds(fc):
                        ch = hbuf[par][:, fc, :]
                        if fc == 0:
                            ew("vector", lambda e, ch=ch: e.tensor_copy(
                                out=mxb[:], in_=ch[:, 1:513]),
                                waits=(("s", silu_s[0]), ("t", mx_free_t)), inc="v")
                            ew("vector", lambda e, ch=ch: e.tensor_copy(
                                out=mnb[:], in_=ch[:, 1:513]), inc="v")
                        else:
                            ew("vector", lambda e, ch=ch: e.tensor_max(
                                out=mxb[:], in0=ch[:, 1:513], in1=mxb[:]),
                                waits=(("s", silu_s[fc]),), inc="v")
                            ew("vector", lambda e, ch=ch: e.tensor_tensor(
                                out=mnb[:], in0=ch[:, 1:513], in1=mnb[:],
                                op=ALU.min), inc="v")

                    tap1_s = {}

                    def emit_tap1(fc):
                        ch = hbuf[par][:, fc, :]
                        tc = tconv[fc % 2]
                        w0 = [("v", conv_v[fc - 2])] if fc >= 2 else []
                        ew("scalar", lambda e, ch=ch, fc=fc, tc=tc: e.activation(
                            out=tc[:], in_=ch[:, 1:513], func=AF.Identity,
                            scale=convp[:, 32 + fc:33 + fc]),
                            waits=tuple(w0), inc="s")
                        tap1_s[fc] = P.val("s")

                    emit_tap1(0)
                    for fc in range(32):
                        ch = hbuf[par][:, fc, :]
                        tc = tconv[fc % 2]
                        if i < NT - 1:
                            rh = hbuf[(i + 1) % 2][:, fc, 1:2]
                            wrh = (("s", evac_s[i + 1][fc]),)
                        else:
                            rh = hedge[:].rearrange(
                                "p (f c) -> p f c", c=2)[:, fc, 1:2]
                            wrh = (("v", v_hedge[0]),)
                        ew("vector", lambda e, ch=ch, rh=rh: e.tensor_copy(
                            out=ch[:, 513:514], in_=rh),
                            waits=wrh + (("s", evac_s[i][fc]),), inc="v")
                        if i + 1 < NT:
                            ew("vector", lambda e, ch=ch, fc=fc, i=i: e.tensor_copy(
                                out=hbuf[(i + 1) % 2][:, fc, 0:1],
                                in_=ch[:, 512:513]),
                                waits=(("s", evac_s[i + 1][fc]),), inc="v")
                            if fc == 31:
                                copyleft_done[i + 1] = P.val("v")
                        ew("vector", lambda e, ch=ch, fc=fc, tc=tc: e.scalar_tensor_tensor(
                            out=tc[:], in0=ch[:, 0:512], scalar=convp[:, fc:fc + 1],
                            in1=tc[:], op0=ALU.mult, op1=ALU.add),
                            waits=(("s", tap1_s[fc]),), inc="v")
                        ew("vector", lambda e, ch=ch, fc=fc, tc=tc: e.scalar_tensor_tensor(
                            out=tc[:], in0=ch[:, 2:514],
                            scalar=convp[:, 64 + fc:65 + fc], in1=tc[:],
                            op0=ALU.mult, op1=ALU.add), inc="v")
                        v_c = P.val("v")
                        conv_v[fc] = v_c
                        if fc + 1 < 32:
                            emit_tap1(fc + 1)
                        ew("scalar", lambda e, ch=ch, fc=fc, tc=tc: e.activation(
                            out=ch[:, 1:513], in_=tc[:], func=AF.Silu,
                            bias=convp[:, 96 + fc:97 + fc], scale=1.0),
                            waits=(("v", v_c),), inc="s")
                        silu_s[fc] = P.val("s")
                        if fc >= 1:
                            emit_folds(fc - 1)
                        yield True
                    emit_folds(31)
                    ew("vector", lambda e: e.scalar_tensor_tensor(
                        out=mxb[:], in0=mnb[:], scalar=-1.0, in1=mxb[:],
                        op0=ALU.mult, op1=ALU.max), inc="v")
                    v_amax = P.val("v")
                    for bq in range(4):
                        ew("tensor", lambda e, bq=bq: e.transpose(
                            out=ps3b[:, bq * 128:(bq + 1) * 128],
                            in_=mxb[:, bq * 128:(bq + 1) * 128], identity=idbf[:]),
                            waits=(("v", v_amax), ("v", ps3b_free_c))
                            if bq == 0 else (), inc="t")
                    t_tp = P.val("t")
                    mx_free_t = t_tp
                    for bq in range(4):
                        w0 = [("t", t_tp)] if bq == 0 else []
                        if bq == 0 and stat_c_free:
                            w0.append(("t", stat_c_free))
                        ew("vector", lambda e, bq=bq: e.tensor_reduce(
                            out=stat[:, 8 + bq:9 + bq],
                            in_=ps3b[:, bq * 128:(bq + 1) * 128],
                            axis=AX.X, op=ALU.max), waits=tuple(w0), inc="v")
                    ps3b_free_c = P.val("v")
                    vsettle()
                    for bq in range(4):
                        ew("vector", lambda e, bq=bq: e.tensor_scalar(
                            out=stat[:, 16 + bq:17 + bq], in0=stat[:, 8 + bq:9 + bq],
                            scalar1=EPS, scalar2=None, op0=ALU.max), inc="v")
                        vsettle()
                        ew("vector", lambda e, bq=bq: e.reciprocal(
                            out=stat[:, 20 + bq:21 + bq], in_=stat[:, 16 + bq:17 + bq]),
                            inc="v")
                        vsettle()
                        ew("vector", lambda e, bq=bq: e.tensor_scalar(
                            out=stat[:, 12 + bq:13 + bq], in0=stat[:, 20 + bq:21 + bq],
                            scalar1=127.0, scalar2=None, op0=ALU.mult), inc="v")
                        vsettle()
                        ew("vector", lambda e, bq=bq, i=i: e.scalar_tensor_tensor(
                            out=invb2res[:, i * 4 + bq:i * 4 + bq + 1],
                            in0=stat[:, 16 + bq:17 + bq], scalar=1.0 / 127,
                            in1=wsc[:, 3:4], op0=ALU.mult, op1=ALU.mult), inc="v")
                        vsettle()
                    v_s2 = P.val("v")
                    for bq in range(4):
                        ew("tensor", lambda e, bq=bq: e.transpose(
                            out=ps5[0:1, bq * 128:(bq + 1) * 128],
                            in_=stat[:, 12 + bq:13 + bq], identity=idf32[:]),
                            waits=(("v", v_s2), ("v", ps5c_free))
                            if bq == 0 else (), inc="t")
                    t_s2 = P.val("t")
                    stat_c_free = t_s2
                    ew("vector", lambda e: e.tensor_copy(
                        out=s2row[:], in_=ps5[0:1, 0:512]), waits=(("t", t_s2),), inc="v")
                    ps5c_free = P.val("v")
                    v_r = P.val("v")
                    ew("tensor", lambda e: e.matmul(
                        out=ps6[:], lhsT=ones1[:], rhs=s2row[:], start=True,
                        stop=True, **MM),
                        waits=(("v", v_r), ("v", ps6c_free)), inc="t")
                    t_b = P.val("t")
                    ew("vector", lambda e: e.tensor_copy(out=s2b[:], in_=ps6[:]),
                       waits=(("t", t_b),), inc="v")
                    ps6c_free = P.val("v")
                    v_s2b = P.val("v")
                    cast_s = {}
                    for fc in range(32):
                        ch = hbuf[par][:, fc, :]
                        w0 = [("v", v_s2b)] if fc == 0 else []
                        if fc >= 2:
                            w0.append(("s", cast_s[fc - 2]))
                        ew("vector", lambda e, ch=ch, fc=fc: e.tensor_tensor(
                            out=qi32[fc % 2][:], in0=ch[:, 1:513], in1=s2b[:],
                            op=ALU.mult), waits=tuple(w0), inc="v")
                        v_tt = P.val("v")
                        ew("scalar", lambda e, ch=ch, fc=fc: e.activation(
                            out=ch[:, 1:513], in_=qi32[fc % 2][:],
                            func=AF.Identity), waits=(("v", v_tt),), inc="s")
                        cast_s[fc] = P.val("s")
                        dma("gpsimd", "dg",
                            hq_d[fc, :, i * 512:(i + 1) * 512],
                            hbuf[par][:, fc, 1:513],
                            waits=(("s", cast_s[fc]),))
                        yield True
                    hbuf_free[par] = qdone("dg")

                for _ in gen_mm1(0):
                    pass
                emit_copyleft(0)
                for i in range(NT):
                    gmm = gen_mm1(i + 1) if i + 1 < NT else None
                    gcv = gen_convquant(i)
                    while True:
                        if gmm is not None and next(gmm, None) is None:
                            gmm = None
                        if next(gcv, None) is None:
                            break
                    while gmm is not None and next(gmm, None) is None:
                        gmm = None

                P.fence()
                P.flush(block, sems)
                if DEBUG_PHASES < 4:
                    return nc

            with ExitStack() as esD:
                hqb = [sb(esD, f"hqb{i}", [128, 512], BF16) for i in range(2)]
                t2res = sb(esD, "t2res", [128, 32, 1024], BF16)
                x2l = [sb(esD, f"x2l{i}", [128, 1024], F32) for i in range(2)]
                y2b = [sb(esD, f"y2b{i}", [128, 1024], F32) for i in range(2)]
                v2b = sb(esD, "v2b", [128, 1024], F32)
                ob = [sb(esD, f"ob{i}", [128, 1024], F32) for i in range(2)]

                accs = [(ps0[:, 0:512], ps0[:, 512:1024]),
                        (ps1[:], ps2[:]),
                        (ps3[:], ps4[:]),
                        (ps5[:], ps6[:])]
                hqb_free = {0: 0, 1: 0}
                x2l_free = {0: 0, 1: 0}
                ob_free = {0: ("v", 0), 1: ("v", 0)}
                acc_free_v = [0, 0, 0, 0]
                y2b_free_s = {0: 0, 1: 0}

                for i in range(NT):
                    for fc in range(32):
                        fpar = fc % 2
                        dh = dma("sync", "dx", hqb[fpar][:],
                                 hq_d[fc, :, i * 512:(i + 1) * 512],
                                 waits=(("t", hqb_free[fpar]),))
                        dt2 = dma("scalar", "dw", t2res[:, fc, :],
                                  tern2_d[fc * 128:(fc + 1) * 128, :]) \
                            if i == 0 else None
                        for tsub in range(4):
                            w0 = []
                            if tsub == 0:
                                w0.append(dh)
                                if dt2 is not None:
                                    w0.append(dt2)
                            if fc == 0:
                                w0.append(("v", acc_free_v[tsub]))
                            a0, a1 = accs[tsub]
                            ew("tensor", lambda e, fpar=fpar, tsub=tsub, a0=a0,
                                fc=fc: e.matmul(
                                    out=a0,
                                    lhsT=hqb[fpar][:, tsub * 128:(tsub + 1) * 128],
                                    rhs=t2res[:, fc, 0:512],
                                    start=(fc == 0), stop=(fc == 31), **MM),
                                waits=tuple(w0), inc="t")
                            ew("tensor", lambda e, fpar=fpar, tsub=tsub, a1=a1,
                                fc=fc: e.matmul(
                                    out=a1,
                                    lhsT=hqb[fpar][:, tsub * 128:(tsub + 1) * 128],
                                    rhs=t2res[:, fc, 512:1024],
                                    start=(fc == 0), stop=(fc == 31), **MM), inc="t")
                        hqb_free[fpar] = P.val("t")
                    t_last = P.val("t")
                    for tsub in range(4):
                        oc = i * 4 + tsub
                        par = oc % 2
                        a0, a1 = accs[tsub]
                        dx2 = dma("scalar", "dw", x2l[par][:],
                                  x2_d[oc * 128:(oc + 1) * 128, :],
                                  waits=(("v", x2l_free[par]),))
                        ew("vector", lambda e, a0=a0, i=i, tsub=tsub, par=par:
                            e.tensor_scalar(
                                out=y2b[par][:, 0:512], in0=a0,
                                scalar1=invb2res[:, i * 4 + tsub:i * 4 + tsub + 1],
                                scalar2=None, op0=ALU.mult),
                            waits=(("t", t_last), ("s", y2b_free_s[par])), inc="v")
                        ew("vector", lambda e, a1=a1, i=i, tsub=tsub, par=par:
                            e.tensor_scalar(
                                out=y2b[par][:, 512:1024], in0=a1,
                                scalar1=invb2res[:, i * 4 + tsub:i * 4 + tsub + 1],
                                scalar2=None, op0=ALU.mult), inc="v")
                        acc_free_v[tsub] = P.val("v")
                        v_y2 = P.val("v")
                        v_st = ln_stats(y2b[par][:], slice(0, 128), oc % 2,
                                        (("v", v_y2),))
                        ew("scalar", lambda e, oc=oc, par=par: e.activation(
                            out=v2b[:], in_=y2b[par][:], func=AF.Identity,
                            bias=stat[:, 16 * (oc % 2) + 3:16 * (oc % 2) + 4],
                            scale=stat[:, 16 * (oc % 2) + 2:16 * (oc % 2) + 3]),
                            waits=(("v", v_st),), inc="s")
                        s_v2 = P.val("s")
                        y2b_free_s[par] = s_v2
                        ew("vector", lambda e, par=par: e.tensor_add(
                            out=ob[par][:], in0=v2b[:], in1=x2l[par][:]),
                            waits=(("s", s_v2), dx2, ob_free[par]),
                            inc="v")
                        x2l_free[par] = P.val("v")
                        v_o = P.val("v")
                        ob_free[par] = dma("gpsimd", "dg",
                                           out_d[oc * 128:(oc + 1) * 128, :],
                                           ob[par][:], waits=(("v", v_o),))

                P.emit("gpsimd", None, waits=qdone("dg"))
                P.flush(block, sems)
    return nc


_cache = {}


def kernel(**inputs):
    x = np.ascontiguousarray(np.asarray(inputs["L0_attn_in_sequence_bwc"], np.float32))
    proj = np.asarray(inputs["L0_attn_projection_1nck"], np.float32)[0]
    halves = np.asarray(inputs["L0_attn_halves"], np.float32)
    diag = np.asarray(inputs["L0_attn_diagonals_nk"], np.float32)
    mw = np.asarray(inputs["L0_attn_mixer_w"], np.float32)
    mb = np.asarray(inputs["L0_attn_mixer_b"], np.float32)
    w1 = np.asarray(inputs["L0_ffn_w1"], np.float32)
    cw = np.asarray(inputs["L0_ffn_conv_w"], np.float32)
    cb = np.asarray(inputs["L0_ffn_conv_b"], np.float32)
    w2 = np.asarray(inputs["L0_ffn_w2"], np.float32)

    p2d = np.ascontiguousarray(proj.transpose(1, 0, 2).reshape(CDIM, NHEAD * KH))
    iu, ju = np.triu_indices(KH, k=1)
    M = np.zeros((NHEAD, KH, KH), np.float32)
    M[:, iu, ju] = halves
    M[:, ju, iu] = halves
    M[:, np.arange(KH), np.arange(KH)] = diag
    mblk0 = M.transpose(1, 0, 2).reshape(KH, NHEAD * KH)
    mblk = np.ascontiguousarray(np.concatenate([mblk0, mblk0], axis=0))
    mwt = np.ascontiguousarray(mw.T)
    mbias = np.ascontiguousarray(np.broadcast_to(mb[None, :], (128, CDIM)))
    w1t = np.ascontiguousarray(w1.T)
    w2t = np.ascontiguousarray(w2.T)
    convp = np.zeros((128, 128), np.float32)
    for tap in range(3):
        convp[:, tap * 32:(tap + 1) * 32] = cw[:, 0, tap].reshape(32, 128).T
    convp[:, 96:128] = cb.reshape(32, 128).T
    idbf = np.eye(128, dtype=ml_dtypes.bfloat16)
    idf32 = np.eye(128, dtype=np.float32)
    ones1 = np.ones((1, 128), np.float32)

    key = (DEBUG_PHASES, DEBUG_A_TILES, DEBUG_WPREP, DEBUG_A_STEPS, DEBUG_TAPS)
    if key not in _cache:
        _cache[key] = build_program()
    nc = _cache[key]

    in_maps = []
    for c in range(8):
        b, s = c // 4, c % 4
        xb = np.roll(x[b], -s * TOK, axis=0) if s else x[b]
        t0 = s * TOK
        xhalo = np.zeros((2, CDIM), np.float32)
        hm = np.zeros((2, 1), np.float32)
        if t0 - 1 >= 0:
            xhalo[0] = x[b, t0 - 1]
            hm[0, 0] = 1.0
        if t0 + TOK < SEQ:
            xhalo[1] = x[b, t0 + TOK]
            hm[1, 0] = 1.0
        in_maps.append({
            "xfull": np.ascontiguousarray(xb), "xhalo": xhalo, "hmask": hm,
            "p2d": p2d, "mblk": mblk, "mwt": mwt, "mbias": mbias,
            "w1t": w1t, "w2t": w2t, "convp": convp,
            "idbf": idbf, "idf32": idf32, "ones1": ones1,
        })

    res = run_bass_kernel_spmd(nc, in_maps, core_ids=list(range(8)))
    if DEBUG_TAPS:
        global _taps
        _taps = res.results
    out = np.empty((BATCH, SEQ, CDIM), np.float32)
    for c in range(8):
        b, s = c // 4, c % 4
        out[b, s * TOK:(s + 1) * TOK] = res.results[c]["out"]
    return out


# version 20
# speedup vs baseline: 1.0234x; 1.0234x over previous
import numpy as np
import ml_dtypes
from contextlib import ExitStack

import concourse.bass as bass
import concourse.mybir as mybir
from concourse.bass_utils import run_bass_kernel_spmd

F32 = mybir.dt.float32
BF16 = mybir.dt.bfloat16
I32 = mybir.dt.int32
AF = mybir.ActivationFunctionType
ALU = mybir.AluOpType
AX = mybir.AxisListType

BATCH, SEQ, CDIM = 2, 8192, 1024
NHEAD, KH = 16, 64
FDIM = 4096
TOK = 2048
TILE = 512
NT_A = SEQ // TILE
NT = TOK // TILE
EPS = 1e-5

ENGS = ("sync", "scalar", "vector", "tensor", "gpsimd")
DEBUG_PHASES = 4
DEBUG_A_TILES = NT_A
DEBUG_WPREP = True
DEBUG_A_STEPS = 9
DEBUG_TAPS = False


class Prog:
    def __init__(self):
        self.items = {e: [] for e in ENGS}
        self.cnt = {}

    def add_sem(self, name):
        self.cnt[name] = 0

    def val(self, name):
        return self.cnt[name]

    def emit(self, eng, fn=None, waits=(), inc=None, inc_by=1):
        ws = tuple((s, v) for s, v in waits if v > 0)
        for s, v in ws:
            assert v <= self.cnt[s], f"wait on future {s}:{v}>{self.cnt[s]}"
        if inc is not None:
            assert fn is not None
            self.cnt[inc] += inc_by
        self.items[eng].append((fn, ws, inc, inc_by))
        return self.cnt[inc] if inc is not None else None

    def fence(self):
        snap = [(s, v) for s, v in self.cnt.items() if v > 0]
        for e in ENGS:
            self.emit(e, None, waits=snap)

    def flush(self, block, sems):
        def runner(items):
            def go(eng):
                for fn, waits, inc, inc_by in items:
                    for snm, v in waits:
                        eng.wait_ge(sems[snm], v)
                    if fn is not None:
                        inst = fn(eng)
                        if inc is not None:
                            inst.then_inc(sems[inc], inc_by)
            return go
        for e in ENGS:
            if self.items[e]:
                getattr(block, e)(runner(self.items[e]))
            self.items[e] = []


def build_program():
    nc = bass.Bass("TRN2", target_bir_lowering=False, detect_race_conditions=False)

    xfull = nc.dram_tensor("xfull", [SEQ, CDIM], F32, kind="ExternalInput").ap()
    xhalo = nc.dram_tensor("xhalo", [2, CDIM], F32, kind="ExternalInput").ap()
    hmask = nc.dram_tensor("hmask", [2, 1], F32, kind="ExternalInput").ap()
    p2d_d = nc.dram_tensor("p2d", [CDIM, CDIM], F32, kind="ExternalInput").ap()
    mblk_d = nc.dram_tensor("mblk", [128, NHEAD * KH], F32, kind="ExternalInput").ap()
    mwt_d = nc.dram_tensor("mwt", [CDIM, CDIM], F32, kind="ExternalInput").ap()
    mbias_d = nc.dram_tensor("mbias", [128, CDIM], F32, kind="ExternalInput").ap()
    w1t_d = nc.dram_tensor("w1t", [CDIM, FDIM], F32, kind="ExternalInput").ap()
    w2t_d = nc.dram_tensor("w2t", [FDIM, CDIM], F32, kind="ExternalInput").ap()
    convp_d = nc.dram_tensor("convp", [128, 128], F32, kind="ExternalInput").ap()
    idbf_d = nc.dram_tensor("idbf", [128, 128], BF16, kind="ExternalInput").ap()
    idf32_d = nc.dram_tensor("idf32", [128, 128], F32, kind="ExternalInput").ap()
    ones1_d = nc.dram_tensor("ones1", [1, 128], F32, kind="ExternalInput").ap()
    out_d = nc.dram_tensor("out", [TOK, CDIM], F32, kind="ExternalOutput").ap()

    KIND = "ExternalOutput" if DEBUG_TAPS else "Internal"
    tern1_d = nc.dram_tensor("tern1_s", [CDIM, FDIM], BF16, kind=KIND).ap()
    tern2_d = nc.dram_tensor("tern2_s", [FDIM, CDIM], BF16, kind=KIND).ap()
    x2_d = nc.dram_tensor("x2_s", [TOK, CDIM], F32, kind=KIND).ap()
    xqs_d = nc.dram_tensor("xqs_s", [8, 128, TOK], BF16, kind=KIND).ap()
    hq_d = nc.dram_tensor("hq_s", [32, 128, TOK], BF16, kind=KIND).ap()
    if DEBUG_TAPS:
        dbg_g = nc.dram_tensor("dbg_g", [KH, 1024], BF16, kind="ExternalOutput").ap()
        dbg_a = nc.dram_tensor("dbg_a", [128, 1024], BF16, kind="ExternalOutput").ap()
        dbg_q = nc.dram_tensor("dbg_q", [128, 8, 512], BF16, kind="ExternalOutput").ap()
        dbg_y = nc.dram_tensor("dbg_y", [128, 1024], F32, kind="ExternalOutput").ap()
        dbg_st = nc.dram_tensor("dbg_st", [128, 32], F32, kind="ExternalOutput").ap()
        dbg_u = nc.dram_tensor("dbg_u", [128, 1024], BF16, kind="ExternalOutput").ap()
        dbg_uT = nc.dram_tensor("dbg_uT", [128, 1024], BF16, kind="ExternalOutput").ap()
        dbg_qt = nc.dram_tensor("dbg_qt", [128, 1024], BF16, kind="ExternalOutput").ap()
        dbg_xt = nc.dram_tensor("dbg_xt", [128, 1024], F32, kind="ExternalOutput").ap()
        dbg_sq = nc.dram_tensor("dbg_sq", [128, 1024], F32, kind="ExternalOutput").ap()
        dbg_qall = nc.dram_tensor("dbg_qall", [64, 128, 1024], BF16, kind="ExternalOutput").ap()

    P = Prog()
    DMA_SEMS = {q: [f"{q}{i}" for i in range(4)] for q in ("dx", "dw", "dg")}
    for q in DMA_SEMS:
        for snm in DMA_SEMS[q]:
            P.add_sem(snm)
    for snm in ("v", "s", "t"):
        P.add_sem(snm)
    _rr = {q: 0 for q in DMA_SEMS}

    MM = dict(skip_group_check=True)

    with ExitStack() as es:
        def sb(stack, name, shape, dt):
            return stack.enter_context(nc.sbuf_tensor(name, shape, dt))

        sems = {}
        for q in DMA_SEMS:
            for snm in DMA_SEMS[q]:
                sems[snm] = es.enter_context(nc.semaphore(snm))
        for snm in ("v", "s", "t"):
            sems[snm] = es.enter_context(nc.semaphore(snm))

        ps0 = es.enter_context(nc.psum_tensor("ps0", [128, 1024], F32))
        ps1 = es.enter_context(nc.psum_tensor("ps1", [128, 512], F32))
        ps2 = es.enter_context(nc.psum_tensor("ps2", [128, 512], F32))
        ps3 = es.enter_context(nc.psum_tensor("ps3", [128, 512], F32))
        ps4 = es.enter_context(nc.psum_tensor("ps4", [128, 512], F32))
        ps5 = es.enter_context(nc.psum_tensor("ps5", [128, 512], F32))
        ps6 = es.enter_context(nc.psum_tensor("ps6", [128, 512], F32))
        ps3b = ps3[:].bitcast(BF16)
        ps6b = ps6[:].bitcast(BF16)

        pb = sb(es, "pb", [128, 8, 1024], BF16)
        mwtb = sb(es, "mwtb", [128, 8, 1024], BF16)
        mblkb = sb(es, "mblkb", [128, NHEAD * KH], BF16)
        ablk = sb(es, "ablk", [128, 8 * 128], BF16)
        mbias = sb(es, "mbias_t", [128, 1024], F32)
        convp = sb(es, "convp_t", [128, 128], F32)
        idbf = sb(es, "idbf_t", [128, 128], BF16)
        idf32 = sb(es, "idf32_t", [128, 128], F32)
        ones1 = sb(es, "ones1_t", [1, 128], F32)
        gsb = sb(es, "gsb", [128, 8 * 128], BF16)
        cpool = sb(es, "cpool", [128, 4], F32)
        wsc = sb(es, "wsc", [128, 4], F32)
        invb2res = sb(es, "invb2res", [128, 16], F32)
        xqsTh = sb(es, "xqsTh", [128, 8, 2], BF16)
        hedge = sb(es, "hedge", [128, 64], BF16)
        stat = sb(es, "stat", [128, 32], F32)
        tiny = sb(es, "tiny", [128, 16], F32)
        sqb = sb(es, "sqb", [128, 1024], F32)
        xh = sb(es, "xh_t", [2, CDIM], F32)
        zcol = sb(es, "zcol", [128, KH], BF16)
        hmk = sb(es, "hmk_t", [2, 1], F32)

        def dma(qeng, queue, out, in_, waits=()):
            qsem = DMA_SEMS[queue][_rr[queue] % len(DMA_SEMS[queue])]
            _rr[queue] += 1
            def f(e, out=out, in_=in_):
                return e.dma_start(out=out, in_=in_)
            waits = tuple(waits) + ((qsem, P.val(qsem)),)
            v = P.emit(qeng, f, waits=waits, inc=qsem, inc_by=16)
            return (qsem, v)

        def qdone(queue):
            return tuple((snm, P.val(snm)) for snm in DMA_SEMS[queue])

        def ew(eng, fn, waits=(), inc=None):
            return P.emit(eng, fn, waits=waits, inc=inc, inc_by=1)

        def vsettle():
            P.emit("vector", None, waits=(("v", P.val("v")),))

        ms_v = {0: 0, 1: 0}
        sqb_free = [0]
        sqrt_s = {0: 0, 1: 0}

        def ln_stats(xap, pslice, par, wx):
            o = 16 * par
            ew("vector", lambda e: e.tensor_reduce(
                out=stat[pslice, o:o + 1], in_=xap, axis=AX.X, op=ALU.add),
                waits=wx, inc="v")
            vsettle()
            ew("scalar", lambda e: e.activation(
                out=sqb[pslice, :], in_=xap, func=AF.Square),
                waits=wx + (("v", sqb_free[0]),), inc="s")
            sq_s = P.val("s")
            ew("vector", lambda e: e.tensor_reduce(
                out=stat[pslice, o + 1:o + 2], in_=sqb[pslice, :], axis=AX.X,
                op=ALU.add), waits=(("v", ms_v[par]), ("s", sq_s)), inc="v")
            vsettle()
            sqb_free[0] = P.val("v")
            ew("vector", lambda e: e.tensor_scalar(
                out=stat[pslice, o + 4:o + 5], in0=stat[pslice, o:o + 1],
                scalar1=1.0 / 1024, scalar2=None, op0=ALU.mult), inc="v")
            vsettle()
            ew("vector", lambda e: e.tensor_scalar(
                out=stat[pslice, o + 5:o + 6], in0=stat[pslice, o + 1:o + 2],
                scalar1=1.0 / 1024, scalar2=None, op0=ALU.mult), inc="v")
            vsettle()
            ms_v[par] = P.val("v")
            ew("vector", lambda e: e.scalar_tensor_tensor(
                out=stat[pslice, o + 6:o + 7], in0=stat[pslice, o + 4:o + 5],
                scalar=stat[pslice, o + 4:o + 5], in1=stat[pslice, o + 5:o + 6],
                op0=ALU.mult, op1=ALU.subtract),
                waits=(("s", sqrt_s[par]),), inc="v")
            vsettle()
            vV = P.val("v")
            ew("scalar", lambda e: e.activation(
                out=stat[pslice, o + 7:o + 8], in_=stat[pslice, o + 6:o + 7],
                func=AF.Sqrt, bias=cpool[pslice, 0:1], scale=-1.0),
                waits=(("v", vV),), inc="s")
            sqrt_s[par] = P.val("s")
            vS2 = P.val("s")
            ew("vector", lambda e: e.reciprocal(
                out=stat[pslice, o + 2:o + 3], in_=stat[pslice, o + 7:o + 8]),
                waits=(("s", vS2),), inc="v")
            vsettle()
            ew("vector", lambda e: e.scalar_tensor_tensor(
                out=stat[pslice, o + 3:o + 4], in0=stat[pslice, o + 4:o + 5],
                scalar=-1.0, in1=stat[pslice, o + 2:o + 3],
                op0=ALU.mult, op1=ALU.mult), inc="v")
            vsettle()
            return P.val("v")

        def quant1(x2ap, pslice, par, wready, xqi, xqsb, xqs_gate):
            o = 16 * par
            ln_stats(x2ap, pslice, par, wready)
            ew("vector", lambda e: e.tensor_reduce(
                out=stat[pslice, o + 8:o + 9], in_=x2ap, axis=AX.X, op=ALU.max), inc="v")
            vsettle()
            ew("vector", lambda e: e.tensor_reduce(
                out=stat[pslice, o + 9:o + 10], in_=x2ap, axis=AX.X, op=ALU.min), inc="v")
            vsettle()
            ew("vector", lambda e: e.tensor_sub(
                out=stat[pslice, o + 10:o + 11], in0=stat[pslice, o + 8:o + 9],
                in1=stat[pslice, o + 4:o + 5]), inc="v")
            vsettle()
            ew("vector", lambda e: e.tensor_sub(
                out=stat[pslice, o + 11:o + 12], in0=stat[pslice, o + 4:o + 5],
                in1=stat[pslice, o + 9:o + 10]), inc="v")
            vsettle()
            ew("vector", lambda e: e.tensor_max(
                out=stat[pslice, o + 12:o + 13], in0=stat[pslice, o + 10:o + 11],
                in1=stat[pslice, o + 11:o + 12]), inc="v")
            vsettle()
            ew("vector", lambda e: e.tensor_tensor(
                out=stat[pslice, o + 13:o + 14], in0=stat[pslice, o + 12:o + 13],
                in1=stat[pslice, o + 2:o + 3], op=ALU.mult), inc="v")
            vsettle()
            ew("vector", lambda e: e.tensor_scalar(
                out=stat[pslice, o + 14:o + 15], in0=stat[pslice, o + 13:o + 14],
                scalar1=EPS, scalar2=None, op0=ALU.max), inc="v")
            vsettle()
            ew("vector", lambda e: e.reciprocal(
                out=stat[pslice, o + 15:o + 16], in_=stat[pslice, o + 14:o + 15]), inc="v")
            vsettle()
            ew("vector", lambda e: e.tensor_scalar(
                out=stat[pslice, o + 8:o + 9], in0=stat[pslice, o + 15:o + 16],
                scalar1=127.0, scalar2=None, op0=ALU.mult), inc="v")
            vsettle()
            ew("vector", lambda e: e.tensor_tensor(
                out=stat[pslice, o + 9:o + 10], in0=stat[pslice, o + 2:o + 3],
                in1=stat[pslice, o + 8:o + 9], op=ALU.mult), inc="v")
            vsettle()
            ew("vector", lambda e: e.scalar_tensor_tensor(
                out=stat[pslice, o + 10:o + 11], in0=stat[pslice, o + 4:o + 5],
                scalar=-1.0, in1=stat[pslice, o + 9:o + 10],
                op0=ALU.mult, op1=ALU.mult), inc="v")
            vsettle()
            ew("vector", lambda e: e.scalar_tensor_tensor(
                out=stat[pslice, o + 11:o + 12], in0=stat[pslice, o + 14:o + 15],
                scalar=1.0 / 127, in1=wsc[pslice, 1:2],
                op0=ALU.mult, op1=ALU.mult), inc="v")
            vsettle()
            ew("vector", lambda e: e.tensor_scalar(
                out=xqi[pslice, :], in0=x2ap, scalar1=stat[pslice, o + 9:o + 10],
                scalar2=stat[pslice, o + 10:o + 11], op0=ALU.mult, op1=ALU.add), inc="v")
            vsettle()
            ew("vector", lambda e: e.tensor_scalar(
                out=xqsb[pslice, :], in0=xqi[pslice, :],
                scalar1=stat[pslice, o + 11:o + 12], scalar2=None, op0=ALU.mult),
                waits=xqs_gate, inc="v")
            vsettle()
            return P.val("v")

        with nc.Block() as block:
            with ExitStack() as esQ:
                qTres = sb(esQ, "qTres", [128, 8, TOK], BF16)
                esA = esQ.enter_context(ExitStack())
                pstage = sb(esA, "pstage", [128, 4 * 1024], F32)
                xt = [sb(esA, f"xt{i}", [128, 1024], F32) for i in range(4)]
                sqb2 = [sb(esA, f"sqb2_{i}", [128, 1024], F32) for i in range(2)]
                statA = sb(esA, "statA", [128, 96], F32)
                ub = [sb(esA, f"ub{i}", [128, 1024], BF16) for i in range(2)]
                uT = [sb(esA, f"uT{i}", [128, 1024], BF16) for i in range(2)]
                qtok = [sb(esA, f"qtok{i}", [128, 1024], BF16) for i in range(2)]
                wstage = [sb(esA, f"wstage{i}", [128, 4096], F32) for i in range(2)]
                wi32 = sb(esA, "wi32", [128, 4096], I32)
                wtb = [sb(esA, f"wtb{i}", [128, 4096], BF16) for i in range(2)]
                wabs = sb(esA, "wabs", [128, 48], F32)

                ew("vector", lambda e: e.memset(cpool[:, 0:1], EPS), inc="v")
                ew("vector", lambda e: e.memset(cpool[:, 1:2], 0.0), inc="v")
                ew("vector", lambda e: e.memset(cpool[:, 2:3], 1.0), inc="v")
                ew("vector", lambda e: e.memset(zcol[:], 0.0), inc="v")
                ew("vector", lambda e: e.memset(ablk[:], 0.0), inc="v")
                ew("vector", lambda e: e.memset(hedge[:], 0.0), inc="v")
                v_consts = P.val("v")

                dma("sync", "dx", mbias[:], mbias_d)
                dma("sync", "dx", convp[:], convp_d)
                dma("sync", "dx", idbf[:], idbf_d)
                dma("sync", "dx", idf32[:], idf32_d)
                dma("sync", "dx", ones1[:], ones1_d)
                dma("sync", "dx", xh[:], xhalo)
                dma("sync", "dx", hmk[:], hmask)
                dma("sync", "dx", pstage[:, 0:NHEAD * KH], mblk_d)
                dx_small = qdone("dx")
                ew("vector", lambda e: e.tensor_copy(
                    out=mblkb[:], in_=pstage[:, 0:NHEAD * KH]),
                    waits=dx_small, inc="v")
                v_gate = P.val("v")
                v_mblk = v_gate
                for dst_b, src_d in ((pb, p2d_d), (mwtb, mwt_d)):
                    for h in range(2):
                        for cc in range(4):
                            dma("sync", "dx",
                                pstage[:, cc * 1024:(cc + 1) * 1024],
                                src_d[(h * 4 + cc) * 128:(h * 4 + cc + 1) * 128, :],
                                waits=(("v", v_gate),) if cc == 0 else ())
                        dxp = qdone("dx")
                        for cc in range(4):
                            ew("vector", lambda e, dst_b=dst_b, cc=cc, h=h: e.tensor_copy(
                                out=dst_b[:, h * 4 + cc, :],
                                in_=pstage[:, cc * 1024:(cc + 1) * 1024]),
                                waits=dxp if cc == 0 else (), inc="v")
                        v_gate = P.val("v")
                v_pb = v_gate
                v_mwtb = v_gate

                wblocks = [("w1", bi) for bi in range(8)] + [("w2", bi) for bi in range(32)]
                wstate = {}

                def w_src(kind, bi):
                    if kind == "w1":
                        return w1t_d[bi * 128:(bi + 1) * 128, :], 4096
                    return w2t_d[bi * 128:(bi + 1) * 128, :], 1024

                def emit_wpass1(idx):
                    kind, bi = wblocks[idx]
                    src, ncols = w_src(kind, bi)
                    dst = wstage[idx % 2][:, 0:ncols]
                    wfree = wstate.get(("p1v", idx - 2), 0)
                    d = dma("scalar", "dw", dst, src, waits=(("v", wfree),))
                    col = bi if kind == "w1" else 8 + bi
                    ew("vector", lambda e, dst=dst, col=col: e.tensor_reduce(
                        out=wabs[:, col:col + 1], in_=dst, axis=AX.X, op=ALU.add,
                        apply_absolute_value=True), waits=(d,), inc="v")
                    wstate[("p1v", idx)] = P.val("v")

                def emit_wscales():
                    vA = wstate[("p1v", len(wblocks) - 1)]
                    vsettle()
                    ew("vector", lambda e: e.tensor_reduce(
                        out=tiny[:, 0:1], in_=wabs[:, 0:8], axis=AX.X, op=ALU.add),
                        waits=(("v", vA),), inc="v")
                    ew("vector", lambda e: e.tensor_reduce(
                        out=tiny[:, 1:2], in_=wabs[:, 8:40], axis=AX.X, op=ALU.add),
                        inc="v")
                    vsettle()
                    vv = P.val("v")
                    ew("tensor", lambda e: e.matmul(
                        out=ps6[0:1, 0:2], lhsT=cpool[:, 2:3], rhs=tiny[:, 0:2],
                        start=True, stop=True, **MM), waits=(("v", vv),), inc="t")
                    tt = P.val("t")
                    ew("vector", lambda e: e.tensor_scalar(
                        out=tiny[0:1, 2:3], in0=ps6[0:1, 0:1],
                        scalar1=1.0 / (CDIM * FDIM), scalar2=1e-5,
                        op0=ALU.mult, op1=ALU.max), waits=(("t", tt),), inc="v")
                    ew("vector", lambda e: e.tensor_scalar(
                        out=tiny[0:1, 3:4], in0=ps6[0:1, 1:2],
                        scalar1=1.0 / (CDIM * FDIM), scalar2=1e-5,
                        op0=ALU.mult, op1=ALU.max), inc="v")
                    ew("vector", lambda e: e.reciprocal(
                        out=tiny[0:1, 4:5], in_=tiny[0:1, 2:3]), inc="v")
                    ew("vector", lambda e: e.reciprocal(
                        out=tiny[0:1, 5:6], in_=tiny[0:1, 3:4]), inc="v")
                    vsettle()
                    vv2 = P.val("v")
                    ew("tensor", lambda e: e.matmul(
                        out=ps6[:, 4:8], lhsT=ones1[:], rhs=tiny[0:1, 2:6],
                        start=True, stop=True, **MM), waits=(("v", vv2),), inc="t")
                    tt2 = P.val("t")
                    ew("vector", lambda e: e.tensor_copy(out=wsc[:, 1:2], in_=ps6[:, 4:5]),
                       waits=(("t", tt2),), inc="v")
                    ew("vector", lambda e: e.tensor_copy(out=wsc[:, 3:4], in_=ps6[:, 5:6]),
                       inc="v")
                    ew("vector", lambda e: e.tensor_copy(out=wsc[:, 0:1], in_=ps6[:, 6:7]),
                       inc="v")
                    ew("vector", lambda e: e.tensor_copy(out=wsc[:, 2:3], in_=ps6[:, 7:8]),
                       inc="v")
                    wstate["scales_v"] = P.val("v")

                def emit_wpass2(idx):
                    kind, bi = wblocks[idx]
                    src, ncols = w_src(kind, bi)
                    dst = wstage[idx % 2][:, 0:ncols]
                    wfree = wstate.get(("p2v", idx - 2), 0)
                    d = dma("scalar", "dw", dst, src,
                            waits=(("v", wstate["scales_v"]), ("v", wfree)))
                    scol = wsc[:, 0:1] if kind == "w1" else wsc[:, 2:3]
                    ew("vector", lambda e, dst=dst, scol=scol, ncols=ncols: e.tensor_scalar(
                        out=wi32[:, 0:ncols], in0=dst, scalar1=scol, scalar2=None,
                        op0=ALU.mult), waits=(d,), inc="v")
                    ew("vector", lambda e, ncols=ncols, idx=idx: e.tensor_scalar(
                        out=wtb[idx % 2][:, 0:ncols], in0=wi32[:, 0:ncols], scalar1=1.0,
                        scalar2=-1.0, op0=ALU.min, op1=ALU.max),
                        waits=(wstate.get(("p2g", idx - 2), ("v", 0)),), inc="v")
                    wstate[("p2v", idx)] = P.val("v")
                    tdst = (tern1_d if kind == "w1" else tern2_d)[bi * 128:(bi + 1) * 128, :]
                    wstate[("p2g", idx)] = dma("gpsimd", "dg", tdst, wtb[idx % 2][:, 0:ncols],
                                               waits=(("v", P.val("v")),))

                wq1 = iter(range(len(wblocks)))
                wq2 = iter(range(len(wblocks)))
                scales_done = [False]

                psA_free = {0: ("s", 0), 1: ("s", 0)}
                psT_free = ("v", 0)
                xt4_free = {k: (0, 0) for k in range(4)}
                sqb2_free = {0: 0, 1: 0}
                statA_sfree = {0: 0, 1: 0}
                t_G_last = 0
                ub_free = {0: 0, 1: 0}
                uT_free = {0: 0, 1: 0}
                qtok_free = {0: 0, 1: 0}
                qtok_dump = {0: ("v", 0), 1: ("v", 0)}

                pendG = [None]
                ps6T_free = [0]

                def flush_G(gt, par, s_qk, owned):
                    nonlocal t_G_last
                    if gt == 0:
                        for pg in (ps1, ps2):
                            ew("tensor", lambda e, pg=pg, par=par: e.matmul(
                                out=pg[0:KH, :], lhsT=zcol[:, :],
                                rhs=qtok[par][:, 0:512],
                                start=True, stop=False, **MM),
                                waits=(("v", s_qk),) if pg is ps1 else (),
                                inc="t")
                    for n in range(16):
                        pg = ps1 if n < 8 else ps2
                        off = (n % 8) * 64
                        ew("tensor", lambda e, par=par, n=n, pg=pg, off=off, gt=gt:
                            e.matmul(
                                out=pg[0:KH, off:off + KH],
                                lhsT=qtok[par][:, n * KH:(n + 1) * KH],
                                rhs=qtok[par][:, n * KH:(n + 1) * KH],
                                start=False, stop=(gt == DEBUG_A_TILES * 4 - 1), **MM),
                            waits=(("v", s_qk),) if n == 0 else (), inc="t")
                    t_G_last = P.val("t")
                    if owned:
                        for j in range(8):
                            ew("tensor", lambda e, par=par, j=j: e.transpose(
                                out=ps6b[:, j * 128:(j + 1) * 128],
                                in_=qtok[par][:, j * 128:(j + 1) * 128],
                                identity=idbf[:]),
                                waits=(("v", ps6T_free[0]),) if j == 0 else (),
                                inc="t")
                        t_tp2 = P.val("t")
                        ew("vector", lambda e, gt=gt: e.tensor_copy(
                            out=qTres[:, :, gt * 128:(gt + 1) * 128],
                            in_=ps6b[:].rearrange("p (c t) -> p c t", c=8)),
                            waits=(("t", t_tp2),), inc="v")
                        ps6T_free[0] = P.val("v")
                    qtok_free[par] = P.val("t")

                def emit_pass1_unit(t, k):
                    so2 = 48 * (t % 2)
                    gt2 = t * 4 + k
                    d = dma("sync", "dx", xt[k][:],
                            xfull[gt2 * 128:(gt2 + 1) * 128, :],
                            waits=(("v", xt4_free[k][0]), ("s", xt4_free[k][1])))
                    ew("vector", lambda e, k=k, so2=so2: e.tensor_reduce(
                        out=statA[:, so2 + k:so2 + k + 1], in_=xt[k][:], axis=AX.X,
                        op=ALU.add), waits=(d, ("v", v_consts),
                                            ("s", statA_sfree[t % 2])), inc="v")
                    ew("scalar", lambda e, k=k: e.activation(
                        out=sqb2[k % 2][:], in_=xt[k][:], func=AF.Square),
                        waits=(d, ("v", sqb2_free[k % 2])), inc="s")
                    sq_s = P.val("s")
                    ew("vector", lambda e, k=k, so2=so2: e.tensor_reduce(
                        out=statA[:, so2 + 4 + k:so2 + 5 + k], in_=sqb2[k % 2][:],
                        axis=AX.X, op=ALU.add), waits=(("s", sq_s),), inc="v")
                    sqb2_free[k % 2] = P.val("v")

                def emit_chain(t):
                    so = 48 * (t % 2)
                    vsettle()
                    ew("vector", lambda e, so=so: e.tensor_scalar(
                        out=statA[:, so + 8:so + 12], in0=statA[:, so + 0:so + 4],
                        scalar1=1.0 / 1024, scalar2=None, op0=ALU.mult), inc="v")
                    vsettle()
                    ew("vector", lambda e, so=so: e.tensor_scalar(
                        out=statA[:, so + 12:so + 16], in0=statA[:, so + 4:so + 8],
                        scalar1=1.0 / 1024, scalar2=None, op0=ALU.mult), inc="v")
                    vsettle()
                    ew("vector", lambda e, so=so: e.tensor_tensor(
                        out=statA[:, so + 16:so + 20], in0=statA[:, so + 8:so + 12],
                        in1=statA[:, so + 8:so + 12], op=ALU.mult), inc="v")
                    vsettle()
                    ew("vector", lambda e, so=so: e.tensor_sub(
                        out=statA[:, so + 20:so + 24], in0=statA[:, so + 16:so + 20],
                        in1=statA[:, so + 12:so + 16]), inc="v")
                    vsettle()
                    vV = P.val("v")
                    ew("scalar", lambda e, so=so: e.activation(
                        out=statA[:, so + 24:so + 28], in_=statA[:, so + 20:so + 24],
                        func=AF.Sqrt, bias=cpool[:, 0:1], scale=-1.0),
                        waits=(("v", vV),), inc="s")
                    sA_sqrt = P.val("s")
                    ew("vector", lambda e, so=so: e.reciprocal(
                        out=statA[:, so + 28:so + 32], in_=statA[:, so + 24:so + 28]),
                        waits=(("s", sA_sqrt),), inc="v")
                    vsettle()
                    ew("vector", lambda e, so=so: e.scalar_tensor_tensor(
                        out=statA[:, so + 32:so + 36], in0=statA[:, so + 8:so + 12],
                        scalar=-1.0, in1=statA[:, so + 28:so + 32],
                        op0=ALU.mult, op1=ALU.mult), inc="v")
                    vsettle()
                    return P.val("v")

                for k in range(4):
                    emit_pass1_unit(0, k)
                v_stats = emit_chain(0)
                for t_i in range(DEBUG_A_TILES):
                    owned = t_i < NT
                    so = 48 * (t_i % 2)
                    for k in range(4):
                        gt = t_i * 4 + k
                        par = gt % 2
                        ew("scalar", lambda e, k=k, par=par, so=so: e.activation(
                            out=ub[par][:], in_=xt[k][:], func=AF.Identity,
                            bias=statA[:, so + 32 + k:so + 33 + k],
                            scale=statA[:, so + 28 + k:so + 29 + k]),
                            waits=(("v", v_stats), ("t", ub_free[par])), inc="s")
                        s_u = P.val("s")
                        xt4_free[k] = (v_stats, s_u)
                        if k == 3:
                            statA_sfree[t_i % 2] = s_u
                        for j in range(8):
                            ew("tensor", lambda e, par=par, j=j: e.transpose(
                                out=ps3b[:, j * 128:(j + 1) * 128],
                                in_=ub[par][:, j * 128:(j + 1) * 128],
                                identity=idbf[:]),
                                waits=(("s", s_u), psT_free) if j == 0 else (),
                                inc="t")
                        ub_free[par] = P.val("t")
                        t_tp = P.val("t")
                        ew("vector", lambda e, par=par: e.tensor_copy(
                            out=uT[par][:], in_=ps3b[:]),
                            waits=(("t", t_tp), ("t", uT_free[par])), inc="v")
                        psT_free = ("v", P.val("v"))
                        s_uT = ("v", P.val("v"))
                        if k >= 1 and t_i + 1 < DEBUG_A_TILES:
                            emit_pass1_unit(t_i + 1, k - 1)
                        if pendG[0] is not None:
                            flush_G(*pendG[0])
                            pendG[0] = None
                        qa0, qa1 = (ps0[:, 0:512], ps0[:, 512:1024]) if par == 0 \
                            else (ps4[:], ps5[:])
                        for cc in range(8):
                            w0 = (s_uT, psA_free[par], ("v", v_pb)) \
                                if cc == 0 else ()
                            ew("tensor", lambda e, par=par, cc=cc, qa0=qa0: e.matmul(
                                out=qa0,
                                lhsT=uT[par][:, cc * 128:(cc + 1) * 128],
                                rhs=pb[:, cc, 0:512],
                                start=(cc == 0), stop=(cc == 7), **MM),
                                waits=w0, inc="t")
                            ew("tensor", lambda e, par=par, cc=cc, qa1=qa1: e.matmul(
                                out=qa1,
                                lhsT=uT[par][:, cc * 128:(cc + 1) * 128],
                                rhs=pb[:, cc, 512:1024],
                                start=(cc == 0), stop=(cc == 7), **MM), inc="t")
                        uT_free[par] = P.val("t")
                        t_q = P.val("t")
                        ew("vector", lambda e, par=par, qa0=qa0, qa1=qa1:
                            (e.tensor_copy(out=qtok[par][:, 0:512], in_=qa0),
                             e.tensor_copy(out=qtok[par][:, 512:1024], in_=qa1))[-1],
                            waits=(("t", t_q), ("t", qtok_free[par]),
                                   qtok_dump[par]), inc="v")
                        psA_free[par] = ("v", P.val("v"))
                        s_qk = P.val("v")
                        if DEBUG_TAPS:
                            qdump = dma("gpsimd", "dg", dbg_qall[gt, :, :],
                                        qtok[par][:], waits=(("v", s_qk),))
                            qtok_dump[par] = qdump
                        if DEBUG_A_STEPS >= 5:
                            pendG[0] = (gt, par, s_qk, owned)
                    if t_i + 1 < DEBUG_A_TILES:
                        emit_pass1_unit(t_i + 1, 3)
                        v_stats = emit_chain(t_i + 1)
                    if not DEBUG_WPREP:
                        continue
                    if t_i < 7 or DEBUG_A_TILES < NT_A:
                        for _ in range(6):
                            idx = next(wq1, None)
                            if idx is not None:
                                emit_wpass1(idx)
                        if t_i == 6 and not scales_done[0]:
                            for idx in wq1:
                                emit_wpass1(idx)
                            emit_wscales()
                            scales_done[0] = True
                    else:
                        for _ in range(5):
                            idx = next(wq2, None)
                            if idx is not None:
                                emit_wpass2(idx)
                if pendG[0] is not None:
                    flush_G(*pendG[0])
                    pendG[0] = None
                if DEBUG_WPREP:
                    if not scales_done[0]:
                        for idx in wq1:
                            emit_wpass1(idx)
                        emit_wscales()
                        scales_done[0] = True
                    for idx in wq2:
                        emit_wpass2(idx)

                if DEBUG_A_STEPS < 6:
                    P.fence()
                    P.flush(block, sems)
                    return nc
                ew("vector", lambda e: e.tensor_copy(out=gsb[0:KH, 0:512], in_=ps1[0:KH, :]),
                   waits=(("t", t_G_last),), inc="v")
                ew("vector", lambda e: e.tensor_copy(out=gsb[0:KH, 512:1024], in_=ps2[0:KH, :]),
                   inc="v")
                v_gsb = P.val("v")
                for n in range(16):
                    pr, ei = n // 2, n % 2
                    ew("tensor", lambda e, n=n, pr=pr, ei=ei: e.matmul(
                        out=ps4[ei * KH:(ei + 1) * KH, pr * KH:(pr + 1) * KH],
                        lhsT=mblkb[0:KH, n * KH:(n + 1) * KH],
                        rhs=gsb[0:KH, n * KH:(n + 1) * KH],
                        start=True, stop=True,
                        tile_position=(0, ei * KH), **MM),
                        waits=(("v", v_gsb), ("v", v_mblk)) if n == 0 else (),
                        inc="t")
                t_a = P.val("t")
                for n in range(16):
                    pr, ei = n // 2, n % 2
                    ew("vector", lambda e, pr=pr, ei=ei: e.tensor_copy(
                        out=ablk[ei * KH:(ei + 1) * KH,
                                 pr * 128 + ei * KH:pr * 128 + (ei + 1) * KH],
                        in_=ps4[ei * KH:(ei + 1) * KH, pr * KH:(pr + 1) * KH]),
                        waits=(("t", t_a),) if n == 0 else (), inc="v")
                v_ablk = P.val("v")
                if DEBUG_TAPS:
                    dma("gpsimd", "dg", dbg_g, gsb[0:KH, :], waits=(("v", v_gsb),))
                    dma("gpsimd", "dg", dbg_a, ablk[:], waits=(("v", v_ablk),))
                    for cc in range(8):
                        dma("gpsimd", "dg", dbg_q[:, cc, :], qTres[:, cc, 0:512],
                            waits=(("v", P.val("v")),))

                P.fence()
                P.flush(block, sems)
                esA.close()
                if DEBUG_PHASES < 2:
                    return nc

                with ExitStack() as esB:
                    nudT = sb(esB, "nudT", [128, 8, 512], BF16)
                    yb = [sb(esB, f"yb{i}", [128, 1024], F32) for i in range(2)]
                    vb = sb(esB, "vb", [128, 1024], F32)
                    xob = [sb(esB, f"xob{i}", [128, 1024], F32) for i in range(2)]
                    x2b = [sb(esB, f"x2b{i}", [128, 1024], F32) for i in range(2)]
                    xqi = sb(esB, "xqi", [128, 1024], I32)
                    xqsb = sb(esB, "xqsb", [128, 1024], BF16)
                    xqsT = sb(esB, "xqsT", [128, 8, 512], BF16)
                    uTh = sb(esB, "uTh", [128, 8, 2], BF16)
                    nudTh = sb(esB, "nudTh", [128, 8, 2], BF16)
                    qhb = sb(esB, "qhb", [2, 1024], BF16)

                    xob_free = {0: 0, 1: 0}
                    x2b_free = {0: (0, ("v", 0)), 1: (0, ("v", 0))}
                    psT_free2 = ("v", 0)
                    ps4_free, ps5_free = 0, 0
                    ps0_free = 0
                    yb_free_s = {0: 0, 1: 0}
                    nudT_free_t = 0
                    xqsT_free = 0
                    xqsT_store_dg = ("v", 0)

                    for i in range(NT):
                        for p in range(8):
                            pg = ps4 if p % 2 == 0 else ps5
                            pfree = ps4_free if p % 2 == 0 else ps5_free
                            ew("tensor", lambda e, p=p, pg=pg, i=i: e.matmul(
                                out=pg[:], lhsT=ablk[:, p * 128:(p + 1) * 128],
                                rhs=qTres[:, p, i * 512:(i + 1) * 512],
                                start=True, stop=True, **MM),
                                waits=(("v", v_ablk), ("v", pfree)), inc="t")
                            tn = P.val("t")
                            ew("vector", lambda e, p=p, pg=pg: e.tensor_copy(
                                out=nudT[:, p, :], in_=pg[:]),
                                waits=(("t", tn), ("t", nudT_free_t)), inc="v")
                            if p % 2 == 0:
                                ps4_free = P.val("v")
                            else:
                                ps5_free = P.val("v")
                        v_nud = P.val("v")
                        for k in range(4):
                            oc = i * 4 + k
                            par = oc % 2
                            for jc in range(8):
                                w0 = (("v", v_nud), ("v", ps0_free), ("v", v_mwtb)) \
                                    if jc == 0 else ()
                                ew("tensor", lambda e, jc=jc, k=k: e.matmul(
                                    out=ps0[:, 0:512],
                                    lhsT=nudT[:, jc, k * 128:(k + 1) * 128],
                                    rhs=mwtb[:, jc, 0:512],
                                    start=(jc == 0), stop=(jc == 7), **MM),
                                    waits=w0, inc="t")
                                ew("tensor", lambda e, jc=jc, k=k: e.matmul(
                                    out=ps0[:, 512:1024],
                                    lhsT=nudT[:, jc, k * 128:(k + 1) * 128],
                                    rhs=mwtb[:, jc, 512:1024],
                                    start=(jc == 0), stop=(jc == 7), **MM), inc="t")
                            if k == 3:
                                nudT_free_t = P.val("t")
                            t_mix = P.val("t")
                            ew("vector", lambda e, par=par: e.tensor_add(
                                out=yb[par][:], in0=ps0[:], in1=mbias[:]),
                                waits=(("t", t_mix), ("s", yb_free_s[par])), inc="v")
                            ps0_free = P.val("v")
                            v_y = P.val("v")
                            if DEBUG_TAPS and i == 0 and k == 0:
                                dma("gpsimd", "dg", dbg_y, yb[par][:], waits=(("v", v_y),))
                            dxo = dma("sync", "dx", xob[par][:],
                                      xfull[oc * 128:(oc + 1) * 128, :],
                                      waits=(("v", xob_free[par]),))
                            v_st = ln_stats(yb[par][:], slice(0, 128), par, (("v", v_y),))
                            ew("scalar", lambda e, par=par: e.activation(
                                out=vb[:], in_=yb[par][:], func=AF.Identity,
                                bias=stat[:, 16 * par + 3:16 * par + 4],
                                scale=stat[:, 16 * par + 2:16 * par + 3]),
                                waits=(("v", v_st),), inc="s")
                            s_v = P.val("s")
                            yb_free_s[par] = s_v
                            ew("vector", lambda e, par=par: e.tensor_add(
                                out=x2b[par][:], in0=vb[:], in1=xob[par][:]),
                                waits=(("s", s_v), dxo,
                                       ("v", x2b_free[par][0]),
                                       x2b_free[par][1]), inc="v")
                            xob_free[par] = P.val("v")
                            v_x2 = P.val("v")
                            dgx2 = dma("gpsimd", "dg",
                                       x2_d[oc * 128:(oc + 1) * 128, :],
                                       x2b[par][:], waits=(("v", v_x2),))
                            v_q = quant1(x2b[par][:], slice(0, 128), par,
                                         (("v", v_x2),), xqi, xqsb,
                                         (("t", xqsT_free),))
                            if DEBUG_TAPS and i == 0 and k == 0:
                                dma("gpsimd", "dg", dbg_st, stat[:], waits=(("v", v_q),))
                                dma("gpsimd", "dg", dbg_xt, x2b[par][:], waits=(("v", v_q),))
                                dma("gpsimd", "dg", dbg_u, xqsb[:], waits=(("v", v_q),))
                                P.fence()
                            x2b_free[par] = (v_q, dgx2)
                            for j in range(8):
                                ew("tensor", lambda e, j=j: e.transpose(
                                    out=ps3b[:, j * 128:(j + 1) * 128],
                                    in_=xqsb[:, j * 128:(j + 1) * 128],
                                    identity=idbf[:]),
                                    waits=(("v", v_q), psT_free2)
                                    if j == 0 else (), inc="t")
                            xqsT_free = P.val("t")
                            t_tp = P.val("t")
                            ew("scalar", lambda e, k=k: e.copy(
                                out=xqsT[:, :, k * 128:(k + 1) * 128],
                                in_=ps3b[:].rearrange("p (c t) -> p c t", c=8)),
                                waits=(("t", t_tp), xqsT_store_dg), inc="s")
                            psT_free2 = ("s", P.val("s"))
                        sw = P.val("s")
                        for cc in range(8):
                            xqsT_store_dg = dma(
                                "gpsimd", "dg",
                                xqs_d[cc, :, i * 512:(i + 1) * 512],
                                xqsT[:, cc, :],
                                waits=(("s", sw),) if cc == 0 else ())
                    dg_xqs = qdone("dg")

                    id2 = idbf[0:2, 0:2]
                    v_st = ln_stats(xh[:], slice(0, 2), 0, dx_small)
                    ew("scalar", lambda e: e.activation(
                        out=xqsb[0:2, :], in_=xh[:], func=AF.Identity,
                        bias=stat[0:2, 3:4], scale=stat[0:2, 2:3]),
                        waits=(("v", v_st),), inc="s")
                    s_uh = P.val("s")
                    for j in range(8):
                        ew("tensor", lambda e, j=j: e.matmul(
                            out=ps4[:, j * 2:(j + 1) * 2],
                            lhsT=xqsb[0:2, j * 128:(j + 1) * 128], rhs=id2,
                            start=True, stop=True, **MM),
                            waits=(("s", s_uh), ("v", ps4_free)) if j == 0 else (),
                            inc="t")
                    t_tp = P.val("t")
                    ew("vector", lambda e: e.tensor_copy(
                        out=uTh[:], in_=ps4[:, 0:16].rearrange("p (c t) -> p c t", c=8)),
                        waits=(("t", t_tp),), inc="v")
                    ps4_free = P.val("v")
                    v_uTh = P.val("v")
                    for cc in range(8):
                        ew("tensor", lambda e, cc=cc: e.matmul(
                            out=ps0[0:2, 0:512], lhsT=uTh[:, cc, :],
                            rhs=pb[:, cc, 0:512], start=(cc == 0), stop=(cc == 7), **MM),
                            waits=(("v", v_uTh), ("v", ps0_free)) if cc == 0 else (),
                            inc="t")
                        ew("tensor", lambda e, cc=cc: e.matmul(
                            out=ps0[0:2, 512:1024], lhsT=uTh[:, cc, :],
                            rhs=pb[:, cc, 512:1024], start=(cc == 0), stop=(cc == 7),
                            **MM), inc="t")
                    t_qh = P.val("t")
                    ew("vector", lambda e: e.tensor_copy(out=qhb[:], in_=ps0[0:2, :]),
                       waits=(("t", t_qh),), inc="v")
                    ps0_free = P.val("v")
                    v_qh = P.val("v")
                    for j in range(8):
                        ew("tensor", lambda e, j=j: e.matmul(
                            out=ps4[:, j * 2:(j + 1) * 2],
                            lhsT=qhb[0:2, j * 128:(j + 1) * 128], rhs=id2,
                            start=True, stop=True, **MM),
                            waits=(("v", v_qh), ("v", ps4_free)) if j == 0 else (),
                            inc="t")
                    t_tp = P.val("t")
                    ew("vector", lambda e: e.tensor_copy(
                        out=uTh[:], in_=ps4[:, 0:16].rearrange("p (c t) -> p c t", c=8)),
                        waits=(("t", t_tp),), inc="v")
                    ps4_free = P.val("v")
                    v_qTh = P.val("v")
                    for p in range(8):
                        ew("tensor", lambda e, p=p: e.matmul(
                            out=ps4[:, 0:2], lhsT=ablk[:, p * 128:(p + 1) * 128],
                            rhs=uTh[:, p, :], start=True, stop=True, **MM),
                            waits=(("v", v_qTh), ("v", ps4_free)) if p == 0 else
                            (("v", ps4_free),), inc="t")
                        tn = P.val("t")
                        ew("vector", lambda e, p=p: e.tensor_copy(
                            out=nudTh[:, p, :], in_=ps4[:, 0:2]),
                            waits=(("t", tn),), inc="v")
                        ps4_free = P.val("v")
                    v_nudh = P.val("v")
                    for jc in range(8):
                        ew("tensor", lambda e, jc=jc: e.matmul(
                            out=ps0[0:2, 0:512], lhsT=nudTh[:, jc, :],
                            rhs=mwtb[:, jc, 0:512], start=(jc == 0), stop=(jc == 7),
                            **MM),
                            waits=(("v", v_nudh), ("v", ps0_free)) if jc == 0 else (),
                            inc="t")
                        ew("tensor", lambda e, jc=jc: e.matmul(
                            out=ps0[0:2, 512:1024], lhsT=nudTh[:, jc, :],
                            rhs=mwtb[:, jc, 512:1024], start=(jc == 0), stop=(jc == 7),
                            **MM), inc="t")
                    t_mixh = P.val("t")
                    ew("vector", lambda e: e.tensor_add(
                        out=yb[0][0:2, :], in0=ps0[0:2, :], in1=mbias[0:2, :]),
                        waits=(("t", t_mixh), ("s", yb_free_s[0])), inc="v")
                    ps0_free = P.val("v")
                    v_yh = P.val("v")
                    v_st = ln_stats(yb[0][0:2, :], slice(0, 2), 0, (("v", v_yh),))
                    ew("scalar", lambda e: e.activation(
                        out=vb[0:2, :], in_=yb[0][0:2, :], func=AF.Identity,
                        bias=stat[0:2, 3:4], scale=stat[0:2, 2:3]),
                        waits=(("v", v_st),), inc="s")
                    s_vh = P.val("s")
                    ew("vector", lambda e: e.tensor_add(
                        out=x2b[0][0:2, :], in0=vb[0:2, :], in1=xh[:]),
                        waits=(("s", s_vh), ("v", x2b_free[0][0]),
                               x2b_free[0][1]), inc="v")
                    v_x2h = P.val("v")
                    v_qq = quant1(x2b[0][0:2, :], slice(0, 2), 0, (("v", v_x2h),),
                                  xqi, xqsb, (("t", xqsT_free),))
                    ew("vector", lambda e: e.tensor_scalar(
                        out=qhb[0:2, :], in0=xqsb[0:2, :], scalar1=hmk[:, 0:1],
                        scalar2=None, op0=ALU.mult),
                        waits=(("v", v_qq),) + dx_small, inc="v")
                    v_mq = P.val("v")
                    for j in range(8):
                        ew("tensor", lambda e, j=j: e.matmul(
                            out=ps4[:, j * 2:(j + 1) * 2],
                            lhsT=qhb[0:2, j * 128:(j + 1) * 128], rhs=id2,
                            start=True, stop=True, **MM),
                            waits=(("v", v_mq), ("v", ps4_free)) if j == 0 else (),
                            inc="t")
                    t_tp = P.val("t")
                    ew("vector", lambda e: e.tensor_copy(
                        out=xqsTh[:], in_=ps4[:, 0:16].rearrange("p (c t) -> p c t", c=8)),
                        waits=(("t", t_tp),), inc="v")
                    v_xqsTh = P.val("v")

                    P.fence()
                    P.flush(block, sems)
                    if DEBUG_PHASES < 3:
                        return nc

            with ExitStack() as esC:
                xqsTb = [sb(esC, f"xqsTb{i}", [128, 8, 512], BF16) for i in range(2)]
                t1s = [sb(esC, f"t1s{i}", [128, 8, 1024], BF16) for i in range(2)]
                hbuf = [sb(esC, f"hbuf{i}", [128, 32, 514], BF16) for i in range(2)]
                tconv = [sb(esC, f"tconv{i}", [128, 512], F32) for i in range(2)]
                mxb = sb(esC, "mxb", [128, 512], BF16)
                mnb = sb(esC, "mnb", [128, 512], BF16)
                s2row = sb(esC, "s2row", [1, 512], F32)
                s2b = sb(esC, "s2b", [128, 512], F32)
                qi32 = [sb(esC, f"qi32_{i}", [128, 512], I32) for i in range(2)]

                xqsTb_free = {0: 0, 1: 0}
                t1s_free = {0: 0, 1: 0}
                hbuf_free = {0: (), 1: ()}
                hbuf_ready = {}
                evac_s = {i: [0] * 32 for i in range(NT)}
                psH_free = {0: 0, 1: 0}
                copyleft_done = {}
                ps4c_free = 0
                mx_free_t = 0
                ps3b_free_c = 0
                ps5c_free = 0
                ps6c_free = 0
                stat_c_free = 0
                v_hedge = [0]

                def gen_mm1(i):
                    par = i % 2
                    for cc in range(8):
                        dma("scalar", "dw", xqsTb[par][:, cc, :],
                            xqs_d[cc, :, i * 512:(i + 1) * 512],
                            waits=dg_xqs + (("t", xqsTb_free[par]),)
                            if cc == 0 else ())
                    dxq = qdone("dw")
                    for g in range(4):
                        gpar = g % 2
                        for dc in range(8):
                            dma("scalar", "dw", t1s[gpar][:, dc, :],
                                tern1_d[dc * 128:(dc + 1) * 128,
                                        g * 1024:(g + 1) * 1024],
                                waits=(("t", t1s_free[gpar]),
                                       wstate[("p2g", 7)])
                                if dc == 0 else ())
                        dt1 = qdone("dw")
                        for fj in range(8):
                            fc = g * 8 + fj
                            hpar = fc % 2
                            ph = ps1 if hpar == 0 else ps2
                            for dc in range(8):
                                w0 = []
                                if fj == 0 and dc == 0:
                                    w0 += list(dt1) + list(dxq)
                                    if i == 0 and g == 0:
                                        w0.append(("v", v_xqsTh))
                                if dc == 0:
                                    w0.append(("s", psH_free[hpar]))
                                if dc == 0 and fc == 0 and i >= 1:
                                    w0 += list(hbuf_free[par]) + \
                                        [("v", copyleft_done.get(i - 1, 0))]
                                ew("tensor", lambda e, gpar=gpar, dc=dc, fj=fj,
                                    ph=ph, par=par: e.matmul(
                                        out=ph[:],
                                        lhsT=t1s[gpar][:, dc, fj * 128:(fj + 1) * 128],
                                        rhs=xqsTb[par][:, dc, :],
                                        start=(dc == 0), stop=(dc == 7), **MM),
                                    waits=tuple(w0), inc="t")
                                if i == 0:
                                    ew("tensor", lambda e, gpar=gpar, dc=dc, fj=fj,
                                        fc=fc: e.matmul(
                                            out=ps4[:, fc * 2:(fc + 1) * 2],
                                            lhsT=t1s[gpar][:, dc,
                                                           fj * 128:(fj + 1) * 128],
                                            rhs=xqsTh[:, dc, :],
                                            start=(dc == 0), stop=(dc == 7), **MM),
                                        inc="t")
                            t_f = P.val("t")
                            ew("scalar", lambda e, ph=ph, par=par, fc=fc: e.copy(
                                out=hbuf[par][:, fc, 1:513], in_=ph[:]),
                                waits=(("t", t_f),), inc="s")
                            psH_free[hpar] = P.val("s")
                            evac_s[i][fc] = P.val("s")
                            yield True
                        t1s_free[gpar] = P.val("t")
                    xqsTb_free[par] = P.val("t")
                    hbuf_ready[i] = P.val("s")
                    if i == 0:
                        t_pig = P.val("t")
                        ew("vector", lambda e: e.tensor_copy(
                            out=hedge[:], in_=ps4[:, 0:64]),
                            waits=(("t", t_pig),), inc="v")
                        v_hedge[0] = P.val("v")

                def emit_copyleft(i):
                    par = i % 2
                    if i > 0:
                        src = hbuf[(i - 1) % 2][:, :, 512:513]
                        wts = (("s", hbuf_ready[i]),)
                    else:
                        src = hedge[:].rearrange("p (f c) -> p f c", c=2)[:, :, 0:1]
                        wts = (("s", hbuf_ready[i]), ("v", v_hedge[0]))
                    ew("vector", lambda e, par=par, src=src: e.tensor_copy(
                        out=hbuf[par][:, :, 0:1], in_=src), waits=wts, inc="v")
                    copyleft_done[i] = P.val("v")

                def gen_convquant(i):
                    nonlocal mx_free_t, ps3b_free_c, ps5c_free, ps6c_free, stat_c_free
                    par = i % 2
                    silu_s = {}
                    conv_v = {}

                    def emit_folds(fc):
                        ch = hbuf[par][:, fc, :]
                        if fc == 0:
                            ew("vector", lambda e, ch=ch: e.tensor_copy(
                                out=mxb[:], in_=ch[:, 1:513]),
                                waits=(("s", silu_s[0]), ("t", mx_free_t)), inc="v")
                            ew("vector", lambda e, ch=ch: e.tensor_copy(
                                out=mnb[:], in_=ch[:, 1:513]), inc="v")
                        else:
                            ew("vector", lambda e, ch=ch: e.tensor_max(
                                out=mxb[:], in0=ch[:, 1:513], in1=mxb[:]),
                                waits=(("s", silu_s[fc]),), inc="v")
                            ew("vector", lambda e, ch=ch: e.tensor_tensor(
                                out=mnb[:], in0=ch[:, 1:513], in1=mnb[:],
                                op=ALU.min), inc="v")

                    tap1_s = {}

                    def emit_tap1(fc):
                        ch = hbuf[par][:, fc, :]
                        tc = tconv[fc % 2]
                        w0 = [("v", conv_v[fc - 2])] if fc >= 2 else []
                        ew("scalar", lambda e, ch=ch, fc=fc, tc=tc: e.activation(
                            out=tc[:], in_=ch[:, 1:513], func=AF.Identity,
                            scale=convp[:, 32 + fc:33 + fc]),
                            waits=tuple(w0), inc="s")
                        tap1_s[fc] = P.val("s")

                    emit_tap1(0)
                    for fc in range(32):
                        ch = hbuf[par][:, fc, :]
                        tc = tconv[fc % 2]
                        if i < NT - 1:
                            rh = hbuf[(i + 1) % 2][:, fc, 1:2]
                            wrh = (("s", evac_s[i + 1][fc]),)
                        else:
                            rh = hedge[:].rearrange(
                                "p (f c) -> p f c", c=2)[:, fc, 1:2]
                            wrh = (("v", v_hedge[0]),)
                        ew("vector", lambda e, ch=ch, rh=rh: e.tensor_copy(
                            out=ch[:, 513:514], in_=rh),
                            waits=wrh + (("s", evac_s[i][fc]),), inc="v")
                        if i + 1 < NT:
                            ew("vector", lambda e, ch=ch, fc=fc, i=i: e.tensor_copy(
                                out=hbuf[(i + 1) % 2][:, fc, 0:1],
                                in_=ch[:, 512:513]),
                                waits=(("s", evac_s[i + 1][fc]),), inc="v")
                            if fc == 31:
                                copyleft_done[i + 1] = P.val("v")
                        ew("vector", lambda e, ch=ch, fc=fc, tc=tc: e.scalar_tensor_tensor(
                            out=tc[:], in0=ch[:, 0:512], scalar=convp[:, fc:fc + 1],
                            in1=tc[:], op0=ALU.mult, op1=ALU.add),
                            waits=(("s", tap1_s[fc]),), inc="v")
                        ew("vector", lambda e, ch=ch, fc=fc, tc=tc: e.scalar_tensor_tensor(
                            out=tc[:], in0=ch[:, 2:514],
                            scalar=convp[:, 64 + fc:65 + fc], in1=tc[:],
                            op0=ALU.mult, op1=ALU.add), inc="v")
                        v_c = P.val("v")
                        conv_v[fc] = v_c
                        if fc + 1 < 32:
                            emit_tap1(fc + 1)
                        ew("scalar", lambda e, ch=ch, fc=fc, tc=tc: e.activation(
                            out=ch[:, 1:513], in_=tc[:], func=AF.Silu,
                            bias=convp[:, 96 + fc:97 + fc], scale=1.0),
                            waits=(("v", v_c),), inc="s")
                        silu_s[fc] = P.val("s")
                        if fc >= 1:
                            emit_folds(fc - 1)
                        yield True
                    emit_folds(31)
                    ew("vector", lambda e: e.scalar_tensor_tensor(
                        out=mxb[:], in0=mnb[:], scalar=-1.0, in1=mxb[:],
                        op0=ALU.mult, op1=ALU.max), inc="v")
                    v_amax = P.val("v")
                    for bq in range(4):
                        ew("tensor", lambda e, bq=bq: e.transpose(
                            out=ps3b[:, bq * 128:(bq + 1) * 128],
                            in_=mxb[:, bq * 128:(bq + 1) * 128], identity=idbf[:]),
                            waits=(("v", v_amax), ("v", ps3b_free_c))
                            if bq == 0 else (), inc="t")
                    t_tp = P.val("t")
                    mx_free_t = t_tp
                    for bq in range(4):
                        w0 = [("t", t_tp)] if bq == 0 else []
                        if bq == 0 and stat_c_free:
                            w0.append(("t", stat_c_free))
                        ew("vector", lambda e, bq=bq: e.tensor_reduce(
                            out=stat[:, 8 + bq:9 + bq],
                            in_=ps3b[:, bq * 128:(bq + 1) * 128],
                            axis=AX.X, op=ALU.max), waits=tuple(w0), inc="v")
                    ps3b_free_c = P.val("v")
                    vsettle()
                    for bq in range(4):
                        ew("vector", lambda e, bq=bq: e.tensor_scalar(
                            out=stat[:, 16 + bq:17 + bq], in0=stat[:, 8 + bq:9 + bq],
                            scalar1=EPS, scalar2=None, op0=ALU.max), inc="v")
                        vsettle()
                        ew("vector", lambda e, bq=bq: e.reciprocal(
                            out=stat[:, 20 + bq:21 + bq], in_=stat[:, 16 + bq:17 + bq]),
                            inc="v")
                        vsettle()
                        ew("vector", lambda e, bq=bq: e.tensor_scalar(
                            out=stat[:, 12 + bq:13 + bq], in0=stat[:, 20 + bq:21 + bq],
                            scalar1=127.0, scalar2=None, op0=ALU.mult), inc="v")
                        vsettle()
                        ew("vector", lambda e, bq=bq, i=i: e.scalar_tensor_tensor(
                            out=invb2res[:, i * 4 + bq:i * 4 + bq + 1],
                            in0=stat[:, 16 + bq:17 + bq], scalar=1.0 / 127,
                            in1=wsc[:, 3:4], op0=ALU.mult, op1=ALU.mult), inc="v")
                        vsettle()
                    v_s2 = P.val("v")
                    for bq in range(4):
                        ew("tensor", lambda e, bq=bq: e.transpose(
                            out=ps5[0:1, bq * 128:(bq + 1) * 128],
                            in_=stat[:, 12 + bq:13 + bq], identity=idf32[:]),
                            waits=(("v", v_s2), ("v", ps5c_free))
                            if bq == 0 else (), inc="t")
                    t_s2 = P.val("t")
                    stat_c_free = t_s2
                    ew("vector", lambda e: e.tensor_copy(
                        out=s2row[:], in_=ps5[0:1, 0:512]), waits=(("t", t_s2),), inc="v")
                    ps5c_free = P.val("v")
                    v_r = P.val("v")
                    ew("tensor", lambda e: e.matmul(
                        out=ps6[:], lhsT=ones1[:], rhs=s2row[:], start=True,
                        stop=True, **MM),
                        waits=(("v", v_r), ("v", ps6c_free)), inc="t")
                    t_b = P.val("t")
                    ew("vector", lambda e: e.tensor_copy(out=s2b[:], in_=ps6[:]),
                       waits=(("t", t_b),), inc="v")
                    ps6c_free = P.val("v")
                    v_s2b = P.val("v")
                    cast_s = {}
                    for fc in range(32):
                        ch = hbuf[par][:, fc, :]
                        w0 = [("v", v_s2b)] if fc == 0 else []
                        if fc >= 2:
                            w0.append(("s", cast_s[fc - 2]))
                        ew("vector", lambda e, ch=ch, fc=fc: e.tensor_tensor(
                            out=qi32[fc % 2][:], in0=ch[:, 1:513], in1=s2b[:],
                            op=ALU.mult), waits=tuple(w0), inc="v")
                        v_tt = P.val("v")
                        ew("scalar", lambda e, ch=ch, fc=fc: e.activation(
                            out=ch[:, 1:513], in_=qi32[fc % 2][:],
                            func=AF.Identity), waits=(("v", v_tt),), inc="s")
                        cast_s[fc] = P.val("s")
                        dma("gpsimd", "dg",
                            hq_d[fc, :, i * 512:(i + 1) * 512],
                            hbuf[par][:, fc, 1:513],
                            waits=(("s", cast_s[fc]),))
                        yield True
                    hbuf_free[par] = qdone("dg")

                for _ in gen_mm1(0):
                    pass
                emit_copyleft(0)
                for i in range(NT):
                    gmm = gen_mm1(i + 1) if i + 1 < NT else None
                    gcv = gen_convquant(i)
                    while True:
                        if gmm is not None and next(gmm, None) is None:
                            gmm = None
                        if next(gcv, None) is None:
                            break
                    while gmm is not None and next(gmm, None) is None:
                        gmm = None

                P.fence()
                P.flush(block, sems)
                if DEBUG_PHASES < 4:
                    return nc

            with ExitStack() as esD:
                hqb = [sb(esD, f"hqb{i}", [128, 512], BF16) for i in range(2)]
                t2res = sb(esD, "t2res", [128, 32, 1024], BF16)
                x2l = [sb(esD, f"x2l{i}", [128, 1024], F32) for i in range(2)]
                y2b = [sb(esD, f"y2b{i}", [128, 1024], F32) for i in range(2)]
                v2b = sb(esD, "v2b", [128, 1024], F32)
                ob = [sb(esD, f"ob{i}", [128, 1024], F32) for i in range(2)]

                accs = [(ps0[:, 0:512], ps0[:, 512:1024]),
                        (ps1[:], ps2[:]),
                        (ps3[:], ps4[:]),
                        (ps5[:], ps6[:])]
                hqb_free = {0: 0, 1: 0}
                x2l_free = {0: 0, 1: 0}
                ob_free = {0: ("v", 0), 1: ("v", 0)}
                acc_free_v = [0, 0, 0, 0]
                y2b_free_s = {0: 0, 1: 0}

                for i in range(NT):
                    for fc in range(32):
                        fpar = fc % 2
                        dh = dma("sync", "dx", hqb[fpar][:],
                                 hq_d[fc, :, i * 512:(i + 1) * 512],
                                 waits=(("t", hqb_free[fpar]),))
                        dt2 = dma("scalar", "dw", t2res[:, fc, :],
                                  tern2_d[fc * 128:(fc + 1) * 128, :]) \
                            if i == 0 else None
                        for tsub in range(4):
                            w0 = []
                            if tsub == 0:
                                w0.append(dh)
                                if dt2 is not None:
                                    w0.append(dt2)
                            if fc == 0:
                                w0.append(("v", acc_free_v[tsub]))
                            a0, a1 = accs[tsub]
                            ew("tensor", lambda e, fpar=fpar, tsub=tsub, a0=a0,
                                fc=fc: e.matmul(
                                    out=a0,
                                    lhsT=hqb[fpar][:, tsub * 128:(tsub + 1) * 128],
                                    rhs=t2res[:, fc, 0:512],
                                    start=(fc == 0), stop=(fc == 31), **MM),
                                waits=tuple(w0), inc="t")
                            ew("tensor", lambda e, fpar=fpar, tsub=tsub, a1=a1,
                                fc=fc: e.matmul(
                                    out=a1,
                                    lhsT=hqb[fpar][:, tsub * 128:(tsub + 1) * 128],
                                    rhs=t2res[:, fc, 512:1024],
                                    start=(fc == 0), stop=(fc == 31), **MM), inc="t")
                        hqb_free[fpar] = P.val("t")
                    t_last = P.val("t")
                    for tsub in range(4):
                        oc = i * 4 + tsub
                        par = oc % 2
                        a0, a1 = accs[tsub]
                        dx2 = dma("scalar", "dw", x2l[par][:],
                                  x2_d[oc * 128:(oc + 1) * 128, :],
                                  waits=(("v", x2l_free[par]),))
                        ew("vector", lambda e, a0=a0, i=i, tsub=tsub, par=par:
                            e.tensor_scalar(
                                out=y2b[par][:, 0:512], in0=a0,
                                scalar1=invb2res[:, i * 4 + tsub:i * 4 + tsub + 1],
                                scalar2=None, op0=ALU.mult),
                            waits=(("t", t_last), ("s", y2b_free_s[par])), inc="v")
                        ew("vector", lambda e, a1=a1, i=i, tsub=tsub, par=par:
                            e.tensor_scalar(
                                out=y2b[par][:, 512:1024], in0=a1,
                                scalar1=invb2res[:, i * 4 + tsub:i * 4 + tsub + 1],
                                scalar2=None, op0=ALU.mult), inc="v")
                        acc_free_v[tsub] = P.val("v")
                        v_y2 = P.val("v")
                        v_st = ln_stats(y2b[par][:], slice(0, 128), oc % 2,
                                        (("v", v_y2),))
                        ew("scalar", lambda e, oc=oc, par=par: e.activation(
                            out=v2b[:], in_=y2b[par][:], func=AF.Identity,
                            bias=stat[:, 16 * (oc % 2) + 3:16 * (oc % 2) + 4],
                            scale=stat[:, 16 * (oc % 2) + 2:16 * (oc % 2) + 3]),
                            waits=(("v", v_st),), inc="s")
                        s_v2 = P.val("s")
                        y2b_free_s[par] = s_v2
                        ew("vector", lambda e, par=par: e.tensor_add(
                            out=ob[par][:], in0=v2b[:], in1=x2l[par][:]),
                            waits=(("s", s_v2), dx2, ob_free[par]),
                            inc="v")
                        x2l_free[par] = P.val("v")
                        v_o = P.val("v")
                        ob_free[par] = dma("gpsimd", "dg",
                                           out_d[oc * 128:(oc + 1) * 128, :],
                                           ob[par][:], waits=(("v", v_o),))

                P.emit("gpsimd", None, waits=qdone("dg"))
                P.flush(block, sems)
    return nc


_cache = {}


def kernel(**inputs):
    x = np.ascontiguousarray(np.asarray(inputs["L0_attn_in_sequence_bwc"], np.float32))
    proj = np.asarray(inputs["L0_attn_projection_1nck"], np.float32)[0]
    halves = np.asarray(inputs["L0_attn_halves"], np.float32)
    diag = np.asarray(inputs["L0_attn_diagonals_nk"], np.float32)
    mw = np.asarray(inputs["L0_attn_mixer_w"], np.float32)
    mb = np.asarray(inputs["L0_attn_mixer_b"], np.float32)
    w1 = np.asarray(inputs["L0_ffn_w1"], np.float32)
    cw = np.asarray(inputs["L0_ffn_conv_w"], np.float32)
    cb = np.asarray(inputs["L0_ffn_conv_b"], np.float32)
    w2 = np.asarray(inputs["L0_ffn_w2"], np.float32)

    p2d = np.ascontiguousarray(proj.transpose(1, 0, 2).reshape(CDIM, NHEAD * KH))
    iu, ju = np.triu_indices(KH, k=1)
    M = np.zeros((NHEAD, KH, KH), np.float32)
    M[:, iu, ju] = halves
    M[:, ju, iu] = halves
    M[:, np.arange(KH), np.arange(KH)] = diag
    mblk0 = M.transpose(1, 0, 2).reshape(KH, NHEAD * KH)
    mblk = np.ascontiguousarray(np.concatenate([mblk0, mblk0], axis=0))
    mwt = np.ascontiguousarray(mw.T)
    mbias = np.ascontiguousarray(np.broadcast_to(mb[None, :], (128, CDIM)))
    w1t = np.ascontiguousarray(w1.T)
    w2t = np.ascontiguousarray(w2.T)
    convp = np.zeros((128, 128), np.float32)
    for tap in range(3):
        convp[:, tap * 32:(tap + 1) * 32] = cw[:, 0, tap].reshape(32, 128).T
    convp[:, 96:128] = cb.reshape(32, 128).T
    idbf = np.eye(128, dtype=ml_dtypes.bfloat16)
    idf32 = np.eye(128, dtype=np.float32)
    ones1 = np.ones((1, 128), np.float32)

    key = (DEBUG_PHASES, DEBUG_A_TILES, DEBUG_WPREP, DEBUG_A_STEPS, DEBUG_TAPS)
    if key not in _cache:
        _cache[key] = build_program()
    nc = _cache[key]

    in_maps = []
    for c in range(8):
        b, s = c // 4, c % 4
        xb = np.roll(x[b], -s * TOK, axis=0) if s else x[b]
        t0 = s * TOK
        xhalo = np.zeros((2, CDIM), np.float32)
        hm = np.zeros((2, 1), np.float32)
        if t0 - 1 >= 0:
            xhalo[0] = x[b, t0 - 1]
            hm[0, 0] = 1.0
        if t0 + TOK < SEQ:
            xhalo[1] = x[b, t0 + TOK]
            hm[1, 0] = 1.0
        in_maps.append({
            "xfull": np.ascontiguousarray(xb), "xhalo": xhalo, "hmask": hm,
            "p2d": p2d, "mblk": mblk, "mwt": mwt, "mbias": mbias,
            "w1t": w1t, "w2t": w2t, "convp": convp,
            "idbf": idbf, "idf32": idf32, "ones1": ones1,
        })

    res = run_bass_kernel_spmd(nc, in_maps, core_ids=list(range(8)))
    if DEBUG_TAPS:
        global _taps
        _taps = res.results
    out = np.empty((BATCH, SEQ, CDIM), np.float32)
    for c in range(8):
        b, s = c // 4, c % 4
        out[b, s * TOK:(s + 1) * TOK] = res.results[c]["out"]
    return out


# version 21
# speedup vs baseline: 1.0657x; 1.0413x over previous
import numpy as np
import ml_dtypes
from contextlib import ExitStack

import concourse.bass as bass
import concourse.mybir as mybir
from concourse.bass_utils import run_bass_kernel_spmd

F32 = mybir.dt.float32
BF16 = mybir.dt.bfloat16
I32 = mybir.dt.int32
AF = mybir.ActivationFunctionType
ALU = mybir.AluOpType
AX = mybir.AxisListType

BATCH, SEQ, CDIM = 2, 8192, 1024
NHEAD, KH = 16, 64
FDIM = 4096
TOK = 2048
TILE = 512
NT_A = SEQ // TILE
NT = TOK // TILE
EPS = 1e-5

ENGS = ("sync", "scalar", "vector", "tensor", "gpsimd")
DEBUG_PHASES = 4
DEBUG_A_TILES = NT_A
DEBUG_WPREP = True
DEBUG_A_STEPS = 9
DEBUG_TAPS = False


class Prog:
    def __init__(self):
        self.items = {e: [] for e in ENGS}
        self.cnt = {}

    def add_sem(self, name):
        self.cnt[name] = 0

    def val(self, name):
        return self.cnt[name]

    def emit(self, eng, fn=None, waits=(), inc=None, inc_by=1):
        ws = tuple((s, v) for s, v in waits if v > 0)
        for s, v in ws:
            assert v <= self.cnt[s], f"wait on future {s}:{v}>{self.cnt[s]}"
        if inc is not None:
            assert fn is not None
            self.cnt[inc] += inc_by
        self.items[eng].append((fn, ws, inc, inc_by))
        return self.cnt[inc] if inc is not None else None

    def fence(self):
        snap = [(s, v) for s, v in self.cnt.items() if v > 0]
        for e in ENGS:
            self.emit(e, None, waits=snap)

    def flush(self, block, sems):
        def runner(items):
            def go(eng):
                for fn, waits, inc, inc_by in items:
                    for snm, v in waits:
                        eng.wait_ge(sems[snm], v)
                    if fn is not None:
                        inst = fn(eng)
                        if inc is not None:
                            inst.then_inc(sems[inc], inc_by)
            return go
        for e in ENGS:
            if self.items[e]:
                getattr(block, e)(runner(self.items[e]))
            self.items[e] = []


def build_program():
    nc = bass.Bass("TRN2", target_bir_lowering=False, detect_race_conditions=False)

    xfull = nc.dram_tensor("xfull", [SEQ, CDIM], F32, kind="ExternalInput").ap()
    xhalo = nc.dram_tensor("xhalo", [2, CDIM], F32, kind="ExternalInput").ap()
    hmask = nc.dram_tensor("hmask", [2, 1], F32, kind="ExternalInput").ap()
    p2d_d = nc.dram_tensor("p2d", [CDIM, CDIM], F32, kind="ExternalInput").ap()
    mblk_d = nc.dram_tensor("mblk", [128, NHEAD * KH], F32, kind="ExternalInput").ap()
    mwt_d = nc.dram_tensor("mwt", [CDIM, CDIM], F32, kind="ExternalInput").ap()
    mbias_d = nc.dram_tensor("mbias", [128, CDIM], F32, kind="ExternalInput").ap()
    w1t_d = nc.dram_tensor("w1t", [CDIM, FDIM], F32, kind="ExternalInput").ap()
    w2t_d = nc.dram_tensor("w2t", [FDIM, CDIM], F32, kind="ExternalInput").ap()
    convp_d = nc.dram_tensor("convp", [128, 128], F32, kind="ExternalInput").ap()
    idbf_d = nc.dram_tensor("idbf", [128, 128], BF16, kind="ExternalInput").ap()
    idf32_d = nc.dram_tensor("idf32", [128, 128], F32, kind="ExternalInput").ap()
    ones1_d = nc.dram_tensor("ones1", [1, 128], F32, kind="ExternalInput").ap()
    out_d = nc.dram_tensor("out", [TOK, CDIM], F32, kind="ExternalOutput").ap()

    KIND = "ExternalOutput" if DEBUG_TAPS else "Internal"
    tern1_d = nc.dram_tensor("tern1_s", [CDIM, FDIM], BF16, kind=KIND).ap()
    tern2_d = nc.dram_tensor("tern2_s", [FDIM, CDIM], BF16, kind=KIND).ap()
    x2_d = nc.dram_tensor("x2_s", [TOK, CDIM], F32, kind=KIND).ap()
    xqs_d = nc.dram_tensor("xqs_s", [8, 128, TOK], BF16, kind=KIND).ap()
    hq_d = nc.dram_tensor("hq_s", [32, 128, TOK], BF16, kind=KIND).ap()
    if DEBUG_TAPS:
        dbg_g = nc.dram_tensor("dbg_g", [KH, 1024], BF16, kind="ExternalOutput").ap()
        dbg_a = nc.dram_tensor("dbg_a", [128, 1024], BF16, kind="ExternalOutput").ap()
        dbg_q = nc.dram_tensor("dbg_q", [128, 8, 512], BF16, kind="ExternalOutput").ap()
        dbg_y = nc.dram_tensor("dbg_y", [128, 1024], F32, kind="ExternalOutput").ap()
        dbg_st = nc.dram_tensor("dbg_st", [128, 32], F32, kind="ExternalOutput").ap()
        dbg_u = nc.dram_tensor("dbg_u", [128, 1024], BF16, kind="ExternalOutput").ap()
        dbg_uT = nc.dram_tensor("dbg_uT", [128, 1024], BF16, kind="ExternalOutput").ap()
        dbg_qt = nc.dram_tensor("dbg_qt", [128, 1024], BF16, kind="ExternalOutput").ap()
        dbg_xt = nc.dram_tensor("dbg_xt", [128, 1024], F32, kind="ExternalOutput").ap()
        dbg_sq = nc.dram_tensor("dbg_sq", [128, 1024], F32, kind="ExternalOutput").ap()
        dbg_qall = nc.dram_tensor("dbg_qall", [64, 128, 1024], BF16, kind="ExternalOutput").ap()

    P = Prog()
    DMA_SEMS = {q: [f"{q}{i}" for i in range(4)] for q in ("dx", "dw", "dg")}
    for q in DMA_SEMS:
        for snm in DMA_SEMS[q]:
            P.add_sem(snm)
    for snm in ("v", "s", "t"):
        P.add_sem(snm)
    _rr = {q: 0 for q in DMA_SEMS}

    MM = dict(skip_group_check=True)

    with ExitStack() as es:
        def sb(stack, name, shape, dt):
            return stack.enter_context(nc.sbuf_tensor(name, shape, dt))

        sems = {}
        for q in DMA_SEMS:
            for snm in DMA_SEMS[q]:
                sems[snm] = es.enter_context(nc.semaphore(snm))
        for snm in ("v", "s", "t"):
            sems[snm] = es.enter_context(nc.semaphore(snm))

        ps0 = es.enter_context(nc.psum_tensor("ps0", [128, 1024], F32))
        ps1 = es.enter_context(nc.psum_tensor("ps1", [128, 512], F32))
        ps2 = es.enter_context(nc.psum_tensor("ps2", [128, 512], F32))
        ps3 = es.enter_context(nc.psum_tensor("ps3", [128, 512], F32))
        ps4 = es.enter_context(nc.psum_tensor("ps4", [128, 512], F32))
        ps5 = es.enter_context(nc.psum_tensor("ps5", [128, 512], F32))
        ps6 = es.enter_context(nc.psum_tensor("ps6", [128, 512], F32))
        ps3b = ps3[:].bitcast(BF16)
        ps6b = ps6[:].bitcast(BF16)

        pb = sb(es, "pb", [128, 8, 1024], BF16)
        mwtb = sb(es, "mwtb", [128, 8, 1024], BF16)
        mblkb = sb(es, "mblkb", [128, NHEAD * KH], BF16)
        ablk = sb(es, "ablk", [128, 8 * 128], BF16)
        mbias = sb(es, "mbias_t", [128, 1024], F32)
        convp = sb(es, "convp_t", [128, 128], F32)
        idbf = sb(es, "idbf_t", [128, 128], BF16)
        idf32 = sb(es, "idf32_t", [128, 128], F32)
        ones1 = sb(es, "ones1_t", [1, 128], F32)
        gsb = sb(es, "gsb", [128, 8 * 128], BF16)
        cpool = sb(es, "cpool", [128, 4], F32)
        wsc = sb(es, "wsc", [128, 4], F32)
        invb2res = sb(es, "invb2res", [128, 16], F32)
        xqsTh = sb(es, "xqsTh", [128, 8, 2], BF16)
        hedge = sb(es, "hedge", [128, 64], BF16)
        stat = sb(es, "stat", [128, 32], F32)
        tiny = sb(es, "tiny", [128, 16], F32)
        sqb = sb(es, "sqb", [128, 1024], F32)
        xh = sb(es, "xh_t", [2, CDIM], F32)
        zcol = sb(es, "zcol", [128, KH], BF16)
        hmk = sb(es, "hmk_t", [2, 1], F32)

        def dma(qeng, queue, out, in_, waits=()):
            qsem = DMA_SEMS[queue][_rr[queue] % len(DMA_SEMS[queue])]
            _rr[queue] += 1
            def f(e, out=out, in_=in_):
                return e.dma_start(out=out, in_=in_)
            waits = tuple(waits) + ((qsem, P.val(qsem)),)
            v = P.emit(qeng, f, waits=waits, inc=qsem, inc_by=16)
            return (qsem, v)

        def qdone(queue):
            return tuple((snm, P.val(snm)) for snm in DMA_SEMS[queue])

        def ew(eng, fn, waits=(), inc=None):
            return P.emit(eng, fn, waits=waits, inc=inc, inc_by=1)

        def vsettle():
            P.emit("vector", None, waits=(("v", P.val("v")),))

        ms_v = {0: 0, 1: 0}
        sqb_free = [0]
        sqrt_s = {0: 0, 1: 0}

        def ln_stats(xap, pslice, par, wx):
            o = 16 * par
            ew("vector", lambda e: e.tensor_reduce(
                out=stat[pslice, o:o + 1], in_=xap, axis=AX.X, op=ALU.add),
                waits=wx, inc="v")
            vsettle()
            ew("scalar", lambda e: e.activation(
                out=sqb[pslice, :], in_=xap, func=AF.Square),
                waits=wx + (("v", sqb_free[0]),), inc="s")
            sq_s = P.val("s")
            ew("vector", lambda e: e.tensor_reduce(
                out=stat[pslice, o + 1:o + 2], in_=sqb[pslice, :], axis=AX.X,
                op=ALU.add), waits=(("v", ms_v[par]), ("s", sq_s)), inc="v")
            vsettle()
            sqb_free[0] = P.val("v")
            ew("vector", lambda e: e.tensor_scalar(
                out=stat[pslice, o + 4:o + 5], in0=stat[pslice, o:o + 1],
                scalar1=1.0 / 1024, scalar2=None, op0=ALU.mult), inc="v")
            vsettle()
            ew("vector", lambda e: e.tensor_scalar(
                out=stat[pslice, o + 5:o + 6], in0=stat[pslice, o + 1:o + 2],
                scalar1=1.0 / 1024, scalar2=None, op0=ALU.mult), inc="v")
            vsettle()
            ms_v[par] = P.val("v")
            ew("vector", lambda e: e.scalar_tensor_tensor(
                out=stat[pslice, o + 6:o + 7], in0=stat[pslice, o + 4:o + 5],
                scalar=stat[pslice, o + 4:o + 5], in1=stat[pslice, o + 5:o + 6],
                op0=ALU.mult, op1=ALU.subtract),
                waits=(("s", sqrt_s[par]),), inc="v")
            vsettle()
            vV = P.val("v")
            ew("scalar", lambda e: e.activation(
                out=stat[pslice, o + 7:o + 8], in_=stat[pslice, o + 6:o + 7],
                func=AF.Sqrt, bias=cpool[pslice, 0:1], scale=-1.0),
                waits=(("v", vV),), inc="s")
            sqrt_s[par] = P.val("s")
            vS2 = P.val("s")
            ew("vector", lambda e: e.reciprocal(
                out=stat[pslice, o + 2:o + 3], in_=stat[pslice, o + 7:o + 8]),
                waits=(("s", vS2),), inc="v")
            vsettle()
            ew("vector", lambda e: e.scalar_tensor_tensor(
                out=stat[pslice, o + 3:o + 4], in0=stat[pslice, o + 4:o + 5],
                scalar=-1.0, in1=stat[pslice, o + 2:o + 3],
                op0=ALU.mult, op1=ALU.mult), inc="v")
            vsettle()
            return P.val("v")

        def quant1(x2ap, pslice, par, wready, xqi, xqsb, xqs_gate):
            o = 16 * par
            ln_stats(x2ap, pslice, par, wready)
            ew("vector", lambda e: e.tensor_reduce(
                out=stat[pslice, o + 8:o + 9], in_=x2ap, axis=AX.X, op=ALU.max), inc="v")
            vsettle()
            ew("vector", lambda e: e.tensor_reduce(
                out=stat[pslice, o + 9:o + 10], in_=x2ap, axis=AX.X, op=ALU.min), inc="v")
            vsettle()
            ew("vector", lambda e: e.tensor_sub(
                out=stat[pslice, o + 10:o + 11], in0=stat[pslice, o + 8:o + 9],
                in1=stat[pslice, o + 4:o + 5]), inc="v")
            vsettle()
            ew("vector", lambda e: e.tensor_sub(
                out=stat[pslice, o + 11:o + 12], in0=stat[pslice, o + 4:o + 5],
                in1=stat[pslice, o + 9:o + 10]), inc="v")
            vsettle()
            ew("vector", lambda e: e.tensor_max(
                out=stat[pslice, o + 12:o + 13], in0=stat[pslice, o + 10:o + 11],
                in1=stat[pslice, o + 11:o + 12]), inc="v")
            vsettle()
            ew("vector", lambda e: e.tensor_tensor(
                out=stat[pslice, o + 13:o + 14], in0=stat[pslice, o + 12:o + 13],
                in1=stat[pslice, o + 2:o + 3], op=ALU.mult), inc="v")
            vsettle()
            ew("vector", lambda e: e.tensor_scalar(
                out=stat[pslice, o + 14:o + 15], in0=stat[pslice, o + 13:o + 14],
                scalar1=EPS, scalar2=None, op0=ALU.max), inc="v")
            vsettle()
            ew("vector", lambda e: e.reciprocal(
                out=stat[pslice, o + 15:o + 16], in_=stat[pslice, o + 14:o + 15]), inc="v")
            vsettle()
            ew("vector", lambda e: e.tensor_scalar(
                out=stat[pslice, o + 8:o + 9], in0=stat[pslice, o + 15:o + 16],
                scalar1=127.0, scalar2=None, op0=ALU.mult), inc="v")
            vsettle()
            ew("vector", lambda e: e.tensor_tensor(
                out=stat[pslice, o + 9:o + 10], in0=stat[pslice, o + 2:o + 3],
                in1=stat[pslice, o + 8:o + 9], op=ALU.mult), inc="v")
            vsettle()
            ew("vector", lambda e: e.scalar_tensor_tensor(
                out=stat[pslice, o + 10:o + 11], in0=stat[pslice, o + 4:o + 5],
                scalar=-1.0, in1=stat[pslice, o + 9:o + 10],
                op0=ALU.mult, op1=ALU.mult), inc="v")
            vsettle()
            ew("vector", lambda e: e.scalar_tensor_tensor(
                out=stat[pslice, o + 11:o + 12], in0=stat[pslice, o + 14:o + 15],
                scalar=1.0 / 127, in1=wsc[pslice, 1:2],
                op0=ALU.mult, op1=ALU.mult), inc="v")
            vsettle()
            ew("vector", lambda e: e.tensor_scalar(
                out=xqi[pslice, :], in0=x2ap, scalar1=stat[pslice, o + 9:o + 10],
                scalar2=stat[pslice, o + 10:o + 11], op0=ALU.mult, op1=ALU.add), inc="v")
            vsettle()
            ew("vector", lambda e: e.tensor_scalar(
                out=xqsb[pslice, :], in0=xqi[pslice, :],
                scalar1=stat[pslice, o + 11:o + 12], scalar2=None, op0=ALU.mult),
                waits=xqs_gate, inc="v")
            vsettle()
            return P.val("v")

        with nc.Block() as block:
            with ExitStack() as esQ:
                qTres = sb(esQ, "qTres", [128, 8, TOK], BF16)
                esA = esQ.enter_context(ExitStack())
                pstage = sb(esA, "pstage", [128, 4 * 1024], F32)
                xt = [sb(esA, f"xt{i}", [128, 1024], F32) for i in range(4)]
                sqb2 = [sb(esA, f"sqb2_{i}", [128, 1024], F32) for i in range(2)]
                statA = sb(esA, "statA", [128, 96], F32)
                ub = [sb(esA, f"ub{i}", [128, 1024], BF16) for i in range(2)]
                uT = [sb(esA, f"uT{i}", [128, 1024], BF16) for i in range(2)]
                qtok = [sb(esA, f"qtok{i}", [128, 1024], BF16) for i in range(2)]
                wstage = [sb(esA, f"wstage{i}", [128, 4096], F32) for i in range(2)]
                wi32 = sb(esA, "wi32", [128, 4096], I32)
                wtb = [sb(esA, f"wtb{i}", [128, 4096], BF16) for i in range(2)]
                wabs = sb(esA, "wabs", [128, 48], F32)

                ew("vector", lambda e: e.memset(cpool[:, 0:1], EPS), inc="v")
                ew("vector", lambda e: e.memset(cpool[:, 1:2], 0.0), inc="v")
                ew("vector", lambda e: e.memset(cpool[:, 2:3], 1.0), inc="v")
                ew("vector", lambda e: e.memset(zcol[:], 0.0), inc="v")
                ew("vector", lambda e: e.memset(ablk[:], 0.0), inc="v")
                ew("vector", lambda e: e.memset(hedge[:], 0.0), inc="v")
                v_consts = P.val("v")

                dma("sync", "dx", mbias[:], mbias_d)
                dma("sync", "dx", convp[:], convp_d)
                dma("sync", "dx", idbf[:], idbf_d)
                dma("sync", "dx", idf32[:], idf32_d)
                dma("sync", "dx", ones1[:], ones1_d)
                dma("sync", "dx", xh[:], xhalo)
                dma("sync", "dx", hmk[:], hmask)
                dma("sync", "dx", pstage[:, 0:NHEAD * KH], mblk_d)
                dx_small = qdone("dx")
                ew("vector", lambda e: e.tensor_copy(
                    out=mblkb[:], in_=pstage[:, 0:NHEAD * KH]),
                    waits=dx_small, inc="v")
                v_gate = P.val("v")
                v_mblk = v_gate
                for dst_b, src_d in ((pb, p2d_d), (mwtb, mwt_d)):
                    for h in range(2):
                        for cc in range(4):
                            dma("sync", "dx",
                                pstage[:, cc * 1024:(cc + 1) * 1024],
                                src_d[(h * 4 + cc) * 128:(h * 4 + cc + 1) * 128, :],
                                waits=(("v", v_gate),) if cc == 0 else ())
                        dxp = qdone("dx")
                        for cc in range(4):
                            ew("vector", lambda e, dst_b=dst_b, cc=cc, h=h: e.tensor_copy(
                                out=dst_b[:, h * 4 + cc, :],
                                in_=pstage[:, cc * 1024:(cc + 1) * 1024]),
                                waits=dxp if cc == 0 else (), inc="v")
                        v_gate = P.val("v")
                v_pb = v_gate
                v_mwtb = v_gate

                wblocks = [("w1", bi) for bi in range(8)] + [("w2", bi) for bi in range(32)]
                wstate = {}

                def w_src(kind, bi):
                    if kind == "w1":
                        return w1t_d[bi * 128:(bi + 1) * 128, :], 4096
                    return w2t_d[bi * 128:(bi + 1) * 128, :], 1024

                def emit_wpass1(idx):
                    kind, bi = wblocks[idx]
                    src, ncols = w_src(kind, bi)
                    dst = wstage[idx % 2][:, 0:ncols]
                    wfree = wstate.get(("p1v", idx - 2), 0)
                    d = dma("scalar", "dw", dst, src, waits=(("v", wfree),))
                    col = bi if kind == "w1" else 8 + bi
                    ew("vector", lambda e, dst=dst, col=col: e.tensor_reduce(
                        out=wabs[:, col:col + 1], in_=dst, axis=AX.X, op=ALU.add,
                        apply_absolute_value=True), waits=(d,), inc="v")
                    wstate[("p1v", idx)] = P.val("v")

                def emit_wscales():
                    vA = wstate[("p1v", len(wblocks) - 1)]
                    vsettle()
                    ew("vector", lambda e: e.tensor_reduce(
                        out=tiny[:, 0:1], in_=wabs[:, 0:8], axis=AX.X, op=ALU.add),
                        waits=(("v", vA),), inc="v")
                    ew("vector", lambda e: e.tensor_reduce(
                        out=tiny[:, 1:2], in_=wabs[:, 8:40], axis=AX.X, op=ALU.add),
                        inc="v")
                    vsettle()
                    vv = P.val("v")
                    ew("tensor", lambda e: e.matmul(
                        out=ps6[0:1, 0:2], lhsT=cpool[:, 2:3], rhs=tiny[:, 0:2],
                        start=True, stop=True, **MM), waits=(("v", vv),), inc="t")
                    tt = P.val("t")
                    ew("vector", lambda e: e.tensor_scalar(
                        out=tiny[0:1, 2:3], in0=ps6[0:1, 0:1],
                        scalar1=1.0 / (CDIM * FDIM), scalar2=1e-5,
                        op0=ALU.mult, op1=ALU.max), waits=(("t", tt),), inc="v")
                    ew("vector", lambda e: e.tensor_scalar(
                        out=tiny[0:1, 3:4], in0=ps6[0:1, 1:2],
                        scalar1=1.0 / (CDIM * FDIM), scalar2=1e-5,
                        op0=ALU.mult, op1=ALU.max), inc="v")
                    ew("vector", lambda e: e.reciprocal(
                        out=tiny[0:1, 4:5], in_=tiny[0:1, 2:3]), inc="v")
                    ew("vector", lambda e: e.reciprocal(
                        out=tiny[0:1, 5:6], in_=tiny[0:1, 3:4]), inc="v")
                    vsettle()
                    vv2 = P.val("v")
                    ew("tensor", lambda e: e.matmul(
                        out=ps6[:, 4:8], lhsT=ones1[:], rhs=tiny[0:1, 2:6],
                        start=True, stop=True, **MM), waits=(("v", vv2),), inc="t")
                    tt2 = P.val("t")
                    ew("vector", lambda e: e.tensor_copy(out=wsc[:, 1:2], in_=ps6[:, 4:5]),
                       waits=(("t", tt2),), inc="v")
                    ew("vector", lambda e: e.tensor_copy(out=wsc[:, 3:4], in_=ps6[:, 5:6]),
                       inc="v")
                    ew("vector", lambda e: e.tensor_copy(out=wsc[:, 0:1], in_=ps6[:, 6:7]),
                       inc="v")
                    ew("vector", lambda e: e.tensor_copy(out=wsc[:, 2:3], in_=ps6[:, 7:8]),
                       inc="v")
                    wstate["scales_v"] = P.val("v")

                def emit_wpass2(idx):
                    kind, bi = wblocks[idx]
                    src, ncols = w_src(kind, bi)
                    dst = wstage[idx % 2][:, 0:ncols]
                    wfree = wstate.get(("p2v", idx - 2), 0)
                    d = dma("scalar", "dw", dst, src,
                            waits=(("v", wstate["scales_v"]), ("v", wfree)))
                    scol = wsc[:, 0:1] if kind == "w1" else wsc[:, 2:3]
                    ew("vector", lambda e, dst=dst, scol=scol, ncols=ncols: e.tensor_scalar(
                        out=wi32[:, 0:ncols], in0=dst, scalar1=scol, scalar2=None,
                        op0=ALU.mult), waits=(d,), inc="v")
                    ew("vector", lambda e, ncols=ncols, idx=idx: e.tensor_scalar(
                        out=wtb[idx % 2][:, 0:ncols], in0=wi32[:, 0:ncols], scalar1=1.0,
                        scalar2=-1.0, op0=ALU.min, op1=ALU.max),
                        waits=(wstate.get(("p2g", idx - 2), ("v", 0)),), inc="v")
                    wstate[("p2v", idx)] = P.val("v")
                    tdst = (tern1_d if kind == "w1" else tern2_d)[bi * 128:(bi + 1) * 128, :]
                    wstate[("p2g", idx)] = dma("gpsimd", "dg", tdst, wtb[idx % 2][:, 0:ncols],
                                               waits=(("v", P.val("v")),))

                wq1 = iter(range(len(wblocks)))
                wq2 = iter(range(len(wblocks)))
                scales_done = [False]

                psA_free = {0: ("s", 0), 1: ("s", 0)}
                psT_free = ("v", 0)
                xt4_free = {k: (0, 0) for k in range(4)}
                sqb2_free = {0: 0, 1: 0}
                statA_sfree = {0: 0, 1: 0}
                t_G_last = 0
                ub_free = {0: 0, 1: 0}
                uT_free = {0: 0, 1: 0}
                qtok_free = {0: 0, 1: 0}
                qtok_dump = {0: ("v", 0), 1: ("v", 0)}

                pendG = [None]
                ps6T_free = [0]

                def flush_G(gt, par, s_qk, owned):
                    nonlocal t_G_last
                    if gt == 0:
                        for pg in (ps1, ps2):
                            ew("tensor", lambda e, pg=pg, par=par: e.matmul(
                                out=pg[0:KH, :], lhsT=zcol[:, :],
                                rhs=qtok[par][:, 0:512],
                                start=True, stop=False, **MM),
                                waits=(("v", s_qk),) if pg is ps1 else (),
                                inc="t")
                    for n in range(16):
                        pg = ps1 if n < 8 else ps2
                        off = (n % 8) * 64
                        ew("tensor", lambda e, par=par, n=n, pg=pg, off=off, gt=gt:
                            e.matmul(
                                out=pg[0:KH, off:off + KH],
                                lhsT=qtok[par][:, n * KH:(n + 1) * KH],
                                rhs=qtok[par][:, n * KH:(n + 1) * KH],
                                start=False, stop=(gt == DEBUG_A_TILES * 4 - 1), **MM),
                            waits=(("v", s_qk),) if n == 0 else (), inc="t")
                    t_G_last = P.val("t")
                    if owned:
                        for j in range(8):
                            ew("tensor", lambda e, par=par, j=j: e.transpose(
                                out=ps6b[:, j * 128:(j + 1) * 128],
                                in_=qtok[par][:, j * 128:(j + 1) * 128],
                                identity=idbf[:]),
                                waits=(("v", ps6T_free[0]),) if j == 0 else (),
                                inc="t")
                        t_tp2 = P.val("t")
                        ew("vector", lambda e, gt=gt: e.tensor_copy(
                            out=qTres[:, :, gt * 128:(gt + 1) * 128],
                            in_=ps6b[:].rearrange("p (c t) -> p c t", c=8)),
                            waits=(("t", t_tp2),), inc="v")
                        ps6T_free[0] = P.val("v")
                    qtok_free[par] = P.val("t")

                def emit_pass1_unit(t, k):
                    so2 = 48 * (t % 2)
                    gt2 = t * 4 + k
                    d = dma("sync", "dx", xt[k][:],
                            xfull[gt2 * 128:(gt2 + 1) * 128, :],
                            waits=(("v", xt4_free[k][0]), ("s", xt4_free[k][1])))
                    ew("vector", lambda e, k=k, so2=so2: e.tensor_reduce(
                        out=statA[:, so2 + k:so2 + k + 1], in_=xt[k][:], axis=AX.X,
                        op=ALU.add), waits=(d, ("v", v_consts),
                                            ("s", statA_sfree[t % 2])), inc="v")
                    ew("scalar", lambda e, k=k: e.activation(
                        out=sqb2[k % 2][:], in_=xt[k][:], func=AF.Square),
                        waits=(d, ("v", sqb2_free[k % 2])), inc="s")
                    sq_s = P.val("s")
                    ew("vector", lambda e, k=k, so2=so2: e.tensor_reduce(
                        out=statA[:, so2 + 4 + k:so2 + 5 + k], in_=sqb2[k % 2][:],
                        axis=AX.X, op=ALU.add), waits=(("s", sq_s),), inc="v")
                    sqb2_free[k % 2] = P.val("v")

                def emit_chain(t):
                    so = 48 * (t % 2)
                    vsettle()
                    ew("vector", lambda e, so=so: e.tensor_scalar(
                        out=statA[:, so + 8:so + 12], in0=statA[:, so + 0:so + 4],
                        scalar1=1.0 / 1024, scalar2=None, op0=ALU.mult), inc="v")
                    vsettle()
                    ew("vector", lambda e, so=so: e.tensor_scalar(
                        out=statA[:, so + 12:so + 16], in0=statA[:, so + 4:so + 8],
                        scalar1=1.0 / 1024, scalar2=None, op0=ALU.mult), inc="v")
                    vsettle()
                    ew("vector", lambda e, so=so: e.tensor_tensor(
                        out=statA[:, so + 16:so + 20], in0=statA[:, so + 8:so + 12],
                        in1=statA[:, so + 8:so + 12], op=ALU.mult), inc="v")
                    vsettle()
                    ew("vector", lambda e, so=so: e.tensor_sub(
                        out=statA[:, so + 20:so + 24], in0=statA[:, so + 16:so + 20],
                        in1=statA[:, so + 12:so + 16]), inc="v")
                    vsettle()
                    vV = P.val("v")
                    ew("scalar", lambda e, so=so: e.activation(
                        out=statA[:, so + 24:so + 28], in_=statA[:, so + 20:so + 24],
                        func=AF.Sqrt, bias=cpool[:, 0:1], scale=-1.0),
                        waits=(("v", vV),), inc="s")
                    sA_sqrt = P.val("s")
                    ew("vector", lambda e, so=so: e.reciprocal(
                        out=statA[:, so + 28:so + 32], in_=statA[:, so + 24:so + 28]),
                        waits=(("s", sA_sqrt),), inc="v")
                    vsettle()
                    ew("vector", lambda e, so=so: e.scalar_tensor_tensor(
                        out=statA[:, so + 32:so + 36], in0=statA[:, so + 8:so + 12],
                        scalar=-1.0, in1=statA[:, so + 28:so + 32],
                        op0=ALU.mult, op1=ALU.mult), inc="v")
                    vsettle()
                    return P.val("v")

                for k in range(4):
                    emit_pass1_unit(0, k)
                v_stats = emit_chain(0)
                for t_i in range(DEBUG_A_TILES):
                    owned = t_i < NT
                    so = 48 * (t_i % 2)
                    for k in range(4):
                        gt = t_i * 4 + k
                        par = gt % 2
                        ew("scalar", lambda e, k=k, par=par, so=so: e.activation(
                            out=ub[par][:], in_=xt[k][:], func=AF.Identity,
                            bias=statA[:, so + 32 + k:so + 33 + k],
                            scale=statA[:, so + 28 + k:so + 29 + k]),
                            waits=(("v", v_stats), ("t", ub_free[par])), inc="s")
                        s_u = P.val("s")
                        xt4_free[k] = (v_stats, s_u)
                        if k == 3:
                            statA_sfree[t_i % 2] = s_u
                        for j in range(8):
                            ew("tensor", lambda e, par=par, j=j: e.transpose(
                                out=ps3b[:, j * 128:(j + 1) * 128],
                                in_=ub[par][:, j * 128:(j + 1) * 128],
                                identity=idbf[:]),
                                waits=(("s", s_u), psT_free) if j == 0 else (),
                                inc="t")
                        ub_free[par] = P.val("t")
                        t_tp = P.val("t")
                        ew("vector", lambda e, par=par: e.tensor_copy(
                            out=uT[par][:], in_=ps3b[:]),
                            waits=(("t", t_tp), ("t", uT_free[par])), inc="v")
                        psT_free = ("v", P.val("v"))
                        s_uT = ("v", P.val("v"))
                        if k >= 1 and t_i + 1 < DEBUG_A_TILES:
                            emit_pass1_unit(t_i + 1, k - 1)
                        if pendG[0] is not None:
                            flush_G(*pendG[0])
                            pendG[0] = None
                        qa0, qa1 = (ps0[:, 0:512], ps0[:, 512:1024]) if par == 0 \
                            else (ps4[:], ps5[:])
                        for cc in range(8):
                            w0 = (s_uT, psA_free[par], ("v", v_pb)) \
                                if cc == 0 else ()
                            ew("tensor", lambda e, par=par, cc=cc, qa0=qa0: e.matmul(
                                out=qa0,
                                lhsT=uT[par][:, cc * 128:(cc + 1) * 128],
                                rhs=pb[:, cc, 0:512],
                                start=(cc == 0), stop=(cc == 7), **MM),
                                waits=w0, inc="t")
                            ew("tensor", lambda e, par=par, cc=cc, qa1=qa1: e.matmul(
                                out=qa1,
                                lhsT=uT[par][:, cc * 128:(cc + 1) * 128],
                                rhs=pb[:, cc, 512:1024],
                                start=(cc == 0), stop=(cc == 7), **MM), inc="t")
                        uT_free[par] = P.val("t")
                        t_q = P.val("t")
                        ew("vector", lambda e, par=par, qa0=qa0, qa1=qa1:
                            (e.tensor_copy(out=qtok[par][:, 0:512], in_=qa0),
                             e.tensor_copy(out=qtok[par][:, 512:1024], in_=qa1))[-1],
                            waits=(("t", t_q), ("t", qtok_free[par]),
                                   qtok_dump[par]), inc="v")
                        psA_free[par] = ("v", P.val("v"))
                        s_qk = P.val("v")
                        if DEBUG_TAPS:
                            qdump = dma("gpsimd", "dg", dbg_qall[gt, :, :],
                                        qtok[par][:], waits=(("v", s_qk),))
                            qtok_dump[par] = qdump
                        if DEBUG_A_STEPS >= 5:
                            pendG[0] = (gt, par, s_qk, owned)
                    if t_i + 1 < DEBUG_A_TILES:
                        emit_pass1_unit(t_i + 1, 3)
                        v_stats = emit_chain(t_i + 1)
                    if not DEBUG_WPREP:
                        continue
                    if t_i < 7 or DEBUG_A_TILES < NT_A:
                        for _ in range(6):
                            idx = next(wq1, None)
                            if idx is not None:
                                emit_wpass1(idx)
                        if t_i == 6 and not scales_done[0]:
                            for idx in wq1:
                                emit_wpass1(idx)
                            emit_wscales()
                            scales_done[0] = True
                    else:
                        for _ in range(5):
                            idx = next(wq2, None)
                            if idx is not None:
                                emit_wpass2(idx)
                if pendG[0] is not None:
                    flush_G(*pendG[0])
                    pendG[0] = None
                if DEBUG_WPREP:
                    if not scales_done[0]:
                        for idx in wq1:
                            emit_wpass1(idx)
                        emit_wscales()
                        scales_done[0] = True
                    for idx in wq2:
                        emit_wpass2(idx)

                if DEBUG_A_STEPS < 6:
                    P.fence()
                    P.flush(block, sems)
                    return nc
                ew("vector", lambda e: e.tensor_copy(out=gsb[0:KH, 0:512], in_=ps1[0:KH, :]),
                   waits=(("t", t_G_last),), inc="v")
                ew("vector", lambda e: e.tensor_copy(out=gsb[0:KH, 512:1024], in_=ps2[0:KH, :]),
                   inc="v")
                v_gsb = P.val("v")
                for n in range(16):
                    pr, ei = n // 2, n % 2
                    ew("tensor", lambda e, n=n, pr=pr, ei=ei: e.matmul(
                        out=ps4[ei * KH:(ei + 1) * KH, pr * KH:(pr + 1) * KH],
                        lhsT=mblkb[0:KH, n * KH:(n + 1) * KH],
                        rhs=gsb[0:KH, n * KH:(n + 1) * KH],
                        start=True, stop=True,
                        tile_position=(0, ei * KH), **MM),
                        waits=(("v", v_gsb), ("v", v_mblk)) if n == 0 else (),
                        inc="t")
                t_a = P.val("t")
                for n in range(16):
                    pr, ei = n // 2, n % 2
                    ew("vector", lambda e, pr=pr, ei=ei: e.tensor_copy(
                        out=ablk[ei * KH:(ei + 1) * KH,
                                 pr * 128 + ei * KH:pr * 128 + (ei + 1) * KH],
                        in_=ps4[ei * KH:(ei + 1) * KH, pr * KH:(pr + 1) * KH]),
                        waits=(("t", t_a),) if n == 0 else (), inc="v")
                v_ablk = P.val("v")
                if DEBUG_TAPS:
                    dma("gpsimd", "dg", dbg_g, gsb[0:KH, :], waits=(("v", v_gsb),))
                    dma("gpsimd", "dg", dbg_a, ablk[:], waits=(("v", v_ablk),))
                    for cc in range(8):
                        dma("gpsimd", "dg", dbg_q[:, cc, :], qTres[:, cc, 0:512],
                            waits=(("v", P.val("v")),))

                P.fence()
                P.flush(block, sems)
                esA.close()
                if DEBUG_PHASES < 2:
                    return nc

                with ExitStack() as esB:
                    nudT = sb(esB, "nudT", [128, 8, 512], BF16)
                    yb = sb(esB, "yb", [128, 1024], F32)
                    vb = sb(esB, "vb", [128, 1024], F32)
                    xob = [sb(esB, f"xob{i}", [128, 1024], F32) for i in range(2)]
                    x2b = [sb(esB, f"x2b{i}", [128, 1024], F32) for i in range(2)]
                    xqi = sb(esB, "xqi", [128, 1024], I32)
                    xqsb = sb(esB, "xqsb", [128, 1024], BF16)
                    xqsT = sb(esB, "xqsT", [128, 8, 512], BF16)
                    uTh = sb(esB, "uTh", [128, 8, 2], BF16)
                    nudTh = sb(esB, "nudTh", [128, 8, 2], BF16)
                    qhb = sb(esB, "qhb", [2, 1024], BF16)

                    xob_free = {0: 0, 1: 0}
                    x2b_free = {0: (0, ("v", 0)), 1: (0, ("v", 0))}
                    psT_free2 = 0
                    ps4_free, ps5_free = 0, 0
                    ps0_free = 0
                    yb_free_s = 0
                    nudT_free_t = 0
                    xqsT_free = 0
                    xqsT_store_dg = ("v", 0)

                    for i in range(NT):
                        for p in range(8):
                            pg = ps4 if p % 2 == 0 else ps5
                            pfree = ps4_free if p % 2 == 0 else ps5_free
                            ew("tensor", lambda e, p=p, pg=pg, i=i: e.matmul(
                                out=pg[:], lhsT=ablk[:, p * 128:(p + 1) * 128],
                                rhs=qTres[:, p, i * 512:(i + 1) * 512],
                                start=True, stop=True, **MM),
                                waits=(("v", v_ablk), ("v", pfree)), inc="t")
                            tn = P.val("t")
                            ew("vector", lambda e, p=p, pg=pg: e.tensor_copy(
                                out=nudT[:, p, :], in_=pg[:]),
                                waits=(("t", tn), ("t", nudT_free_t)), inc="v")
                            if p % 2 == 0:
                                ps4_free = P.val("v")
                            else:
                                ps5_free = P.val("v")
                        v_nud = P.val("v")
                        for k in range(4):
                            oc = i * 4 + k
                            par = oc % 2
                            for jc in range(8):
                                w0 = (("v", v_nud), ("v", ps0_free), ("v", v_mwtb)) \
                                    if jc == 0 else ()
                                ew("tensor", lambda e, jc=jc, k=k: e.matmul(
                                    out=ps0[:, 0:512],
                                    lhsT=nudT[:, jc, k * 128:(k + 1) * 128],
                                    rhs=mwtb[:, jc, 0:512],
                                    start=(jc == 0), stop=(jc == 7), **MM),
                                    waits=w0, inc="t")
                                ew("tensor", lambda e, jc=jc, k=k: e.matmul(
                                    out=ps0[:, 512:1024],
                                    lhsT=nudT[:, jc, k * 128:(k + 1) * 128],
                                    rhs=mwtb[:, jc, 512:1024],
                                    start=(jc == 0), stop=(jc == 7), **MM), inc="t")
                            if k == 3:
                                nudT_free_t = P.val("t")
                            t_mix = P.val("t")
                            ew("vector", lambda e: e.tensor_add(
                                out=yb[:], in0=ps0[:], in1=mbias[:]),
                                waits=(("t", t_mix), ("s", yb_free_s)), inc="v")
                            ps0_free = P.val("v")
                            v_y = P.val("v")
                            if DEBUG_TAPS and i == 0 and k == 0:
                                dma("gpsimd", "dg", dbg_y, yb[:], waits=(("v", v_y),))
                            dxo = dma("sync", "dx", xob[par][:],
                                      xfull[oc * 128:(oc + 1) * 128, :],
                                      waits=(("v", xob_free[par]),))
                            v_st = ln_stats(yb[:], slice(0, 128), par, (("v", v_y),))
                            ew("scalar", lambda e, par=par: e.activation(
                                out=vb[:], in_=yb[:], func=AF.Identity,
                                bias=stat[:, 16 * par + 3:16 * par + 4],
                                scale=stat[:, 16 * par + 2:16 * par + 3]),
                                waits=(("v", v_st),), inc="s")
                            s_v = P.val("s")
                            yb_free_s = s_v
                            ew("vector", lambda e, par=par: e.tensor_add(
                                out=x2b[par][:], in0=vb[:], in1=xob[par][:]),
                                waits=(("s", s_v), dxo,
                                       ("v", x2b_free[par][0]),
                                       x2b_free[par][1]), inc="v")
                            xob_free[par] = P.val("v")
                            v_x2 = P.val("v")
                            dgx2 = dma("gpsimd", "dg",
                                       x2_d[oc * 128:(oc + 1) * 128, :],
                                       x2b[par][:], waits=(("v", v_x2),))
                            v_q = quant1(x2b[par][:], slice(0, 128), par,
                                         (("v", v_x2),), xqi, xqsb,
                                         (("t", xqsT_free),))
                            if DEBUG_TAPS and i == 0 and k == 0:
                                dma("gpsimd", "dg", dbg_st, stat[:], waits=(("v", v_q),))
                                dma("gpsimd", "dg", dbg_xt, x2b[par][:], waits=(("v", v_q),))
                                dma("gpsimd", "dg", dbg_u, xqsb[:], waits=(("v", v_q),))
                                P.fence()
                            x2b_free[par] = (v_q, dgx2)
                            for j in range(8):
                                ew("tensor", lambda e, j=j: e.transpose(
                                    out=ps3b[:, j * 128:(j + 1) * 128],
                                    in_=xqsb[:, j * 128:(j + 1) * 128],
                                    identity=idbf[:]),
                                    waits=(("v", v_q), ("v", psT_free2))
                                    if j == 0 else (), inc="t")
                            xqsT_free = P.val("t")
                            t_tp = P.val("t")
                            ew("vector", lambda e, k=k: e.tensor_copy(
                                out=xqsT[:, :, k * 128:(k + 1) * 128],
                                in_=ps3b[:].rearrange("p (c t) -> p c t", c=8)),
                                waits=(("t", t_tp), xqsT_store_dg), inc="v")
                            psT_free2 = P.val("v")
                        vw = P.val("v")
                        for cc in range(8):
                            xqsT_store_dg = dma(
                                "gpsimd", "dg",
                                xqs_d[cc, :, i * 512:(i + 1) * 512],
                                xqsT[:, cc, :],
                                waits=(("v", vw),) if cc == 0 else ())
                    dg_xqs = qdone("dg")

                    id2 = idbf[0:2, 0:2]
                    v_st = ln_stats(xh[:], slice(0, 2), 0, dx_small)
                    ew("scalar", lambda e: e.activation(
                        out=xqsb[0:2, :], in_=xh[:], func=AF.Identity,
                        bias=stat[0:2, 3:4], scale=stat[0:2, 2:3]),
                        waits=(("v", v_st),), inc="s")
                    s_uh = P.val("s")
                    for j in range(8):
                        ew("tensor", lambda e, j=j: e.matmul(
                            out=ps4[:, j * 2:(j + 1) * 2],
                            lhsT=xqsb[0:2, j * 128:(j + 1) * 128], rhs=id2,
                            start=True, stop=True, **MM),
                            waits=(("s", s_uh), ("v", ps4_free)) if j == 0 else (),
                            inc="t")
                    t_tp = P.val("t")
                    ew("vector", lambda e: e.tensor_copy(
                        out=uTh[:], in_=ps4[:, 0:16].rearrange("p (c t) -> p c t", c=8)),
                        waits=(("t", t_tp),), inc="v")
                    ps4_free = P.val("v")
                    v_uTh = P.val("v")
                    for cc in range(8):
                        ew("tensor", lambda e, cc=cc: e.matmul(
                            out=ps0[0:2, 0:512], lhsT=uTh[:, cc, :],
                            rhs=pb[:, cc, 0:512], start=(cc == 0), stop=(cc == 7), **MM),
                            waits=(("v", v_uTh), ("v", ps0_free)) if cc == 0 else (),
                            inc="t")
                        ew("tensor", lambda e, cc=cc: e.matmul(
                            out=ps0[0:2, 512:1024], lhsT=uTh[:, cc, :],
                            rhs=pb[:, cc, 512:1024], start=(cc == 0), stop=(cc == 7),
                            **MM), inc="t")
                    t_qh = P.val("t")
                    ew("vector", lambda e: e.tensor_copy(out=qhb[:], in_=ps0[0:2, :]),
                       waits=(("t", t_qh),), inc="v")
                    ps0_free = P.val("v")
                    v_qh = P.val("v")
                    for j in range(8):
                        ew("tensor", lambda e, j=j: e.matmul(
                            out=ps4[:, j * 2:(j + 1) * 2],
                            lhsT=qhb[0:2, j * 128:(j + 1) * 128], rhs=id2,
                            start=True, stop=True, **MM),
                            waits=(("v", v_qh), ("v", ps4_free)) if j == 0 else (),
                            inc="t")
                    t_tp = P.val("t")
                    ew("vector", lambda e: e.tensor_copy(
                        out=uTh[:], in_=ps4[:, 0:16].rearrange("p (c t) -> p c t", c=8)),
                        waits=(("t", t_tp),), inc="v")
                    ps4_free = P.val("v")
                    v_qTh = P.val("v")
                    for p in range(8):
                        ew("tensor", lambda e, p=p: e.matmul(
                            out=ps4[:, 0:2], lhsT=ablk[:, p * 128:(p + 1) * 128],
                            rhs=uTh[:, p, :], start=True, stop=True, **MM),
                            waits=(("v", v_qTh), ("v", ps4_free)) if p == 0 else
                            (("v", ps4_free),), inc="t")
                        tn = P.val("t")
                        ew("vector", lambda e, p=p: e.tensor_copy(
                            out=nudTh[:, p, :], in_=ps4[:, 0:2]),
                            waits=(("t", tn),), inc="v")
                        ps4_free = P.val("v")
                    v_nudh = P.val("v")
                    for jc in range(8):
                        ew("tensor", lambda e, jc=jc: e.matmul(
                            out=ps0[0:2, 0:512], lhsT=nudTh[:, jc, :],
                            rhs=mwtb[:, jc, 0:512], start=(jc == 0), stop=(jc == 7),
                            **MM),
                            waits=(("v", v_nudh), ("v", ps0_free)) if jc == 0 else (),
                            inc="t")
                        ew("tensor", lambda e, jc=jc: e.matmul(
                            out=ps0[0:2, 512:1024], lhsT=nudTh[:, jc, :],
                            rhs=mwtb[:, jc, 512:1024], start=(jc == 0), stop=(jc == 7),
                            **MM), inc="t")
                    t_mixh = P.val("t")
                    ew("vector", lambda e: e.tensor_add(
                        out=yb[0:2, :], in0=ps0[0:2, :], in1=mbias[0:2, :]),
                        waits=(("t", t_mixh), ("s", yb_free_s)), inc="v")
                    ps0_free = P.val("v")
                    v_yh = P.val("v")
                    v_st = ln_stats(yb[0:2, :], slice(0, 2), 0, (("v", v_yh),))
                    ew("scalar", lambda e: e.activation(
                        out=vb[0:2, :], in_=yb[0:2, :], func=AF.Identity,
                        bias=stat[0:2, 3:4], scale=stat[0:2, 2:3]),
                        waits=(("v", v_st),), inc="s")
                    s_vh = P.val("s")
                    ew("vector", lambda e: e.tensor_add(
                        out=x2b[0][0:2, :], in0=vb[0:2, :], in1=xh[:]),
                        waits=(("s", s_vh), ("v", x2b_free[0][0]),
                               x2b_free[0][1]), inc="v")
                    v_x2h = P.val("v")
                    v_qq = quant1(x2b[0][0:2, :], slice(0, 2), 0, (("v", v_x2h),),
                                  xqi, xqsb, (("t", xqsT_free),))
                    ew("vector", lambda e: e.tensor_scalar(
                        out=qhb[0:2, :], in0=xqsb[0:2, :], scalar1=hmk[:, 0:1],
                        scalar2=None, op0=ALU.mult),
                        waits=(("v", v_qq),) + dx_small, inc="v")
                    v_mq = P.val("v")
                    for j in range(8):
                        ew("tensor", lambda e, j=j: e.matmul(
                            out=ps4[:, j * 2:(j + 1) * 2],
                            lhsT=qhb[0:2, j * 128:(j + 1) * 128], rhs=id2,
                            start=True, stop=True, **MM),
                            waits=(("v", v_mq), ("v", ps4_free)) if j == 0 else (),
                            inc="t")
                    t_tp = P.val("t")
                    ew("vector", lambda e: e.tensor_copy(
                        out=xqsTh[:], in_=ps4[:, 0:16].rearrange("p (c t) -> p c t", c=8)),
                        waits=(("t", t_tp),), inc="v")
                    v_xqsTh = P.val("v")

                    P.fence()
                    P.flush(block, sems)
                    if DEBUG_PHASES < 3:
                        return nc

            with ExitStack() as esC:
                xqsTb = [sb(esC, f"xqsTb{i}", [128, 8, 512], BF16) for i in range(2)]
                t1s = [sb(esC, f"t1s{i}", [128, 8, 1024], BF16) for i in range(2)]
                hbuf = [sb(esC, f"hbuf{i}", [128, 32, 514], BF16) for i in range(2)]
                tconv = [sb(esC, f"tconv{i}", [128, 512], F32) for i in range(2)]
                mxb = sb(esC, "mxb", [128, 512], BF16)
                mnb = sb(esC, "mnb", [128, 512], BF16)
                s2row = sb(esC, "s2row", [1, 512], F32)
                s2b = sb(esC, "s2b", [128, 512], F32)
                qi32 = [sb(esC, f"qi32_{i}", [128, 512], I32) for i in range(2)]

                xqsTb_free = {0: 0, 1: 0}
                t1s_free = {0: 0, 1: 0}
                hbuf_free = {0: (), 1: ()}
                hbuf_ready = {}
                evac_s = {i: [0] * 32 for i in range(NT)}
                psH_free = {0: 0, 1: 0}
                copyleft_done = {}
                ps4c_free = 0
                mx_free_t = 0
                ps3b_free_c = 0
                ps5c_free = 0
                ps6c_free = 0
                stat_c_free = 0
                v_hedge = [0]

                def gen_mm1(i):
                    par = i % 2
                    for cc in range(8):
                        dma("scalar", "dw", xqsTb[par][:, cc, :],
                            xqs_d[cc, :, i * 512:(i + 1) * 512],
                            waits=dg_xqs + (("t", xqsTb_free[par]),)
                            if cc == 0 else ())
                    dxq = qdone("dw")
                    for g in range(4):
                        gpar = g % 2
                        for dc in range(8):
                            dma("scalar", "dw", t1s[gpar][:, dc, :],
                                tern1_d[dc * 128:(dc + 1) * 128,
                                        g * 1024:(g + 1) * 1024],
                                waits=(("t", t1s_free[gpar]),
                                       wstate[("p2g", 7)])
                                if dc == 0 else ())
                        dt1 = qdone("dw")
                        for fj in range(8):
                            fc = g * 8 + fj
                            hpar = fc % 2
                            ph = ps1 if hpar == 0 else ps2
                            for dc in range(8):
                                w0 = []
                                if fj == 0 and dc == 0:
                                    w0 += list(dt1) + list(dxq)
                                    if i == 0 and g == 0:
                                        w0.append(("v", v_xqsTh))
                                if dc == 0:
                                    w0.append(("s", psH_free[hpar]))
                                if dc == 0 and fc == 0 and i >= 1:
                                    w0 += list(hbuf_free[par]) + \
                                        [("v", copyleft_done.get(i - 1, 0))]
                                ew("tensor", lambda e, gpar=gpar, dc=dc, fj=fj,
                                    ph=ph, par=par: e.matmul(
                                        out=ph[:],
                                        lhsT=t1s[gpar][:, dc, fj * 128:(fj + 1) * 128],
                                        rhs=xqsTb[par][:, dc, :],
                                        start=(dc == 0), stop=(dc == 7), **MM),
                                    waits=tuple(w0), inc="t")
                                if i == 0:
                                    ew("tensor", lambda e, gpar=gpar, dc=dc, fj=fj,
                                        fc=fc: e.matmul(
                                            out=ps4[:, fc * 2:(fc + 1) * 2],
                                            lhsT=t1s[gpar][:, dc,
                                                           fj * 128:(fj + 1) * 128],
                                            rhs=xqsTh[:, dc, :],
                                            start=(dc == 0), stop=(dc == 7), **MM),
                                        inc="t")
                            t_f = P.val("t")
                            ew("scalar", lambda e, ph=ph, par=par, fc=fc: e.copy(
                                out=hbuf[par][:, fc, 1:513], in_=ph[:]),
                                waits=(("t", t_f),), inc="s")
                            psH_free[hpar] = P.val("s")
                            evac_s[i][fc] = P.val("s")
                            yield True
                        t1s_free[gpar] = P.val("t")
                    xqsTb_free[par] = P.val("t")
                    hbuf_ready[i] = P.val("s")
                    if i == 0:
                        t_pig = P.val("t")
                        ew("vector", lambda e: e.tensor_copy(
                            out=hedge[:], in_=ps4[:, 0:64]),
                            waits=(("t", t_pig),), inc="v")
                        v_hedge[0] = P.val("v")

                def emit_copyleft(i):
                    par = i % 2
                    if i > 0:
                        src = hbuf[(i - 1) % 2][:, :, 512:513]
                        wts = (("s", hbuf_ready[i]),)
                    else:
                        src = hedge[:].rearrange("p (f c) -> p f c", c=2)[:, :, 0:1]
                        wts = (("s", hbuf_ready[i]), ("v", v_hedge[0]))
                    ew("vector", lambda e, par=par, src=src: e.tensor_copy(
                        out=hbuf[par][:, :, 0:1], in_=src), waits=wts, inc="v")
                    copyleft_done[i] = P.val("v")

                def gen_convquant(i):
                    nonlocal mx_free_t, ps3b_free_c, ps5c_free, ps6c_free, stat_c_free
                    par = i % 2
                    silu_s = {}
                    conv_v = {}

                    def emit_folds(fc):
                        ch = hbuf[par][:, fc, :]
                        if fc == 0:
                            ew("vector", lambda e, ch=ch: e.tensor_copy(
                                out=mxb[:], in_=ch[:, 1:513]),
                                waits=(("s", silu_s[0]), ("t", mx_free_t)), inc="v")
                            ew("vector", lambda e, ch=ch: e.tensor_copy(
                                out=mnb[:], in_=ch[:, 1:513]), inc="v")
                        else:
                            ew("vector", lambda e, ch=ch: e.tensor_max(
                                out=mxb[:], in0=ch[:, 1:513], in1=mxb[:]),
                                waits=(("s", silu_s[fc]),), inc="v")
                            ew("vector", lambda e, ch=ch: e.tensor_tensor(
                                out=mnb[:], in0=ch[:, 1:513], in1=mnb[:],
                                op=ALU.min), inc="v")

                    tap1_s = {}

                    def emit_tap1(fc):
                        ch = hbuf[par][:, fc, :]
                        tc = tconv[fc % 2]
                        w0 = [("v", conv_v[fc - 2])] if fc >= 2 else []
                        ew("scalar", lambda e, ch=ch, fc=fc, tc=tc: e.activation(
                            out=tc[:], in_=ch[:, 1:513], func=AF.Identity,
                            scale=convp[:, 32 + fc:33 + fc]),
                            waits=tuple(w0), inc="s")
                        tap1_s[fc] = P.val("s")

                    emit_tap1(0)
                    for fc in range(32):
                        ch = hbuf[par][:, fc, :]
                        tc = tconv[fc % 2]
                        if i < NT - 1:
                            rh = hbuf[(i + 1) % 2][:, fc, 1:2]
                            wrh = (("s", evac_s[i + 1][fc]),)
                        else:
                            rh = hedge[:].rearrange(
                                "p (f c) -> p f c", c=2)[:, fc, 1:2]
                            wrh = (("v", v_hedge[0]),)
                        ew("vector", lambda e, ch=ch, rh=rh: e.tensor_copy(
                            out=ch[:, 513:514], in_=rh),
                            waits=wrh + (("s", evac_s[i][fc]),), inc="v")
                        if i + 1 < NT:
                            ew("vector", lambda e, ch=ch, fc=fc, i=i: e.tensor_copy(
                                out=hbuf[(i + 1) % 2][:, fc, 0:1],
                                in_=ch[:, 512:513]),
                                waits=(("s", evac_s[i + 1][fc]),), inc="v")
                            if fc == 31:
                                copyleft_done[i + 1] = P.val("v")
                        ew("vector", lambda e, ch=ch, fc=fc, tc=tc: e.scalar_tensor_tensor(
                            out=tc[:], in0=ch[:, 0:512], scalar=convp[:, fc:fc + 1],
                            in1=tc[:], op0=ALU.mult, op1=ALU.add),
                            waits=(("s", tap1_s[fc]),), inc="v")
                        ew("vector", lambda e, ch=ch, fc=fc, tc=tc: e.scalar_tensor_tensor(
                            out=tc[:], in0=ch[:, 2:514],
                            scalar=convp[:, 64 + fc:65 + fc], in1=tc[:],
                            op0=ALU.mult, op1=ALU.add), inc="v")
                        v_c = P.val("v")
                        conv_v[fc] = v_c
                        if fc + 1 < 32:
                            emit_tap1(fc + 1)
                        ew("scalar", lambda e, ch=ch, fc=fc, tc=tc: e.activation(
                            out=ch[:, 1:513], in_=tc[:], func=AF.Silu,
                            bias=convp[:, 96 + fc:97 + fc], scale=1.0),
                            waits=(("v", v_c),), inc="s")
                        silu_s[fc] = P.val("s")
                        if fc >= 1:
                            emit_folds(fc - 1)
                        yield True
                    emit_folds(31)
                    ew("vector", lambda e: e.scalar_tensor_tensor(
                        out=mxb[:], in0=mnb[:], scalar=-1.0, in1=mxb[:],
                        op0=ALU.mult, op1=ALU.max), inc="v")
                    v_amax = P.val("v")
                    for bq in range(4):
                        ew("tensor", lambda e, bq=bq: e.transpose(
                            out=ps3b[:, bq * 128:(bq + 1) * 128],
                            in_=mxb[:, bq * 128:(bq + 1) * 128], identity=idbf[:]),
                            waits=(("v", v_amax), ("v", ps3b_free_c))
                            if bq == 0 else (), inc="t")
                    t_tp = P.val("t")
                    mx_free_t = t_tp
                    for bq in range(4):
                        w0 = [("t", t_tp)] if bq == 0 else []
                        if bq == 0 and stat_c_free:
                            w0.append(("t", stat_c_free))
                        ew("vector", lambda e, bq=bq: e.tensor_reduce(
                            out=stat[:, 8 + bq:9 + bq],
                            in_=ps3b[:, bq * 128:(bq + 1) * 128],
                            axis=AX.X, op=ALU.max), waits=tuple(w0), inc="v")
                    ps3b_free_c = P.val("v")
                    vsettle()
                    for bq in range(4):
                        ew("vector", lambda e, bq=bq: e.tensor_scalar(
                            out=stat[:, 16 + bq:17 + bq], in0=stat[:, 8 + bq:9 + bq],
                            scalar1=EPS, scalar2=None, op0=ALU.max), inc="v")
                        vsettle()
                        ew("vector", lambda e, bq=bq: e.reciprocal(
                            out=stat[:, 20 + bq:21 + bq], in_=stat[:, 16 + bq:17 + bq]),
                            inc="v")
                        vsettle()
                        ew("vector", lambda e, bq=bq: e.tensor_scalar(
                            out=stat[:, 12 + bq:13 + bq], in0=stat[:, 20 + bq:21 + bq],
                            scalar1=127.0, scalar2=None, op0=ALU.mult), inc="v")
                        vsettle()
                        ew("vector", lambda e, bq=bq, i=i: e.scalar_tensor_tensor(
                            out=invb2res[:, i * 4 + bq:i * 4 + bq + 1],
                            in0=stat[:, 16 + bq:17 + bq], scalar=1.0 / 127,
                            in1=wsc[:, 3:4], op0=ALU.mult, op1=ALU.mult), inc="v")
                        vsettle()
                    v_s2 = P.val("v")
                    for bq in range(4):
                        ew("tensor", lambda e, bq=bq: e.transpose(
                            out=ps5[0:1, bq * 128:(bq + 1) * 128],
                            in_=stat[:, 12 + bq:13 + bq], identity=idf32[:]),
                            waits=(("v", v_s2), ("v", ps5c_free))
                            if bq == 0 else (), inc="t")
                    t_s2 = P.val("t")
                    stat_c_free = t_s2
                    ew("vector", lambda e: e.tensor_copy(
                        out=s2row[:], in_=ps5[0:1, 0:512]), waits=(("t", t_s2),), inc="v")
                    ps5c_free = P.val("v")
                    v_r = P.val("v")
                    ew("tensor", lambda e: e.matmul(
                        out=ps6[:], lhsT=ones1[:], rhs=s2row[:], start=True,
                        stop=True, **MM),
                        waits=(("v", v_r), ("v", ps6c_free)), inc="t")
                    t_b = P.val("t")
                    ew("vector", lambda e: e.tensor_copy(out=s2b[:], in_=ps6[:]),
                       waits=(("t", t_b),), inc="v")
                    ps6c_free = P.val("v")
                    v_s2b = P.val("v")
                    cast_s = {}
                    for fc in range(32):
                        ch = hbuf[par][:, fc, :]
                        w0 = [("v", v_s2b)] if fc == 0 else []
                        if fc >= 2:
                            w0.append(("s", cast_s[fc - 2]))
                        ew("vector", lambda e, ch=ch, fc=fc: e.tensor_tensor(
                            out=qi32[fc % 2][:], in0=ch[:, 1:513], in1=s2b[:],
                            op=ALU.mult), waits=tuple(w0), inc="v")
                        v_tt = P.val("v")
                        ew("scalar", lambda e, ch=ch, fc=fc: e.activation(
                            out=ch[:, 1:513], in_=qi32[fc % 2][:],
                            func=AF.Identity), waits=(("v", v_tt),), inc="s")
                        cast_s[fc] = P.val("s")
                        dma("gpsimd", "dg",
                            hq_d[fc, :, i * 512:(i + 1) * 512],
                            hbuf[par][:, fc, 1:513],
                            waits=(("s", cast_s[fc]),))
                        yield True
                    hbuf_free[par] = qdone("dg")

                for _ in gen_mm1(0):
                    pass
                emit_copyleft(0)
                for i in range(NT):
                    gmm = gen_mm1(i + 1) if i + 1 < NT else None
                    gcv = gen_convquant(i)
                    while True:
                        if gmm is not None and next(gmm, None) is None:
                            gmm = None
                        if next(gcv, None) is None:
                            break
                    while gmm is not None and next(gmm, None) is None:
                        gmm = None

                P.fence()
                P.flush(block, sems)
                if DEBUG_PHASES < 4:
                    return nc

            with ExitStack() as esD:
                hqb = [sb(esD, f"hqb{i}", [128, 512], BF16) for i in range(4)]
                t2res = sb(esD, "t2res", [128, 32, 1024], BF16)
                x2l = [sb(esD, f"x2l{i}", [128, 1024], F32) for i in range(2)]
                y2b = [sb(esD, f"y2b{i}", [128, 1024], F32) for i in range(2)]
                v2b = sb(esD, "v2b", [128, 1024], F32)
                ob = [sb(esD, f"ob{i}", [128, 1024], F32) for i in range(2)]

                accs = [(ps0[:, 0:512], ps0[:, 512:1024]),
                        (ps1[:], ps2[:]),
                        (ps3[:], ps4[:]),
                        (ps5[:], ps6[:])]
                hqb_free = {0: 0, 1: 0, 2: 0, 3: 0}
                x2l_free = {0: 0, 1: 0}
                ob_free = {0: ("v", 0), 1: ("v", 0)}
                acc_free_v = [0, 0, 0, 0]
                y2b_free_s = {0: 0, 1: 0}

                for i in range(NT):
                    for fc in range(32):
                        fpar = fc % 4
                        dh = dma("sync", "dx", hqb[fpar][:],
                                 hq_d[fc, :, i * 512:(i + 1) * 512],
                                 waits=(("t", hqb_free[fpar]),))
                        dt2 = dma("scalar", "dw", t2res[:, fc, :],
                                  tern2_d[fc * 128:(fc + 1) * 128, :]) \
                            if i == 0 else None
                        for tsub in range(4):
                            w0 = []
                            if tsub == 0:
                                w0.append(dh)
                                if dt2 is not None:
                                    w0.append(dt2)
                            if fc == 0:
                                w0.append(("v", acc_free_v[tsub]))
                            a0, a1 = accs[tsub]
                            ew("tensor", lambda e, fpar=fpar, tsub=tsub, a0=a0,
                                fc=fc: e.matmul(
                                    out=a0,
                                    lhsT=hqb[fpar][:, tsub * 128:(tsub + 1) * 128],
                                    rhs=t2res[:, fc, 0:512],
                                    start=(fc == 0), stop=(fc == 31), **MM),
                                waits=tuple(w0), inc="t")
                            ew("tensor", lambda e, fpar=fpar, tsub=tsub, a1=a1,
                                fc=fc: e.matmul(
                                    out=a1,
                                    lhsT=hqb[fpar][:, tsub * 128:(tsub + 1) * 128],
                                    rhs=t2res[:, fc, 512:1024],
                                    start=(fc == 0), stop=(fc == 31), **MM), inc="t")
                        hqb_free[fpar] = P.val("t")
                    t_last = P.val("t")
                    for tsub in range(4):
                        oc = i * 4 + tsub
                        par = oc % 2
                        a0, a1 = accs[tsub]
                        dx2 = dma("scalar", "dw", x2l[par][:],
                                  x2_d[oc * 128:(oc + 1) * 128, :],
                                  waits=(("v", x2l_free[par]),))
                        ew("vector", lambda e, a0=a0, i=i, tsub=tsub, par=par:
                            e.tensor_scalar(
                                out=y2b[par][:, 0:512], in0=a0,
                                scalar1=invb2res[:, i * 4 + tsub:i * 4 + tsub + 1],
                                scalar2=None, op0=ALU.mult),
                            waits=(("t", t_last), ("s", y2b_free_s[par])), inc="v")
                        ew("vector", lambda e, a1=a1, i=i, tsub=tsub, par=par:
                            e.tensor_scalar(
                                out=y2b[par][:, 512:1024], in0=a1,
                                scalar1=invb2res[:, i * 4 + tsub:i * 4 + tsub + 1],
                                scalar2=None, op0=ALU.mult), inc="v")
                        acc_free_v[tsub] = P.val("v")
                        v_y2 = P.val("v")
                        v_st = ln_stats(y2b[par][:], slice(0, 128), oc % 2,
                                        (("v", v_y2),))
                        ew("scalar", lambda e, oc=oc, par=par: e.activation(
                            out=v2b[:], in_=y2b[par][:], func=AF.Identity,
                            bias=stat[:, 16 * (oc % 2) + 3:16 * (oc % 2) + 4],
                            scale=stat[:, 16 * (oc % 2) + 2:16 * (oc % 2) + 3]),
                            waits=(("v", v_st),), inc="s")
                        s_v2 = P.val("s")
                        y2b_free_s[par] = s_v2
                        ew("vector", lambda e, par=par: e.tensor_add(
                            out=ob[par][:], in0=v2b[:], in1=x2l[par][:]),
                            waits=(("s", s_v2), dx2, ob_free[par]),
                            inc="v")
                        x2l_free[par] = P.val("v")
                        v_o = P.val("v")
                        ob_free[par] = dma("gpsimd", "dg",
                                           out_d[oc * 128:(oc + 1) * 128, :],
                                           ob[par][:], waits=(("v", v_o),))

                P.emit("gpsimd", None, waits=qdone("dg"))
                P.flush(block, sems)
    return nc


_cache = {}


def kernel(**inputs):
    x = np.ascontiguousarray(np.asarray(inputs["L0_attn_in_sequence_bwc"], np.float32))
    proj = np.asarray(inputs["L0_attn_projection_1nck"], np.float32)[0]
    halves = np.asarray(inputs["L0_attn_halves"], np.float32)
    diag = np.asarray(inputs["L0_attn_diagonals_nk"], np.float32)
    mw = np.asarray(inputs["L0_attn_mixer_w"], np.float32)
    mb = np.asarray(inputs["L0_attn_mixer_b"], np.float32)
    w1 = np.asarray(inputs["L0_ffn_w1"], np.float32)
    cw = np.asarray(inputs["L0_ffn_conv_w"], np.float32)
    cb = np.asarray(inputs["L0_ffn_conv_b"], np.float32)
    w2 = np.asarray(inputs["L0_ffn_w2"], np.float32)

    p2d = np.ascontiguousarray(proj.transpose(1, 0, 2).reshape(CDIM, NHEAD * KH))
    iu, ju = np.triu_indices(KH, k=1)
    M = np.zeros((NHEAD, KH, KH), np.float32)
    M[:, iu, ju] = halves
    M[:, ju, iu] = halves
    M[:, np.arange(KH), np.arange(KH)] = diag
    mblk0 = M.transpose(1, 0, 2).reshape(KH, NHEAD * KH)
    mblk = np.ascontiguousarray(np.concatenate([mblk0, mblk0], axis=0))
    mwt = np.ascontiguousarray(mw.T)
    mbias = np.ascontiguousarray(np.broadcast_to(mb[None, :], (128, CDIM)))
    w1t = np.ascontiguousarray(w1.T)
    w2t = np.ascontiguousarray(w2.T)
    convp = np.zeros((128, 128), np.float32)
    for tap in range(3):
        convp[:, tap * 32:(tap + 1) * 32] = cw[:, 0, tap].reshape(32, 128).T
    convp[:, 96:128] = cb.reshape(32, 128).T
    idbf = np.eye(128, dtype=ml_dtypes.bfloat16)
    idf32 = np.eye(128, dtype=np.float32)
    ones1 = np.ones((1, 128), np.float32)

    key = (DEBUG_PHASES, DEBUG_A_TILES, DEBUG_WPREP, DEBUG_A_STEPS, DEBUG_TAPS)
    if key not in _cache:
        _cache[key] = build_program()
    nc = _cache[key]

    in_maps = []
    for c in range(8):
        b, s = c // 4, c % 4
        xb = np.roll(x[b], -s * TOK, axis=0) if s else x[b]
        t0 = s * TOK
        xhalo = np.zeros((2, CDIM), np.float32)
        hm = np.zeros((2, 1), np.float32)
        if t0 - 1 >= 0:
            xhalo[0] = x[b, t0 - 1]
            hm[0, 0] = 1.0
        if t0 + TOK < SEQ:
            xhalo[1] = x[b, t0 + TOK]
            hm[1, 0] = 1.0
        in_maps.append({
            "xfull": np.ascontiguousarray(xb), "xhalo": xhalo, "hmask": hm,
            "p2d": p2d, "mblk": mblk, "mwt": mwt, "mbias": mbias,
            "w1t": w1t, "w2t": w2t, "convp": convp,
            "idbf": idbf, "idf32": idf32, "ones1": ones1,
        })

    res = run_bass_kernel_spmd(nc, in_maps, core_ids=list(range(8)))
    if DEBUG_TAPS:
        global _taps
        _taps = res.results
    out = np.empty((BATCH, SEQ, CDIM), np.float32)
    for c in range(8):
        b, s = c // 4, c % 4
        out[b, s * TOK:(s + 1) * TOK] = res.results[c]["out"]
    return out


# version 22
# speedup vs baseline: 1.0698x; 1.0039x over previous
import numpy as np
import ml_dtypes
from contextlib import ExitStack

import concourse.bass as bass
import concourse.mybir as mybir
from concourse.bass_utils import run_bass_kernel_spmd

F32 = mybir.dt.float32
BF16 = mybir.dt.bfloat16
I32 = mybir.dt.int32
AF = mybir.ActivationFunctionType
ALU = mybir.AluOpType
AX = mybir.AxisListType

BATCH, SEQ, CDIM = 2, 8192, 1024
NHEAD, KH = 16, 64
FDIM = 4096
TOK = 2048
TILE = 512
NT_A = SEQ // TILE
NT = TOK // TILE
EPS = 1e-5

ENGS = ("sync", "scalar", "vector", "tensor", "gpsimd")
DEBUG_PHASES = 4
DEBUG_A_TILES = NT_A
DEBUG_WPREP = True
DEBUG_A_STEPS = 9
DEBUG_TAPS = False


class Prog:
    def __init__(self):
        self.items = {e: [] for e in ENGS}
        self.cnt = {}

    def add_sem(self, name):
        self.cnt[name] = 0

    def val(self, name):
        return self.cnt[name]

    def emit(self, eng, fn=None, waits=(), inc=None, inc_by=1):
        ws = tuple((s, v) for s, v in waits if v > 0)
        for s, v in ws:
            assert v <= self.cnt[s], f"wait on future {s}:{v}>{self.cnt[s]}"
        if inc is not None:
            assert fn is not None
            self.cnt[inc] += inc_by
        self.items[eng].append((fn, ws, inc, inc_by))
        return self.cnt[inc] if inc is not None else None

    def fence(self):
        snap = [(s, v) for s, v in self.cnt.items() if v > 0]
        for e in ENGS:
            self.emit(e, None, waits=snap)

    def flush(self, block, sems):
        def runner(items):
            def go(eng):
                for fn, waits, inc, inc_by in items:
                    for snm, v in waits:
                        eng.wait_ge(sems[snm], v)
                    if fn is not None:
                        inst = fn(eng)
                        if inc is not None:
                            inst.then_inc(sems[inc], inc_by)
            return go
        for e in ENGS:
            if self.items[e]:
                getattr(block, e)(runner(self.items[e]))
            self.items[e] = []


def build_program():
    nc = bass.Bass("TRN2", target_bir_lowering=False, detect_race_conditions=False)

    xfull = nc.dram_tensor("xfull", [SEQ, CDIM], F32, kind="ExternalInput").ap()
    xhalo = nc.dram_tensor("xhalo", [2, CDIM], F32, kind="ExternalInput").ap()
    hmask = nc.dram_tensor("hmask", [2, 1], F32, kind="ExternalInput").ap()
    p2d_d = nc.dram_tensor("p2d", [CDIM, CDIM], F32, kind="ExternalInput").ap()
    mblk_d = nc.dram_tensor("mblk", [128, NHEAD * KH], F32, kind="ExternalInput").ap()
    mwt_d = nc.dram_tensor("mwt", [CDIM, CDIM], F32, kind="ExternalInput").ap()
    mbias_d = nc.dram_tensor("mbias", [128, CDIM], F32, kind="ExternalInput").ap()
    w1t_d = nc.dram_tensor("w1t", [CDIM, FDIM], F32, kind="ExternalInput").ap()
    w2t_d = nc.dram_tensor("w2t", [FDIM, CDIM], F32, kind="ExternalInput").ap()
    convp_d = nc.dram_tensor("convp", [128, 128], F32, kind="ExternalInput").ap()
    idbf_d = nc.dram_tensor("idbf", [128, 128], BF16, kind="ExternalInput").ap()
    idf32_d = nc.dram_tensor("idf32", [128, 128], F32, kind="ExternalInput").ap()
    ones1_d = nc.dram_tensor("ones1", [1, 128], F32, kind="ExternalInput").ap()
    out_d = nc.dram_tensor("out", [TOK, CDIM], F32, kind="ExternalOutput").ap()

    KIND = "ExternalOutput" if DEBUG_TAPS else "Internal"
    tern1_d = nc.dram_tensor("tern1_s", [CDIM, FDIM], BF16, kind=KIND).ap()
    tern2_d = nc.dram_tensor("tern2_s", [FDIM, CDIM], BF16, kind=KIND).ap()
    x2_d = nc.dram_tensor("x2_s", [TOK, CDIM], F32, kind=KIND).ap()
    xqs_d = nc.dram_tensor("xqs_s", [8, 128, TOK], BF16, kind=KIND).ap()
    hq_d = nc.dram_tensor("hq_s", [32, 128, TOK], BF16, kind=KIND).ap()
    if DEBUG_TAPS:
        dbg_g = nc.dram_tensor("dbg_g", [KH, 1024], BF16, kind="ExternalOutput").ap()
        dbg_a = nc.dram_tensor("dbg_a", [128, 1024], BF16, kind="ExternalOutput").ap()
        dbg_q = nc.dram_tensor("dbg_q", [128, 8, 512], BF16, kind="ExternalOutput").ap()
        dbg_y = nc.dram_tensor("dbg_y", [128, 1024], F32, kind="ExternalOutput").ap()
        dbg_st = nc.dram_tensor("dbg_st", [128, 32], F32, kind="ExternalOutput").ap()
        dbg_u = nc.dram_tensor("dbg_u", [128, 1024], BF16, kind="ExternalOutput").ap()
        dbg_uT = nc.dram_tensor("dbg_uT", [128, 1024], BF16, kind="ExternalOutput").ap()
        dbg_qt = nc.dram_tensor("dbg_qt", [128, 1024], BF16, kind="ExternalOutput").ap()
        dbg_xt = nc.dram_tensor("dbg_xt", [128, 1024], F32, kind="ExternalOutput").ap()
        dbg_sq = nc.dram_tensor("dbg_sq", [128, 1024], F32, kind="ExternalOutput").ap()
        dbg_qall = nc.dram_tensor("dbg_qall", [64, 128, 1024], BF16, kind="ExternalOutput").ap()

    P = Prog()
    DMA_SEMS = {q: [f"{q}{i}" for i in range(4)] for q in ("dx", "dw", "dg")}
    for q in DMA_SEMS:
        for snm in DMA_SEMS[q]:
            P.add_sem(snm)
    for snm in ("v", "s", "t"):
        P.add_sem(snm)
    _rr = {q: 0 for q in DMA_SEMS}

    MM = dict(skip_group_check=True)

    with ExitStack() as es:
        def sb(stack, name, shape, dt):
            return stack.enter_context(nc.sbuf_tensor(name, shape, dt))

        sems = {}
        for q in DMA_SEMS:
            for snm in DMA_SEMS[q]:
                sems[snm] = es.enter_context(nc.semaphore(snm))
        for snm in ("v", "s", "t"):
            sems[snm] = es.enter_context(nc.semaphore(snm))

        ps0 = es.enter_context(nc.psum_tensor("ps0", [128, 1024], F32))
        ps1 = es.enter_context(nc.psum_tensor("ps1", [128, 512], F32))
        ps2 = es.enter_context(nc.psum_tensor("ps2", [128, 512], F32))
        ps3 = es.enter_context(nc.psum_tensor("ps3", [128, 512], F32))
        ps4 = es.enter_context(nc.psum_tensor("ps4", [128, 512], F32))
        ps5 = es.enter_context(nc.psum_tensor("ps5", [128, 512], F32))
        ps6 = es.enter_context(nc.psum_tensor("ps6", [128, 512], F32))
        ps3b = ps3[:].bitcast(BF16)
        ps6b = ps6[:].bitcast(BF16)

        pb = sb(es, "pb", [128, 8, 1024], BF16)
        mwtb = sb(es, "mwtb", [128, 8, 1024], BF16)
        mblkb = sb(es, "mblkb", [128, NHEAD * KH], BF16)
        ablk = sb(es, "ablk", [128, 8 * 128], BF16)
        mbias = sb(es, "mbias_t", [128, 1024], F32)
        convp = sb(es, "convp_t", [128, 128], F32)
        idbf = sb(es, "idbf_t", [128, 128], BF16)
        idf32 = sb(es, "idf32_t", [128, 128], F32)
        ones1 = sb(es, "ones1_t", [1, 128], F32)
        gsb = sb(es, "gsb", [128, 8 * 128], BF16)
        cpool = sb(es, "cpool", [128, 4], F32)
        wsc = sb(es, "wsc", [128, 4], F32)
        invb2res = sb(es, "invb2res", [128, 16], F32)
        xqsTh = sb(es, "xqsTh", [128, 8, 2], BF16)
        hedge = sb(es, "hedge", [128, 64], BF16)
        stat = sb(es, "stat", [128, 32], F32)
        tiny = sb(es, "tiny", [128, 16], F32)
        sqb = sb(es, "sqb", [128, 1024], F32)
        xh = sb(es, "xh_t", [2, CDIM], F32)
        zcol = sb(es, "zcol", [128, KH], BF16)
        hmk = sb(es, "hmk_t", [2, 1], F32)

        def dma(qeng, queue, out, in_, waits=()):
            qsem = DMA_SEMS[queue][_rr[queue] % len(DMA_SEMS[queue])]
            _rr[queue] += 1
            def f(e, out=out, in_=in_):
                return e.dma_start(out=out, in_=in_)
            waits = tuple(waits) + ((qsem, P.val(qsem)),)
            v = P.emit(qeng, f, waits=waits, inc=qsem, inc_by=16)
            return (qsem, v)

        def qdone(queue):
            return tuple((snm, P.val(snm)) for snm in DMA_SEMS[queue])

        def ew(eng, fn, waits=(), inc=None):
            return P.emit(eng, fn, waits=waits, inc=inc, inc_by=1)

        def vsettle():
            P.emit("vector", None, waits=(("v", P.val("v")),))

        ms_v = {0: 0, 1: 0}
        sqb_free = [0]
        sqrt_s = {0: 0, 1: 0}

        def ln_stats(xap, pslice, par, wx):
            o = 16 * par
            ew("vector", lambda e: e.tensor_reduce(
                out=stat[pslice, o:o + 1], in_=xap, axis=AX.X, op=ALU.add),
                waits=wx, inc="v")
            vsettle()
            ew("scalar", lambda e: e.activation(
                out=sqb[pslice, :], in_=xap, func=AF.Square),
                waits=wx + (("v", sqb_free[0]),), inc="s")
            sq_s = P.val("s")
            ew("vector", lambda e: e.tensor_reduce(
                out=stat[pslice, o + 1:o + 2], in_=sqb[pslice, :], axis=AX.X,
                op=ALU.add), waits=(("v", ms_v[par]), ("s", sq_s)), inc="v")
            vsettle()
            sqb_free[0] = P.val("v")
            ew("vector", lambda e: e.tensor_scalar(
                out=stat[pslice, o + 4:o + 5], in0=stat[pslice, o:o + 1],
                scalar1=1.0 / 1024, scalar2=None, op0=ALU.mult), inc="v")
            vsettle()
            ew("vector", lambda e: e.tensor_scalar(
                out=stat[pslice, o + 5:o + 6], in0=stat[pslice, o + 1:o + 2],
                scalar1=1.0 / 1024, scalar2=None, op0=ALU.mult), inc="v")
            vsettle()
            ms_v[par] = P.val("v")
            ew("vector", lambda e: e.scalar_tensor_tensor(
                out=stat[pslice, o + 6:o + 7], in0=stat[pslice, o + 4:o + 5],
                scalar=stat[pslice, o + 4:o + 5], in1=stat[pslice, o + 5:o + 6],
                op0=ALU.mult, op1=ALU.subtract),
                waits=(("s", sqrt_s[par]),), inc="v")
            vsettle()
            vV = P.val("v")
            ew("scalar", lambda e: e.activation(
                out=stat[pslice, o + 7:o + 8], in_=stat[pslice, o + 6:o + 7],
                func=AF.Sqrt, bias=cpool[pslice, 0:1], scale=-1.0),
                waits=(("v", vV),), inc="s")
            sqrt_s[par] = P.val("s")
            vS2 = P.val("s")
            ew("vector", lambda e: e.reciprocal(
                out=stat[pslice, o + 2:o + 3], in_=stat[pslice, o + 7:o + 8]),
                waits=(("s", vS2),), inc="v")
            vsettle()
            ew("vector", lambda e: e.scalar_tensor_tensor(
                out=stat[pslice, o + 3:o + 4], in0=stat[pslice, o + 4:o + 5],
                scalar=-1.0, in1=stat[pslice, o + 2:o + 3],
                op0=ALU.mult, op1=ALU.mult), inc="v")
            vsettle()
            return P.val("v")

        def quant1(x2ap, pslice, par, wready, xqi, xqsb, xqs_gate):
            o = 16 * par
            ln_stats(x2ap, pslice, par, wready)
            ew("vector", lambda e: e.tensor_reduce(
                out=stat[pslice, o + 8:o + 9], in_=x2ap, axis=AX.X, op=ALU.max), inc="v")
            vsettle()
            ew("vector", lambda e: e.tensor_reduce(
                out=stat[pslice, o + 9:o + 10], in_=x2ap, axis=AX.X, op=ALU.min), inc="v")
            vsettle()
            ew("vector", lambda e: e.tensor_sub(
                out=stat[pslice, o + 10:o + 11], in0=stat[pslice, o + 8:o + 9],
                in1=stat[pslice, o + 4:o + 5]), inc="v")
            vsettle()
            ew("vector", lambda e: e.tensor_sub(
                out=stat[pslice, o + 11:o + 12], in0=stat[pslice, o + 4:o + 5],
                in1=stat[pslice, o + 9:o + 10]), inc="v")
            vsettle()
            ew("vector", lambda e: e.tensor_max(
                out=stat[pslice, o + 12:o + 13], in0=stat[pslice, o + 10:o + 11],
                in1=stat[pslice, o + 11:o + 12]), inc="v")
            vsettle()
            ew("vector", lambda e: e.tensor_tensor(
                out=stat[pslice, o + 13:o + 14], in0=stat[pslice, o + 12:o + 13],
                in1=stat[pslice, o + 2:o + 3], op=ALU.mult), inc="v")
            vsettle()
            ew("vector", lambda e: e.tensor_scalar(
                out=stat[pslice, o + 14:o + 15], in0=stat[pslice, o + 13:o + 14],
                scalar1=EPS, scalar2=None, op0=ALU.max), inc="v")
            vsettle()
            ew("vector", lambda e: e.reciprocal(
                out=stat[pslice, o + 15:o + 16], in_=stat[pslice, o + 14:o + 15]), inc="v")
            vsettle()
            ew("vector", lambda e: e.tensor_scalar(
                out=stat[pslice, o + 8:o + 9], in0=stat[pslice, o + 15:o + 16],
                scalar1=127.0, scalar2=None, op0=ALU.mult), inc="v")
            vsettle()
            ew("vector", lambda e: e.tensor_tensor(
                out=stat[pslice, o + 9:o + 10], in0=stat[pslice, o + 2:o + 3],
                in1=stat[pslice, o + 8:o + 9], op=ALU.mult), inc="v")
            vsettle()
            ew("vector", lambda e: e.scalar_tensor_tensor(
                out=stat[pslice, o + 10:o + 11], in0=stat[pslice, o + 4:o + 5],
                scalar=-1.0, in1=stat[pslice, o + 9:o + 10],
                op0=ALU.mult, op1=ALU.mult), inc="v")
            vsettle()
            ew("vector", lambda e: e.scalar_tensor_tensor(
                out=stat[pslice, o + 11:o + 12], in0=stat[pslice, o + 14:o + 15],
                scalar=1.0 / 127, in1=wsc[pslice, 1:2],
                op0=ALU.mult, op1=ALU.mult), inc="v")
            vsettle()
            ew("vector", lambda e: e.tensor_scalar(
                out=xqi[pslice, :], in0=x2ap, scalar1=stat[pslice, o + 9:o + 10],
                scalar2=stat[pslice, o + 10:o + 11], op0=ALU.mult, op1=ALU.add), inc="v")
            vsettle()
            ew("vector", lambda e: e.tensor_scalar(
                out=xqsb[pslice, :], in0=xqi[pslice, :],
                scalar1=stat[pslice, o + 11:o + 12], scalar2=None, op0=ALU.mult),
                waits=xqs_gate, inc="v")
            vsettle()
            return P.val("v")

        with nc.Block() as block:
            with ExitStack() as esQ:
                qTres = sb(esQ, "qTres", [128, 8, TOK], BF16)
                esA = esQ.enter_context(ExitStack())
                pstage = sb(esA, "pstage", [128, 4 * 1024], F32)
                xt = [sb(esA, f"xt{i}", [128, 1024], F32) for i in range(4)]
                sqb2 = [sb(esA, f"sqb2_{i}", [128, 1024], F32) for i in range(2)]
                statA = sb(esA, "statA", [128, 96], F32)
                ub = [sb(esA, f"ub{i}", [128, 1024], BF16) for i in range(2)]
                uT = [sb(esA, f"uT{i}", [128, 1024], BF16) for i in range(2)]
                qtok = [sb(esA, f"qtok{i}", [128, 1024], BF16) for i in range(2)]
                wstage = [sb(esA, f"wstage{i}", [128, 4096], F32) for i in range(2)]
                wi32 = sb(esA, "wi32", [128, 4096], I32)
                wtb = [sb(esA, f"wtb{i}", [128, 4096], BF16) for i in range(2)]
                wabs = sb(esA, "wabs", [128, 48], F32)

                ew("vector", lambda e: e.memset(cpool[:, 0:1], EPS), inc="v")
                ew("vector", lambda e: e.memset(cpool[:, 1:2], 0.0), inc="v")
                ew("vector", lambda e: e.memset(cpool[:, 2:3], 1.0), inc="v")
                ew("vector", lambda e: e.memset(zcol[:], 0.0), inc="v")
                ew("vector", lambda e: e.memset(ablk[:], 0.0), inc="v")
                ew("vector", lambda e: e.memset(hedge[:], 0.0), inc="v")
                v_consts = P.val("v")

                dma("sync", "dx", mbias[:], mbias_d)
                dma("sync", "dx", convp[:], convp_d)
                dma("sync", "dx", idbf[:], idbf_d)
                dma("sync", "dx", idf32[:], idf32_d)
                dma("sync", "dx", ones1[:], ones1_d)
                dma("sync", "dx", xh[:], xhalo)
                dma("sync", "dx", hmk[:], hmask)
                dma("sync", "dx", pstage[:, 0:NHEAD * KH], mblk_d)
                dx_small = qdone("dx")
                ew("vector", lambda e: e.tensor_copy(
                    out=mblkb[:], in_=pstage[:, 0:NHEAD * KH]),
                    waits=dx_small, inc="v")
                v_gate = P.val("v")
                v_mblk = v_gate
                for dst_b, src_d in ((pb, p2d_d), (mwtb, mwt_d)):
                    for h in range(2):
                        for cc in range(4):
                            dma("sync", "dx",
                                pstage[:, cc * 1024:(cc + 1) * 1024],
                                src_d[(h * 4 + cc) * 128:(h * 4 + cc + 1) * 128, :],
                                waits=(("v", v_gate),) if cc == 0 else ())
                        dxp = qdone("dx")
                        for cc in range(4):
                            ew("vector", lambda e, dst_b=dst_b, cc=cc, h=h: e.tensor_copy(
                                out=dst_b[:, h * 4 + cc, :],
                                in_=pstage[:, cc * 1024:(cc + 1) * 1024]),
                                waits=dxp if cc == 0 else (), inc="v")
                        v_gate = P.val("v")
                v_pb = v_gate
                v_mwtb = v_gate

                wblocks = [("w1", bi) for bi in range(8)] + [("w2", bi) for bi in range(32)]
                wstate = {}

                def w_src(kind, bi):
                    if kind == "w1":
                        return w1t_d[bi * 128:(bi + 1) * 128, :], 4096
                    return w2t_d[bi * 128:(bi + 1) * 128, :], 1024

                def emit_wpass1(idx):
                    kind, bi = wblocks[idx]
                    src, ncols = w_src(kind, bi)
                    dst = wstage[idx % 2][:, 0:ncols]
                    wfree = wstate.get(("p1v", idx - 2), 0)
                    d = dma("scalar", "dw", dst, src, waits=(("v", wfree),))
                    col = bi if kind == "w1" else 8 + bi
                    ew("vector", lambda e, dst=dst, col=col: e.tensor_reduce(
                        out=wabs[:, col:col + 1], in_=dst, axis=AX.X, op=ALU.add,
                        apply_absolute_value=True), waits=(d,), inc="v")
                    wstate[("p1v", idx)] = P.val("v")

                def emit_wscales():
                    vA = wstate[("p1v", len(wblocks) - 1)]
                    vsettle()
                    ew("vector", lambda e: e.tensor_reduce(
                        out=tiny[:, 0:1], in_=wabs[:, 0:8], axis=AX.X, op=ALU.add),
                        waits=(("v", vA),), inc="v")
                    ew("vector", lambda e: e.tensor_reduce(
                        out=tiny[:, 1:2], in_=wabs[:, 8:40], axis=AX.X, op=ALU.add),
                        inc="v")
                    vsettle()
                    vv = P.val("v")
                    ew("tensor", lambda e: e.matmul(
                        out=ps6[0:1, 0:2], lhsT=cpool[:, 2:3], rhs=tiny[:, 0:2],
                        start=True, stop=True, **MM), waits=(("v", vv),), inc="t")
                    tt = P.val("t")
                    ew("vector", lambda e: e.tensor_scalar(
                        out=tiny[0:1, 2:3], in0=ps6[0:1, 0:1],
                        scalar1=1.0 / (CDIM * FDIM), scalar2=1e-5,
                        op0=ALU.mult, op1=ALU.max), waits=(("t", tt),), inc="v")
                    ew("vector", lambda e: e.tensor_scalar(
                        out=tiny[0:1, 3:4], in0=ps6[0:1, 1:2],
                        scalar1=1.0 / (CDIM * FDIM), scalar2=1e-5,
                        op0=ALU.mult, op1=ALU.max), inc="v")
                    ew("vector", lambda e: e.reciprocal(
                        out=tiny[0:1, 4:5], in_=tiny[0:1, 2:3]), inc="v")
                    ew("vector", lambda e: e.reciprocal(
                        out=tiny[0:1, 5:6], in_=tiny[0:1, 3:4]), inc="v")
                    vsettle()
                    vv2 = P.val("v")
                    ew("tensor", lambda e: e.matmul(
                        out=ps6[:, 4:8], lhsT=ones1[:], rhs=tiny[0:1, 2:6],
                        start=True, stop=True, **MM), waits=(("v", vv2),), inc="t")
                    tt2 = P.val("t")
                    ew("vector", lambda e: e.tensor_copy(out=wsc[:, 1:2], in_=ps6[:, 4:5]),
                       waits=(("t", tt2),), inc="v")
                    ew("vector", lambda e: e.tensor_copy(out=wsc[:, 3:4], in_=ps6[:, 5:6]),
                       inc="v")
                    ew("vector", lambda e: e.tensor_copy(out=wsc[:, 0:1], in_=ps6[:, 6:7]),
                       inc="v")
                    ew("vector", lambda e: e.tensor_copy(out=wsc[:, 2:3], in_=ps6[:, 7:8]),
                       inc="v")
                    wstate["scales_v"] = P.val("v")

                def emit_wpass2(idx):
                    kind, bi = wblocks[idx]
                    src, ncols = w_src(kind, bi)
                    dst = wstage[idx % 2][:, 0:ncols]
                    wfree = wstate.get(("p2v", idx - 2), 0)
                    d = dma("scalar", "dw", dst, src,
                            waits=(("v", wstate["scales_v"]), ("v", wfree)))
                    scol = wsc[:, 0:1] if kind == "w1" else wsc[:, 2:3]
                    ew("vector", lambda e, dst=dst, scol=scol, ncols=ncols: e.tensor_scalar(
                        out=wi32[:, 0:ncols], in0=dst, scalar1=scol, scalar2=None,
                        op0=ALU.mult), waits=(d,), inc="v")
                    ew("vector", lambda e, ncols=ncols, idx=idx: e.tensor_scalar(
                        out=wtb[idx % 2][:, 0:ncols], in0=wi32[:, 0:ncols], scalar1=1.0,
                        scalar2=-1.0, op0=ALU.min, op1=ALU.max),
                        waits=(wstate.get(("p2g", idx - 2), ("v", 0)),), inc="v")
                    wstate[("p2v", idx)] = P.val("v")
                    tdst = (tern1_d if kind == "w1" else tern2_d)[bi * 128:(bi + 1) * 128, :]
                    wstate[("p2g", idx)] = dma("gpsimd", "dg", tdst, wtb[idx % 2][:, 0:ncols],
                                               waits=(("v", P.val("v")),))

                wq1 = iter(range(len(wblocks)))
                wq2 = iter(range(len(wblocks)))
                scales_done = [False]

                psA_free = {0: ("s", 0), 1: ("s", 0)}
                psT_free = ("v", 0)
                xt4_free = {k: (0, 0) for k in range(4)}
                sqb2_free = {0: 0, 1: 0}
                statA_sfree = {0: 0, 1: 0}
                t_G_last = 0
                ub_free = {0: 0, 1: 0}
                uT_free = {0: 0, 1: 0}
                qtok_free = {0: 0, 1: 0}
                qtok_dump = {0: ("v", 0), 1: ("v", 0)}

                pendG = [None]
                ps6T_free = [0]

                def flush_G(gt, par, s_qk, owned):
                    nonlocal t_G_last
                    if gt == 0:
                        for pg in (ps1, ps2):
                            ew("tensor", lambda e, pg=pg, par=par: e.matmul(
                                out=pg[0:KH, :], lhsT=zcol[:, :],
                                rhs=qtok[par][:, 0:512],
                                start=True, stop=False, **MM),
                                waits=(("v", s_qk),) if pg is ps1 else (),
                                inc="t")
                    for n in range(16):
                        pg = ps1 if n < 8 else ps2
                        off = (n % 8) * 64
                        ew("tensor", lambda e, par=par, n=n, pg=pg, off=off, gt=gt:
                            e.matmul(
                                out=pg[0:KH, off:off + KH],
                                lhsT=qtok[par][:, n * KH:(n + 1) * KH],
                                rhs=qtok[par][:, n * KH:(n + 1) * KH],
                                start=False, stop=(gt == DEBUG_A_TILES * 4 - 1), **MM),
                            waits=(("v", s_qk),) if n == 0 else (), inc="t")
                    t_G_last = P.val("t")
                    if owned:
                        for j in range(8):
                            ew("tensor", lambda e, par=par, j=j: e.transpose(
                                out=ps6b[:, j * 128:(j + 1) * 128],
                                in_=qtok[par][:, j * 128:(j + 1) * 128],
                                identity=idbf[:]),
                                waits=(("v", ps6T_free[0]),) if j == 0 else (),
                                inc="t")
                        t_tp2 = P.val("t")
                        ew("vector", lambda e, gt=gt: e.tensor_copy(
                            out=qTres[:, :, gt * 128:(gt + 1) * 128],
                            in_=ps6b[:].rearrange("p (c t) -> p c t", c=8)),
                            waits=(("t", t_tp2),), inc="v")
                        ps6T_free[0] = P.val("v")
                    qtok_free[par] = P.val("t")

                def emit_pass1_unit(t, k):
                    so2 = 48 * (t % 2)
                    gt2 = t * 4 + k
                    d = dma("sync", "dx", xt[k][:],
                            xfull[gt2 * 128:(gt2 + 1) * 128, :],
                            waits=(("v", xt4_free[k][0]), ("s", xt4_free[k][1])))
                    ew("vector", lambda e, k=k, so2=so2: e.tensor_reduce(
                        out=statA[:, so2 + k:so2 + k + 1], in_=xt[k][:], axis=AX.X,
                        op=ALU.add), waits=(d, ("v", v_consts),
                                            ("s", statA_sfree[t % 2])), inc="v")
                    ew("scalar", lambda e, k=k: e.activation(
                        out=sqb2[k % 2][:], in_=xt[k][:], func=AF.Square),
                        waits=(d, ("v", sqb2_free[k % 2])), inc="s")
                    sq_s = P.val("s")
                    ew("vector", lambda e, k=k, so2=so2: e.tensor_reduce(
                        out=statA[:, so2 + 4 + k:so2 + 5 + k], in_=sqb2[k % 2][:],
                        axis=AX.X, op=ALU.add), waits=(("s", sq_s),), inc="v")
                    sqb2_free[k % 2] = P.val("v")

                def emit_chain(t):
                    so = 48 * (t % 2)
                    vsettle()
                    ew("vector", lambda e, so=so: e.tensor_scalar(
                        out=statA[:, so + 8:so + 12], in0=statA[:, so + 0:so + 4],
                        scalar1=1.0 / 1024, scalar2=None, op0=ALU.mult), inc="v")
                    vsettle()
                    ew("vector", lambda e, so=so: e.tensor_scalar(
                        out=statA[:, so + 12:so + 16], in0=statA[:, so + 4:so + 8],
                        scalar1=1.0 / 1024, scalar2=None, op0=ALU.mult), inc="v")
                    vsettle()
                    ew("vector", lambda e, so=so: e.tensor_tensor(
                        out=statA[:, so + 16:so + 20], in0=statA[:, so + 8:so + 12],
                        in1=statA[:, so + 8:so + 12], op=ALU.mult), inc="v")
                    vsettle()
                    ew("vector", lambda e, so=so: e.tensor_sub(
                        out=statA[:, so + 20:so + 24], in0=statA[:, so + 16:so + 20],
                        in1=statA[:, so + 12:so + 16]), inc="v")
                    vsettle()
                    vV = P.val("v")
                    ew("scalar", lambda e, so=so: e.activation(
                        out=statA[:, so + 24:so + 28], in_=statA[:, so + 20:so + 24],
                        func=AF.Sqrt, bias=cpool[:, 0:1], scale=-1.0),
                        waits=(("v", vV),), inc="s")
                    sA_sqrt = P.val("s")
                    ew("vector", lambda e, so=so: e.reciprocal(
                        out=statA[:, so + 28:so + 32], in_=statA[:, so + 24:so + 28]),
                        waits=(("s", sA_sqrt),), inc="v")
                    vsettle()
                    ew("vector", lambda e, so=so: e.scalar_tensor_tensor(
                        out=statA[:, so + 32:so + 36], in0=statA[:, so + 8:so + 12],
                        scalar=-1.0, in1=statA[:, so + 28:so + 32],
                        op0=ALU.mult, op1=ALU.mult), inc="v")
                    vsettle()
                    return P.val("v")

                for k in range(4):
                    emit_pass1_unit(0, k)
                v_stats = emit_chain(0)
                for t_i in range(DEBUG_A_TILES):
                    owned = t_i < NT
                    so = 48 * (t_i % 2)
                    for k in range(4):
                        gt = t_i * 4 + k
                        par = gt % 2
                        ew("scalar", lambda e, k=k, par=par, so=so: e.activation(
                            out=ub[par][:], in_=xt[k][:], func=AF.Identity,
                            bias=statA[:, so + 32 + k:so + 33 + k],
                            scale=statA[:, so + 28 + k:so + 29 + k]),
                            waits=(("v", v_stats), ("t", ub_free[par])), inc="s")
                        s_u = P.val("s")
                        xt4_free[k] = (v_stats, s_u)
                        if k == 3:
                            statA_sfree[t_i % 2] = s_u
                        for j in range(8):
                            ew("tensor", lambda e, par=par, j=j: e.transpose(
                                out=ps3b[:, j * 128:(j + 1) * 128],
                                in_=ub[par][:, j * 128:(j + 1) * 128],
                                identity=idbf[:]),
                                waits=(("s", s_u), psT_free) if j == 0 else (),
                                inc="t")
                        ub_free[par] = P.val("t")
                        t_tp = P.val("t")
                        ew("vector", lambda e, par=par: e.tensor_copy(
                            out=uT[par][:], in_=ps3b[:]),
                            waits=(("t", t_tp), ("t", uT_free[par])), inc="v")
                        psT_free = ("v", P.val("v"))
                        s_uT = ("v", P.val("v"))
                        if k >= 1 and t_i + 1 < DEBUG_A_TILES:
                            emit_pass1_unit(t_i + 1, k - 1)
                        if pendG[0] is not None:
                            flush_G(*pendG[0])
                            pendG[0] = None
                        qa0, qa1 = (ps0[:, 0:512], ps0[:, 512:1024]) if par == 0 \
                            else (ps4[:], ps5[:])
                        for cc in range(8):
                            w0 = (s_uT, psA_free[par], ("v", v_pb)) \
                                if cc == 0 else ()
                            ew("tensor", lambda e, par=par, cc=cc, qa0=qa0: e.matmul(
                                out=qa0,
                                lhsT=uT[par][:, cc * 128:(cc + 1) * 128],
                                rhs=pb[:, cc, 0:512],
                                start=(cc == 0), stop=(cc == 7), **MM),
                                waits=w0, inc="t")
                            ew("tensor", lambda e, par=par, cc=cc, qa1=qa1: e.matmul(
                                out=qa1,
                                lhsT=uT[par][:, cc * 128:(cc + 1) * 128],
                                rhs=pb[:, cc, 512:1024],
                                start=(cc == 0), stop=(cc == 7), **MM), inc="t")
                        uT_free[par] = P.val("t")
                        t_q = P.val("t")
                        ew("vector", lambda e, par=par, qa0=qa0, qa1=qa1:
                            (e.tensor_copy(out=qtok[par][:, 0:512], in_=qa0),
                             e.tensor_copy(out=qtok[par][:, 512:1024], in_=qa1))[-1],
                            waits=(("t", t_q), ("t", qtok_free[par]),
                                   qtok_dump[par]), inc="v")
                        psA_free[par] = ("v", P.val("v"))
                        s_qk = P.val("v")
                        if DEBUG_TAPS:
                            qdump = dma("gpsimd", "dg", dbg_qall[gt, :, :],
                                        qtok[par][:], waits=(("v", s_qk),))
                            qtok_dump[par] = qdump
                        if DEBUG_A_STEPS >= 5:
                            pendG[0] = (gt, par, s_qk, owned)
                    if t_i + 1 < DEBUG_A_TILES:
                        emit_pass1_unit(t_i + 1, 3)
                        v_stats = emit_chain(t_i + 1)
                    if not DEBUG_WPREP:
                        continue
                    if t_i < 7 or DEBUG_A_TILES < NT_A:
                        for _ in range(6):
                            idx = next(wq1, None)
                            if idx is not None:
                                emit_wpass1(idx)
                        if t_i == 6 and not scales_done[0]:
                            for idx in wq1:
                                emit_wpass1(idx)
                            emit_wscales()
                            scales_done[0] = True
                    else:
                        for _ in range(5):
                            idx = next(wq2, None)
                            if idx is not None:
                                emit_wpass2(idx)
                if pendG[0] is not None:
                    flush_G(*pendG[0])
                    pendG[0] = None
                if DEBUG_WPREP:
                    if not scales_done[0]:
                        for idx in wq1:
                            emit_wpass1(idx)
                        emit_wscales()
                        scales_done[0] = True
                    for idx in wq2:
                        emit_wpass2(idx)

                if DEBUG_A_STEPS < 6:
                    P.fence()
                    P.flush(block, sems)
                    return nc
                ew("vector", lambda e: e.tensor_copy(out=gsb[0:KH, 0:512], in_=ps1[0:KH, :]),
                   waits=(("t", t_G_last),), inc="v")
                ew("vector", lambda e: e.tensor_copy(out=gsb[0:KH, 512:1024], in_=ps2[0:KH, :]),
                   inc="v")
                v_gsb = P.val("v")
                for n in range(16):
                    pr, ei = n // 2, n % 2
                    ew("tensor", lambda e, n=n, pr=pr, ei=ei: e.matmul(
                        out=ps4[ei * KH:(ei + 1) * KH, pr * KH:(pr + 1) * KH],
                        lhsT=mblkb[0:KH, n * KH:(n + 1) * KH],
                        rhs=gsb[0:KH, n * KH:(n + 1) * KH],
                        start=True, stop=True,
                        tile_position=(0, ei * KH), **MM),
                        waits=(("v", v_gsb), ("v", v_mblk)) if n == 0 else (),
                        inc="t")
                t_a = P.val("t")
                for n in range(16):
                    pr, ei = n // 2, n % 2
                    ew("vector", lambda e, pr=pr, ei=ei: e.tensor_copy(
                        out=ablk[ei * KH:(ei + 1) * KH,
                                 pr * 128 + ei * KH:pr * 128 + (ei + 1) * KH],
                        in_=ps4[ei * KH:(ei + 1) * KH, pr * KH:(pr + 1) * KH]),
                        waits=(("t", t_a),) if n == 0 else (), inc="v")
                v_ablk = P.val("v")
                if DEBUG_TAPS:
                    dma("gpsimd", "dg", dbg_g, gsb[0:KH, :], waits=(("v", v_gsb),))
                    dma("gpsimd", "dg", dbg_a, ablk[:], waits=(("v", v_ablk),))
                    for cc in range(8):
                        dma("gpsimd", "dg", dbg_q[:, cc, :], qTres[:, cc, 0:512],
                            waits=(("v", P.val("v")),))

                P.fence()
                P.flush(block, sems)
                esA.close()
                if DEBUG_PHASES < 2:
                    return nc

                with ExitStack() as esB:
                    nudT = sb(esB, "nudT", [128, 8, 512], BF16)
                    yb = sb(esB, "yb", [128, 1024], F32)
                    vb = sb(esB, "vb", [128, 1024], F32)
                    xob = [sb(esB, f"xob{i}", [128, 1024], F32) for i in range(4)]
                    x2b = [sb(esB, f"x2b{i}", [128, 1024], F32) for i in range(2)]
                    xqi = sb(esB, "xqi", [128, 1024], I32)
                    xqsb = sb(esB, "xqsb", [128, 1024], BF16)
                    xqsT = sb(esB, "xqsT", [128, 8, 512], BF16)
                    uTh = sb(esB, "uTh", [128, 8, 2], BF16)
                    nudTh = sb(esB, "nudTh", [128, 8, 2], BF16)
                    qhb = sb(esB, "qhb", [2, 1024], BF16)

                    xob_free = {0: 0, 1: 0, 2: 0, 3: 0}
                    x2b_free = {0: (0, ("v", 0)), 1: (0, ("v", 0))}
                    psT_free2 = 0
                    ps4_free, ps5_free = 0, 0
                    ps0_free = 0
                    yb_free_s = 0
                    nudT_free_t = 0
                    xqsT_free = 0
                    xqsT_store_dg = ("v", 0)

                    for i in range(NT):
                        for p in range(8):
                            pg = ps4 if p % 2 == 0 else ps5
                            pfree = ps4_free if p % 2 == 0 else ps5_free
                            ew("tensor", lambda e, p=p, pg=pg, i=i: e.matmul(
                                out=pg[:], lhsT=ablk[:, p * 128:(p + 1) * 128],
                                rhs=qTres[:, p, i * 512:(i + 1) * 512],
                                start=True, stop=True, **MM),
                                waits=(("v", v_ablk), ("v", pfree)), inc="t")
                            tn = P.val("t")
                            ew("vector", lambda e, p=p, pg=pg: e.tensor_copy(
                                out=nudT[:, p, :], in_=pg[:]),
                                waits=(("t", tn), ("t", nudT_free_t)), inc="v")
                            if p % 2 == 0:
                                ps4_free = P.val("v")
                            else:
                                ps5_free = P.val("v")
                        v_nud = P.val("v")
                        for k in range(4):
                            oc = i * 4 + k
                            par = oc % 2
                            for jc in range(8):
                                w0 = (("v", v_nud), ("v", ps0_free), ("v", v_mwtb)) \
                                    if jc == 0 else ()
                                ew("tensor", lambda e, jc=jc, k=k: e.matmul(
                                    out=ps0[:, 0:512],
                                    lhsT=nudT[:, jc, k * 128:(k + 1) * 128],
                                    rhs=mwtb[:, jc, 0:512],
                                    start=(jc == 0), stop=(jc == 7), **MM),
                                    waits=w0, inc="t")
                                ew("tensor", lambda e, jc=jc, k=k: e.matmul(
                                    out=ps0[:, 512:1024],
                                    lhsT=nudT[:, jc, k * 128:(k + 1) * 128],
                                    rhs=mwtb[:, jc, 512:1024],
                                    start=(jc == 0), stop=(jc == 7), **MM), inc="t")
                            if k == 3:
                                nudT_free_t = P.val("t")
                            t_mix = P.val("t")
                            ew("vector", lambda e: e.tensor_add(
                                out=yb[:], in0=ps0[:], in1=mbias[:]),
                                waits=(("t", t_mix), ("s", yb_free_s)), inc="v")
                            ps0_free = P.val("v")
                            v_y = P.val("v")
                            if DEBUG_TAPS and i == 0 and k == 0:
                                dma("gpsimd", "dg", dbg_y, yb[:], waits=(("v", v_y),))
                            opar = oc % 4
                            dxo = dma("sync", "dx", xob[opar][:],
                                      xfull[oc * 128:(oc + 1) * 128, :],
                                      waits=(("v", xob_free[opar]),))
                            v_st = ln_stats(yb[:], slice(0, 128), par, (("v", v_y),))
                            ew("scalar", lambda e, par=par: e.activation(
                                out=vb[:], in_=yb[:], func=AF.Identity,
                                bias=stat[:, 16 * par + 3:16 * par + 4],
                                scale=stat[:, 16 * par + 2:16 * par + 3]),
                                waits=(("v", v_st),), inc="s")
                            s_v = P.val("s")
                            yb_free_s = s_v
                            ew("vector", lambda e, par=par, opar=opar: e.tensor_add(
                                out=x2b[par][:], in0=vb[:], in1=xob[opar][:]),
                                waits=(("s", s_v), dxo,
                                       ("v", x2b_free[par][0]),
                                       x2b_free[par][1]), inc="v")
                            xob_free[opar] = P.val("v")
                            v_x2 = P.val("v")
                            dgx2 = dma("gpsimd", "dg",
                                       x2_d[oc * 128:(oc + 1) * 128, :],
                                       x2b[par][:], waits=(("v", v_x2),))
                            v_q = quant1(x2b[par][:], slice(0, 128), par,
                                         (("v", v_x2),), xqi, xqsb,
                                         (("t", xqsT_free),))
                            if DEBUG_TAPS and i == 0 and k == 0:
                                dma("gpsimd", "dg", dbg_st, stat[:], waits=(("v", v_q),))
                                dma("gpsimd", "dg", dbg_xt, x2b[par][:], waits=(("v", v_q),))
                                dma("gpsimd", "dg", dbg_u, xqsb[:], waits=(("v", v_q),))
                                P.fence()
                            x2b_free[par] = (v_q, dgx2)
                            for j in range(8):
                                ew("tensor", lambda e, j=j: e.transpose(
                                    out=ps3b[:, j * 128:(j + 1) * 128],
                                    in_=xqsb[:, j * 128:(j + 1) * 128],
                                    identity=idbf[:]),
                                    waits=(("v", v_q), ("v", psT_free2))
                                    if j == 0 else (), inc="t")
                            xqsT_free = P.val("t")
                            t_tp = P.val("t")
                            ew("vector", lambda e, k=k: e.tensor_copy(
                                out=xqsT[:, :, k * 128:(k + 1) * 128],
                                in_=ps3b[:].rearrange("p (c t) -> p c t", c=8)),
                                waits=(("t", t_tp), xqsT_store_dg), inc="v")
                            psT_free2 = P.val("v")
                        vw = P.val("v")
                        for cc in range(8):
                            xqsT_store_dg = dma(
                                "gpsimd", "dg",
                                xqs_d[cc, :, i * 512:(i + 1) * 512],
                                xqsT[:, cc, :],
                                waits=(("v", vw),) if cc == 0 else ())
                    dg_xqs = qdone("dg")

                    id2 = idbf[0:2, 0:2]
                    v_st = ln_stats(xh[:], slice(0, 2), 0, dx_small)
                    ew("scalar", lambda e: e.activation(
                        out=xqsb[0:2, :], in_=xh[:], func=AF.Identity,
                        bias=stat[0:2, 3:4], scale=stat[0:2, 2:3]),
                        waits=(("v", v_st),), inc="s")
                    s_uh = P.val("s")
                    for j in range(8):
                        ew("tensor", lambda e, j=j: e.matmul(
                            out=ps4[:, j * 2:(j + 1) * 2],
                            lhsT=xqsb[0:2, j * 128:(j + 1) * 128], rhs=id2,
                            start=True, stop=True, **MM),
                            waits=(("s", s_uh), ("v", ps4_free)) if j == 0 else (),
                            inc="t")
                    t_tp = P.val("t")
                    ew("vector", lambda e: e.tensor_copy(
                        out=uTh[:], in_=ps4[:, 0:16].rearrange("p (c t) -> p c t", c=8)),
                        waits=(("t", t_tp),), inc="v")
                    ps4_free = P.val("v")
                    v_uTh = P.val("v")
                    for cc in range(8):
                        ew("tensor", lambda e, cc=cc: e.matmul(
                            out=ps0[0:2, 0:512], lhsT=uTh[:, cc, :],
                            rhs=pb[:, cc, 0:512], start=(cc == 0), stop=(cc == 7), **MM),
                            waits=(("v", v_uTh), ("v", ps0_free)) if cc == 0 else (),
                            inc="t")
                        ew("tensor", lambda e, cc=cc: e.matmul(
                            out=ps0[0:2, 512:1024], lhsT=uTh[:, cc, :],
                            rhs=pb[:, cc, 512:1024], start=(cc == 0), stop=(cc == 7),
                            **MM), inc="t")
                    t_qh = P.val("t")
                    ew("vector", lambda e: e.tensor_copy(out=qhb[:], in_=ps0[0:2, :]),
                       waits=(("t", t_qh),), inc="v")
                    ps0_free = P.val("v")
                    v_qh = P.val("v")
                    for j in range(8):
                        ew("tensor", lambda e, j=j: e.matmul(
                            out=ps4[:, j * 2:(j + 1) * 2],
                            lhsT=qhb[0:2, j * 128:(j + 1) * 128], rhs=id2,
                            start=True, stop=True, **MM),
                            waits=(("v", v_qh), ("v", ps4_free)) if j == 0 else (),
                            inc="t")
                    t_tp = P.val("t")
                    ew("vector", lambda e: e.tensor_copy(
                        out=uTh[:], in_=ps4[:, 0:16].rearrange("p (c t) -> p c t", c=8)),
                        waits=(("t", t_tp),), inc="v")
                    ps4_free = P.val("v")
                    v_qTh = P.val("v")
                    for p in range(8):
                        ew("tensor", lambda e, p=p: e.matmul(
                            out=ps4[:, 0:2], lhsT=ablk[:, p * 128:(p + 1) * 128],
                            rhs=uTh[:, p, :], start=True, stop=True, **MM),
                            waits=(("v", v_qTh), ("v", ps4_free)) if p == 0 else
                            (("v", ps4_free),), inc="t")
                        tn = P.val("t")
                        ew("vector", lambda e, p=p: e.tensor_copy(
                            out=nudTh[:, p, :], in_=ps4[:, 0:2]),
                            waits=(("t", tn),), inc="v")
                        ps4_free = P.val("v")
                    v_nudh = P.val("v")
                    for jc in range(8):
                        ew("tensor", lambda e, jc=jc: e.matmul(
                            out=ps0[0:2, 0:512], lhsT=nudTh[:, jc, :],
                            rhs=mwtb[:, jc, 0:512], start=(jc == 0), stop=(jc == 7),
                            **MM),
                            waits=(("v", v_nudh), ("v", ps0_free)) if jc == 0 else (),
                            inc="t")
                        ew("tensor", lambda e, jc=jc: e.matmul(
                            out=ps0[0:2, 512:1024], lhsT=nudTh[:, jc, :],
                            rhs=mwtb[:, jc, 512:1024], start=(jc == 0), stop=(jc == 7),
                            **MM), inc="t")
                    t_mixh = P.val("t")
                    ew("vector", lambda e: e.tensor_add(
                        out=yb[0:2, :], in0=ps0[0:2, :], in1=mbias[0:2, :]),
                        waits=(("t", t_mixh), ("s", yb_free_s)), inc="v")
                    ps0_free = P.val("v")
                    v_yh = P.val("v")
                    v_st = ln_stats(yb[0:2, :], slice(0, 2), 0, (("v", v_yh),))
                    ew("scalar", lambda e: e.activation(
                        out=vb[0:2, :], in_=yb[0:2, :], func=AF.Identity,
                        bias=stat[0:2, 3:4], scale=stat[0:2, 2:3]),
                        waits=(("v", v_st),), inc="s")
                    s_vh = P.val("s")
                    ew("vector", lambda e: e.tensor_add(
                        out=x2b[0][0:2, :], in0=vb[0:2, :], in1=xh[:]),
                        waits=(("s", s_vh), ("v", x2b_free[0][0]),
                               x2b_free[0][1]), inc="v")
                    v_x2h = P.val("v")
                    v_qq = quant1(x2b[0][0:2, :], slice(0, 2), 0, (("v", v_x2h),),
                                  xqi, xqsb, (("t", xqsT_free),))
                    ew("vector", lambda e: e.tensor_scalar(
                        out=qhb[0:2, :], in0=xqsb[0:2, :], scalar1=hmk[:, 0:1],
                        scalar2=None, op0=ALU.mult),
                        waits=(("v", v_qq),) + dx_small, inc="v")
                    v_mq = P.val("v")
                    for j in range(8):
                        ew("tensor", lambda e, j=j: e.matmul(
                            out=ps4[:, j * 2:(j + 1) * 2],
                            lhsT=qhb[0:2, j * 128:(j + 1) * 128], rhs=id2,
                            start=True, stop=True, **MM),
                            waits=(("v", v_mq), ("v", ps4_free)) if j == 0 else (),
                            inc="t")
                    t_tp = P.val("t")
                    ew("vector", lambda e: e.tensor_copy(
                        out=xqsTh[:], in_=ps4[:, 0:16].rearrange("p (c t) -> p c t", c=8)),
                        waits=(("t", t_tp),), inc="v")
                    v_xqsTh = P.val("v")

                    P.fence()
                    P.flush(block, sems)
                    if DEBUG_PHASES < 3:
                        return nc

            with ExitStack() as esC:
                xqsTb = [sb(esC, f"xqsTb{i}", [128, 8, 512], BF16) for i in range(2)]
                t1s = [sb(esC, f"t1s{i}", [128, 8, 1024], BF16) for i in range(2)]
                hbuf = [sb(esC, f"hbuf{i}", [128, 32, 514], BF16) for i in range(2)]
                tconv = [sb(esC, f"tconv{i}", [128, 512], F32) for i in range(2)]
                mxb = sb(esC, "mxb", [128, 512], BF16)
                mnb = sb(esC, "mnb", [128, 512], BF16)
                s2row = sb(esC, "s2row", [1, 512], F32)
                s2b = sb(esC, "s2b", [128, 512], F32)
                qi32 = [sb(esC, f"qi32_{i}", [128, 512], I32) for i in range(2)]

                xqsTb_free = {0: 0, 1: 0}
                t1s_free = {0: 0, 1: 0}
                hbuf_free = {0: (), 1: ()}
                hbuf_ready = {}
                evac_s = {i: [0] * 32 for i in range(NT)}
                psH_free = {0: 0, 1: 0}
                copyleft_done = {}
                ps4c_free = 0
                mx_free_t = 0
                ps3b_free_c = 0
                ps5c_free = 0
                ps6c_free = 0
                stat_c_free = 0
                v_hedge = [0]

                def gen_mm1(i):
                    par = i % 2
                    for cc in range(8):
                        dma("scalar", "dw", xqsTb[par][:, cc, :],
                            xqs_d[cc, :, i * 512:(i + 1) * 512],
                            waits=dg_xqs + (("t", xqsTb_free[par]),)
                            if cc == 0 else ())
                    dxq = qdone("dw")
                    for g in range(4):
                        gpar = g % 2
                        for dc in range(8):
                            dma("scalar", "dw", t1s[gpar][:, dc, :],
                                tern1_d[dc * 128:(dc + 1) * 128,
                                        g * 1024:(g + 1) * 1024],
                                waits=(("t", t1s_free[gpar]),
                                       wstate[("p2g", 7)])
                                if dc == 0 else ())
                        dt1 = qdone("dw")
                        for fj in range(8):
                            fc = g * 8 + fj
                            hpar = fc % 2
                            ph = ps1 if hpar == 0 else ps2
                            for dc in range(8):
                                w0 = []
                                if fj == 0 and dc == 0:
                                    w0 += list(dt1) + list(dxq)
                                    if i == 0 and g == 0:
                                        w0.append(("v", v_xqsTh))
                                if dc == 0:
                                    w0.append(("s", psH_free[hpar]))
                                if dc == 0 and fc == 0 and i >= 1:
                                    w0 += list(hbuf_free[par]) + \
                                        [("v", copyleft_done.get(i - 1, 0))]
                                ew("tensor", lambda e, gpar=gpar, dc=dc, fj=fj,
                                    ph=ph, par=par: e.matmul(
                                        out=ph[:],
                                        lhsT=t1s[gpar][:, dc, fj * 128:(fj + 1) * 128],
                                        rhs=xqsTb[par][:, dc, :],
                                        start=(dc == 0), stop=(dc == 7), **MM),
                                    waits=tuple(w0), inc="t")
                                if i == 0:
                                    ew("tensor", lambda e, gpar=gpar, dc=dc, fj=fj,
                                        fc=fc: e.matmul(
                                            out=ps4[:, fc * 2:(fc + 1) * 2],
                                            lhsT=t1s[gpar][:, dc,
                                                           fj * 128:(fj + 1) * 128],
                                            rhs=xqsTh[:, dc, :],
                                            start=(dc == 0), stop=(dc == 7), **MM),
                                        inc="t")
                            t_f = P.val("t")
                            ew("scalar", lambda e, ph=ph, par=par, fc=fc: e.copy(
                                out=hbuf[par][:, fc, 1:513], in_=ph[:]),
                                waits=(("t", t_f),), inc="s")
                            psH_free[hpar] = P.val("s")
                            evac_s[i][fc] = P.val("s")
                            yield True
                        t1s_free[gpar] = P.val("t")
                    xqsTb_free[par] = P.val("t")
                    hbuf_ready[i] = P.val("s")
                    if i == 0:
                        t_pig = P.val("t")
                        ew("vector", lambda e: e.tensor_copy(
                            out=hedge[:], in_=ps4[:, 0:64]),
                            waits=(("t", t_pig),), inc="v")
                        v_hedge[0] = P.val("v")

                def emit_copyleft(i):
                    par = i % 2
                    if i > 0:
                        src = hbuf[(i - 1) % 2][:, :, 512:513]
                        wts = (("s", hbuf_ready[i]),)
                    else:
                        src = hedge[:].rearrange("p (f c) -> p f c", c=2)[:, :, 0:1]
                        wts = (("s", hbuf_ready[i]), ("v", v_hedge[0]))
                    ew("vector", lambda e, par=par, src=src: e.tensor_copy(
                        out=hbuf[par][:, :, 0:1], in_=src), waits=wts, inc="v")
                    copyleft_done[i] = P.val("v")

                def gen_convquant(i):
                    nonlocal mx_free_t, ps3b_free_c, ps5c_free, ps6c_free, stat_c_free
                    par = i % 2
                    silu_s = {}
                    conv_v = {}

                    def emit_folds(fc):
                        ch = hbuf[par][:, fc, :]
                        if fc == 0:
                            ew("vector", lambda e, ch=ch: e.tensor_copy(
                                out=mxb[:], in_=ch[:, 1:513]),
                                waits=(("s", silu_s[0]), ("t", mx_free_t)), inc="v")
                            ew("vector", lambda e, ch=ch: e.tensor_copy(
                                out=mnb[:], in_=ch[:, 1:513]), inc="v")
                        else:
                            ew("vector", lambda e, ch=ch: e.tensor_max(
                                out=mxb[:], in0=ch[:, 1:513], in1=mxb[:]),
                                waits=(("s", silu_s[fc]),), inc="v")
                            ew("vector", lambda e, ch=ch: e.tensor_tensor(
                                out=mnb[:], in0=ch[:, 1:513], in1=mnb[:],
                                op=ALU.min), inc="v")

                    tap1_s = {}

                    def emit_tap1(fc):
                        ch = hbuf[par][:, fc, :]
                        tc = tconv[fc % 2]
                        w0 = [("v", conv_v[fc - 2])] if fc >= 2 else []
                        ew("scalar", lambda e, ch=ch, fc=fc, tc=tc: e.activation(
                            out=tc[:], in_=ch[:, 1:513], func=AF.Identity,
                            scale=convp[:, 32 + fc:33 + fc]),
                            waits=tuple(w0), inc="s")
                        tap1_s[fc] = P.val("s")

                    emit_tap1(0)
                    for fc in range(32):
                        ch = hbuf[par][:, fc, :]
                        tc = tconv[fc % 2]
                        if i < NT - 1:
                            rh = hbuf[(i + 1) % 2][:, fc, 1:2]
                            wrh = (("s", evac_s[i + 1][fc]),)
                        else:
                            rh = hedge[:].rearrange(
                                "p (f c) -> p f c", c=2)[:, fc, 1:2]
                            wrh = (("v", v_hedge[0]),)
                        ew("vector", lambda e, ch=ch, rh=rh: e.tensor_copy(
                            out=ch[:, 513:514], in_=rh),
                            waits=wrh + (("s", evac_s[i][fc]),), inc="v")
                        if i + 1 < NT:
                            ew("vector", lambda e, ch=ch, fc=fc, i=i: e.tensor_copy(
                                out=hbuf[(i + 1) % 2][:, fc, 0:1],
                                in_=ch[:, 512:513]),
                                waits=(("s", evac_s[i + 1][fc]),), inc="v")
                            if fc == 31:
                                copyleft_done[i + 1] = P.val("v")
                        ew("vector", lambda e, ch=ch, fc=fc, tc=tc: e.scalar_tensor_tensor(
                            out=tc[:], in0=ch[:, 0:512], scalar=convp[:, fc:fc + 1],
                            in1=tc[:], op0=ALU.mult, op1=ALU.add),
                            waits=(("s", tap1_s[fc]),), inc="v")
                        ew("vector", lambda e, ch=ch, fc=fc, tc=tc: e.scalar_tensor_tensor(
                            out=tc[:], in0=ch[:, 2:514],
                            scalar=convp[:, 64 + fc:65 + fc], in1=tc[:],
                            op0=ALU.mult, op1=ALU.add), inc="v")
                        v_c = P.val("v")
                        conv_v[fc] = v_c
                        if fc + 1 < 32:
                            emit_tap1(fc + 1)
                        ew("scalar", lambda e, ch=ch, fc=fc, tc=tc: e.activation(
                            out=ch[:, 1:513], in_=tc[:], func=AF.Silu,
                            bias=convp[:, 96 + fc:97 + fc], scale=1.0),
                            waits=(("v", v_c),), inc="s")
                        silu_s[fc] = P.val("s")
                        if fc >= 1:
                            emit_folds(fc - 1)
                        yield True
                    emit_folds(31)
                    ew("vector", lambda e: e.scalar_tensor_tensor(
                        out=mxb[:], in0=mnb[:], scalar=-1.0, in1=mxb[:],
                        op0=ALU.mult, op1=ALU.max), inc="v")
                    v_amax = P.val("v")
                    for bq in range(4):
                        ew("tensor", lambda e, bq=bq: e.transpose(
                            out=ps3b[:, bq * 128:(bq + 1) * 128],
                            in_=mxb[:, bq * 128:(bq + 1) * 128], identity=idbf[:]),
                            waits=(("v", v_amax), ("v", ps3b_free_c))
                            if bq == 0 else (), inc="t")
                    t_tp = P.val("t")
                    mx_free_t = t_tp
                    for bq in range(4):
                        w0 = [("t", t_tp)] if bq == 0 else []
                        if bq == 0 and stat_c_free:
                            w0.append(("t", stat_c_free))
                        ew("vector", lambda e, bq=bq: e.tensor_reduce(
                            out=stat[:, 8 + bq:9 + bq],
                            in_=ps3b[:, bq * 128:(bq + 1) * 128],
                            axis=AX.X, op=ALU.max), waits=tuple(w0), inc="v")
                    ps3b_free_c = P.val("v")
                    vsettle()
                    for bq in range(4):
                        ew("vector", lambda e, bq=bq: e.tensor_scalar(
                            out=stat[:, 16 + bq:17 + bq], in0=stat[:, 8 + bq:9 + bq],
                            scalar1=EPS, scalar2=None, op0=ALU.max), inc="v")
                        vsettle()
                        ew("vector", lambda e, bq=bq: e.reciprocal(
                            out=stat[:, 20 + bq:21 + bq], in_=stat[:, 16 + bq:17 + bq]),
                            inc="v")
                        vsettle()
                        ew("vector", lambda e, bq=bq: e.tensor_scalar(
                            out=stat[:, 12 + bq:13 + bq], in0=stat[:, 20 + bq:21 + bq],
                            scalar1=127.0, scalar2=None, op0=ALU.mult), inc="v")
                        vsettle()
                        ew("vector", lambda e, bq=bq, i=i: e.scalar_tensor_tensor(
                            out=invb2res[:, i * 4 + bq:i * 4 + bq + 1],
                            in0=stat[:, 16 + bq:17 + bq], scalar=1.0 / 127,
                            in1=wsc[:, 3:4], op0=ALU.mult, op1=ALU.mult), inc="v")
                        vsettle()
                    v_s2 = P.val("v")
                    for bq in range(4):
                        ew("tensor", lambda e, bq=bq: e.transpose(
                            out=ps5[0:1, bq * 128:(bq + 1) * 128],
                            in_=stat[:, 12 + bq:13 + bq], identity=idf32[:]),
                            waits=(("v", v_s2), ("v", ps5c_free))
                            if bq == 0 else (), inc="t")
                    t_s2 = P.val("t")
                    stat_c_free = t_s2
                    ew("vector", lambda e: e.tensor_copy(
                        out=s2row[:], in_=ps5[0:1, 0:512]), waits=(("t", t_s2),), inc="v")
                    ps5c_free = P.val("v")
                    v_r = P.val("v")
                    ew("tensor", lambda e: e.matmul(
                        out=ps6[:], lhsT=ones1[:], rhs=s2row[:], start=True,
                        stop=True, **MM),
                        waits=(("v", v_r), ("v", ps6c_free)), inc="t")
                    t_b = P.val("t")
                    ew("vector", lambda e: e.tensor_copy(out=s2b[:], in_=ps6[:]),
                       waits=(("t", t_b),), inc="v")
                    ps6c_free = P.val("v")
                    v_s2b = P.val("v")
                    cast_s = {}
                    for fc in range(32):
                        ch = hbuf[par][:, fc, :]
                        w0 = [("v", v_s2b)] if fc == 0 else []
                        if fc >= 2:
                            w0.append(("s", cast_s[fc - 2]))
                        ew("vector", lambda e, ch=ch, fc=fc: e.tensor_tensor(
                            out=qi32[fc % 2][:], in0=ch[:, 1:513], in1=s2b[:],
                            op=ALU.mult), waits=tuple(w0), inc="v")
                        v_tt = P.val("v")
                        ew("scalar", lambda e, ch=ch, fc=fc: e.activation(
                            out=ch[:, 1:513], in_=qi32[fc % 2][:],
                            func=AF.Identity), waits=(("v", v_tt),), inc="s")
                        cast_s[fc] = P.val("s")
                        dma("gpsimd", "dg",
                            hq_d[fc, :, i * 512:(i + 1) * 512],
                            hbuf[par][:, fc, 1:513],
                            waits=(("s", cast_s[fc]),))
                        yield True
                    hbuf_free[par] = qdone("dg")

                for _ in gen_mm1(0):
                    pass
                emit_copyleft(0)
                for i in range(NT):
                    gmm = gen_mm1(i + 1) if i + 1 < NT else None
                    gcv = gen_convquant(i)
                    while True:
                        if gmm is not None and next(gmm, None) is None:
                            gmm = None
                        if next(gcv, None) is None:
                            break
                    while gmm is not None and next(gmm, None) is None:
                        gmm = None

                P.fence()
                P.flush(block, sems)
                if DEBUG_PHASES < 4:
                    return nc

            with ExitStack() as esD:
                hqb = [sb(esD, f"hqb{i}", [128, 512], BF16) for i in range(4)]
                t2res = sb(esD, "t2res", [128, 32, 1024], BF16)
                x2l = [sb(esD, f"x2l{i}", [128, 1024], F32) for i in range(4)]
                y2b = [sb(esD, f"y2b{i}", [128, 1024], F32) for i in range(2)]
                v2b = sb(esD, "v2b", [128, 1024], F32)
                ob = [sb(esD, f"ob{i}", [128, 1024], F32) for i in range(2)]

                accs = [(ps0[:, 0:512], ps0[:, 512:1024]),
                        (ps1[:], ps2[:]),
                        (ps3[:], ps4[:]),
                        (ps5[:], ps6[:])]
                hqb_free = {0: 0, 1: 0, 2: 0, 3: 0}
                x2l_free = {0: 0, 1: 0, 2: 0, 3: 0}
                ob_free = {0: ("v", 0), 1: ("v", 0)}
                acc_free_v = [0, 0, 0, 0]
                y2b_free_s = {0: 0, 1: 0}

                for i in range(NT):
                    for fc in range(32):
                        fpar = fc % 4
                        dh = dma("sync", "dx", hqb[fpar][:],
                                 hq_d[fc, :, i * 512:(i + 1) * 512],
                                 waits=(("t", hqb_free[fpar]),))
                        dt2 = dma("scalar", "dw", t2res[:, fc, :],
                                  tern2_d[fc * 128:(fc + 1) * 128, :]) \
                            if i == 0 else None
                        for tsub in range(4):
                            w0 = []
                            if tsub == 0:
                                w0.append(dh)
                                if dt2 is not None:
                                    w0.append(dt2)
                            if fc == 0:
                                w0.append(("v", acc_free_v[tsub]))
                            a0, a1 = accs[tsub]
                            ew("tensor", lambda e, fpar=fpar, tsub=tsub, a0=a0,
                                fc=fc: e.matmul(
                                    out=a0,
                                    lhsT=hqb[fpar][:, tsub * 128:(tsub + 1) * 128],
                                    rhs=t2res[:, fc, 0:512],
                                    start=(fc == 0), stop=(fc == 31), **MM),
                                waits=tuple(w0), inc="t")
                            ew("tensor", lambda e, fpar=fpar, tsub=tsub, a1=a1,
                                fc=fc: e.matmul(
                                    out=a1,
                                    lhsT=hqb[fpar][:, tsub * 128:(tsub + 1) * 128],
                                    rhs=t2res[:, fc, 512:1024],
                                    start=(fc == 0), stop=(fc == 31), **MM), inc="t")
                        hqb_free[fpar] = P.val("t")
                    t_last = P.val("t")
                    for tsub in range(4):
                        oc = i * 4 + tsub
                        par = oc % 2
                        a0, a1 = accs[tsub]
                        xpar = oc % 4
                        dx2 = dma("scalar", "dw", x2l[xpar][:],
                                  x2_d[oc * 128:(oc + 1) * 128, :],
                                  waits=(("v", x2l_free[xpar]),))
                        ew("vector", lambda e, a0=a0, i=i, tsub=tsub, par=par:
                            e.tensor_scalar(
                                out=y2b[par][:, 0:512], in0=a0,
                                scalar1=invb2res[:, i * 4 + tsub:i * 4 + tsub + 1],
                                scalar2=None, op0=ALU.mult),
                            waits=(("t", t_last), ("s", y2b_free_s[par])), inc="v")
                        ew("vector", lambda e, a1=a1, i=i, tsub=tsub, par=par:
                            e.tensor_scalar(
                                out=y2b[par][:, 512:1024], in0=a1,
                                scalar1=invb2res[:, i * 4 + tsub:i * 4 + tsub + 1],
                                scalar2=None, op0=ALU.mult), inc="v")
                        acc_free_v[tsub] = P.val("v")
                        v_y2 = P.val("v")
                        v_st = ln_stats(y2b[par][:], slice(0, 128), oc % 2,
                                        (("v", v_y2),))
                        ew("scalar", lambda e, oc=oc, par=par: e.activation(
                            out=v2b[:], in_=y2b[par][:], func=AF.Identity,
                            bias=stat[:, 16 * (oc % 2) + 3:16 * (oc % 2) + 4],
                            scale=stat[:, 16 * (oc % 2) + 2:16 * (oc % 2) + 3]),
                            waits=(("v", v_st),), inc="s")
                        s_v2 = P.val("s")
                        y2b_free_s[par] = s_v2
                        ew("vector", lambda e, par=par, xpar=xpar: e.tensor_add(
                            out=ob[par][:], in0=v2b[:], in1=x2l[xpar][:]),
                            waits=(("s", s_v2), dx2, ob_free[par]),
                            inc="v")
                        x2l_free[xpar] = P.val("v")
                        v_o = P.val("v")
                        ob_free[par] = dma("gpsimd", "dg",
                                           out_d[oc * 128:(oc + 1) * 128, :],
                                           ob[par][:], waits=(("v", v_o),))

                P.emit("gpsimd", None, waits=qdone("dg"))
                P.flush(block, sems)
    return nc


_cache = {}


def kernel(**inputs):
    x = np.ascontiguousarray(np.asarray(inputs["L0_attn_in_sequence_bwc"], np.float32))
    proj = np.asarray(inputs["L0_attn_projection_1nck"], np.float32)[0]
    halves = np.asarray(inputs["L0_attn_halves"], np.float32)
    diag = np.asarray(inputs["L0_attn_diagonals_nk"], np.float32)
    mw = np.asarray(inputs["L0_attn_mixer_w"], np.float32)
    mb = np.asarray(inputs["L0_attn_mixer_b"], np.float32)
    w1 = np.asarray(inputs["L0_ffn_w1"], np.float32)
    cw = np.asarray(inputs["L0_ffn_conv_w"], np.float32)
    cb = np.asarray(inputs["L0_ffn_conv_b"], np.float32)
    w2 = np.asarray(inputs["L0_ffn_w2"], np.float32)

    p2d = np.ascontiguousarray(proj.transpose(1, 0, 2).reshape(CDIM, NHEAD * KH))
    iu, ju = np.triu_indices(KH, k=1)
    M = np.zeros((NHEAD, KH, KH), np.float32)
    M[:, iu, ju] = halves
    M[:, ju, iu] = halves
    M[:, np.arange(KH), np.arange(KH)] = diag
    mblk0 = M.transpose(1, 0, 2).reshape(KH, NHEAD * KH)
    mblk = np.ascontiguousarray(np.concatenate([mblk0, mblk0], axis=0))
    mwt = np.ascontiguousarray(mw.T)
    mbias = np.ascontiguousarray(np.broadcast_to(mb[None, :], (128, CDIM)))
    w1t = np.ascontiguousarray(w1.T)
    w2t = np.ascontiguousarray(w2.T)
    convp = np.zeros((128, 128), np.float32)
    for tap in range(3):
        convp[:, tap * 32:(tap + 1) * 32] = cw[:, 0, tap].reshape(32, 128).T
    convp[:, 96:128] = cb.reshape(32, 128).T
    idbf = np.eye(128, dtype=ml_dtypes.bfloat16)
    idf32 = np.eye(128, dtype=np.float32)
    ones1 = np.ones((1, 128), np.float32)

    key = (DEBUG_PHASES, DEBUG_A_TILES, DEBUG_WPREP, DEBUG_A_STEPS, DEBUG_TAPS)
    if key not in _cache:
        _cache[key] = build_program()
    nc = _cache[key]

    in_maps = []
    for c in range(8):
        b, s = c // 4, c % 4
        xb = np.roll(x[b], -s * TOK, axis=0) if s else x[b]
        t0 = s * TOK
        xhalo = np.zeros((2, CDIM), np.float32)
        hm = np.zeros((2, 1), np.float32)
        if t0 - 1 >= 0:
            xhalo[0] = x[b, t0 - 1]
            hm[0, 0] = 1.0
        if t0 + TOK < SEQ:
            xhalo[1] = x[b, t0 + TOK]
            hm[1, 0] = 1.0
        in_maps.append({
            "xfull": np.ascontiguousarray(xb), "xhalo": xhalo, "hmask": hm,
            "p2d": p2d, "mblk": mblk, "mwt": mwt, "mbias": mbias,
            "w1t": w1t, "w2t": w2t, "convp": convp,
            "idbf": idbf, "idf32": idf32, "ones1": ones1,
        })

    res = run_bass_kernel_spmd(nc, in_maps, core_ids=list(range(8)))
    if DEBUG_TAPS:
        global _taps
        _taps = res.results
    out = np.empty((BATCH, SEQ, CDIM), np.float32)
    for c in range(8):
        b, s = c // 4, c % 4
        out[b, s * TOK:(s + 1) * TOK] = res.results[c]["out"]
    return out
